# version 15
# baseline (speedup 1.0000x reference)
import contextlib
import numpy as np
import ml_dtypes
import concourse.bass as bass
import concourse.mybir as mybir
from concourse.bass_utils import run_bass_kernel_spmd

F32 = mybir.dt.float32
BF16 = mybir.dt.bfloat16
AF = mybir.ActivationFunctionType
ALU = mybir.AluOpType
AX = mybir.AxisListType

B, L, D, H, HD = 4, 8192, 1024, 16, 64
KC, NE, FF = 4, 8, 2048
SEG, CH = 512, 128
NSEG = L // SEG
NCH = SEG // CH
PRE = 8
OWN = NSEG - PRE
LN_EPS = 1e-5
EPS = 1e-6
TFF = 1024
DC = D // 128
FC = FF // 128

ENGINES = ("pe", "act", "dve", "pool", "sp")


def hp(h):
    return (h % 2) * 8 + h // 2


HORDER = [0, 2, 4, 6, 8, 10, 12, 14, 1, 3, 5, 7, 9, 11, 13, 15]


class _Op:
    __slots__ = ("engine", "fn", "deps", "idx", "dma", "inc_sem", "inc_amt",
                 "inc_val", "waits", "prewaits", "tag")

    def __init__(self, engine, fn, deps, idx, dma, tag):
        self.engine, self.fn, self.deps, self.idx, self.dma, self.tag = (
            engine, fn, deps, idx, dma, tag)
        self.inc_sem = None
        self.inc_amt = 0
        self.inc_val = 0
        self.waits = []
        self.prewaits = []


class Prog:
    def __init__(self, nc):
        self.nc = nc
        self.ops = []
        self.last_writer = {}
        self.readers = {}
        self.barrier_idx = None

    def emit(self, engine, fn, reads=(), writes=(), dma=False, tag=""):
        ps_r = [k for k in reads if k.startswith("ps_")]
        if ps_r:
            reads = [k for k in reads if not k.startswith("ps_")]
            writes = list(writes) + ps_r
        idx = len(self.ops)
        deps = set()
        if self.barrier_idx is not None:
            deps.add(self.barrier_idx)
        for k in reads:
            w = self.last_writer.get(k)
            if w is not None:
                deps.add(w)
        for k in writes:
            w = self.last_writer.get(k)
            if w is not None:
                deps.add(w)
            for r in self.readers.get(k, ()):
                deps.add(r)
        deps.discard(idx)
        for k in reads:
            self.readers.setdefault(k, []).append(idx)
        for k in writes:
            self.last_writer[k] = idx
            self.readers[k] = []
        self.ops.append(_Op(engine, fn, deps, idx, dma, tag))
        return idx

    def barrier(self, engine="sp"):
        keys = list(self.last_writer.keys())
        idx = self.emit(engine, lambda e: e.nop(), reads=keys, tag="barrier")
        self.barrier_idx = idx
        return idx

    DMA_SEM_ROT = 8

    def finalize(self):
        ops = self.ops
        for o in ops:
            pruned = set()
            for d in o.deps:
                pp = ops[d]
                if (pp.engine == "pe" and o.engine == "pe"
                        and not pp.dma and not o.dma):
                    continue
                pruned.add(d)
            best = {}
            keep = set()
            for d in pruned:
                pp = ops[d]
                if pp.dma:
                    keep.add(d)
                else:
                    key = pp.engine
                    if key not in best or d > best[key]:
                        best[key] = d
            o.deps = keep | set(best.values())
        needed = set()
        for o in ops:
            needed.update(o.deps)
        cnt = {}
        dma_idx = {}
        R = self.DMA_SEM_ROT
        sem_names = set()
        for o in ops:
            if o.dma:
                i = dma_idx.get(o.engine, 0)
                dma_idx[o.engine] = i + 1
                sem = f"dq_{o.engine}_{i % R}"
                sem_names.add(sem)
                o.inc_sem, o.inc_amt = sem, 16
                o.inc_val = 16 * (i // R + 1)
                if i >= R:
                    o.prewaits = [(sem, 16 * (i // R))]
            elif o.idx in needed:
                sem = "eng_" + o.engine
                sem_names.add(sem)
                cnt[sem] = cnt.get(sem, 0) + 1
                o.inc_sem, o.inc_amt, o.inc_val = sem, 1, cnt[sem]
        self.sem_names = sorted(sem_names)
        known = {e: {} for e in ENGINES}
        by_engine = {e: [] for e in ENGINES}
        for o in ops:
            by_engine[o.engine].append(o)
            k = known[o.engine]
            for d in sorted(o.deps):
                pp = ops[d]
                if pp.inc_sem is None:
                    continue
                if k.get(pp.inc_sem, 0) >= pp.inc_val:
                    continue
                o.waits.append((pp.inc_sem, pp.inc_val))
                k[pp.inc_sem] = pp.inc_val
        finals = {}
        for o in ops:
            if o.dma:
                finals[o.inc_sem] = max(finals.get(o.inc_sem, 0), o.inc_val)
        fin = by_engine["sp"][-1]
        k = known["sp"]
        for sem in sorted(finals):
            if k.get(sem, 0) < finals[sem]:
                fin.waits.append((sem, finals[sem]))
                k[sem] = finals[sem]
        self.by_engine = by_engine

    def emit_block(self, es):
        sems = {name: es.enter_context(self.nc.semaphore(name))
                for name in self.sem_names}
        block = es.enter_context(self.nc.Block())
        prog = self

        def run(eng_name, eng):
            for o in prog.by_engine[eng_name]:
                for sem, val in o.prewaits:
                    eng.wait_ge(sems[sem], val)
                for sem, val in o.waits:
                    eng.wait_ge(sems[sem], val)
                instr = o.fn(eng)
                if o.inc_sem is not None:
                    assert instr is not None, f"op {o.tag} returned no instr"
                    instr.then_inc(sems[o.inc_sem], o.inc_amt)

        @block.tensor
        def _(t):
            run("pe", t)

        @block.scalar
        def _(s):
            run("act", s)

        @block.vector
        def _(v):
            run("dve", v)

        @block.gpsimd
        def _(g):
            run("pool", g)

        @block.sync
        def _(sy):
            run("sp", sy)


def head_slice(buf, sp_, h, c0, c1):
    par, oc = h % 2, h // 2
    return buf[64 * par:64 * par + 64, sp_, oc, c0:c1]


def build_program(pre=PRE, own=OWN, ffn=True, n_experts=NE, debug_taps=(),
                  do_delta=True, do_wo=True, fe_parts=("ln", "beta", "k", "v", "q"),
                  conv_on=True, norm_on=True, conv_ocs=DC,
                  cv_omit=(), full_prefix=False):
    nseg = pre + own
    ntok = nseg * SEG
    owntok = own * SEG
    nft = owntok // TFF if ffn else 0
    nc = bass.Bass("TRN2", target_bir_lowering=False)
    p = Prog(nc)

    def din(name, shape, dt=F32):
        return nc.dram_tensor(name, list(shape), dt, kind="ExternalInput").ap()

    x_in = din("x", (ntok, D))
    kvmask = din("kvmask", (H, pre * SEG)) if pre else None
    halomask = din("halomask", (128, 1))
    wqt = din("wqt", (D, D)); wkt = din("wkt", (D, D)); wvt = din("wvt", (D, D))
    wot = din("wot", (D, D))
    wbetat = din("wbetat", (DC, 128, H))
    wgt = din("wgt", (DC, 128, NE))
    b_in = {n: din(n, (128, DC)) for n in ("bq", "bk", "bv")}
    bo_row = din("bo_row", (1, D))
    bbeta_in = din("bbeta", (H, 1))
    wconv_in = {pr: din(f"w{pr}conv", (128, DC, KC)) for pr in "qkv"}
    bconv_in = {pr: din(f"b{pr}conv", (128, DC)) for pr in "qkv"}
    w1t = din("w1t", (n_experts, D, FF)); w2t = din("w2t", (n_experts, D, FF))
    w3t = din("w3t", (n_experts, FF, D))
    ident_in = din("ident", (128, 128))
    ident64_in = din("ident64", (128, 64))
    trils_in = din("trils", (128, 128))
    md16_in = din("md16", (128, 128))
    md16t_in = din("md16t", (128, 128))
    mo16_in = din("mo16", (128, 128))
    mo32_in = din("mo32", (128, 128))
    triui_in = din("triui", (128, 128))
    selnorm_in = din("selnorm", (DC, 128, H))
    selexp_in = din("selexp", (DC, H, 128))
    nwsel_in = din("nwsel", (H, D))

    out = nc.dram_tensor("out", [owntok, D], F32, kind="ExternalOutput").ap()
    out_t = out.rearrange("(a p) d -> p a d", p=128)
    ln2t_scr = (nc.dram_tensor("ln2t_scr", [128, DC * owntok], BF16).ap()
                if ffn else None)
    wscr_t = nc.dram_tensor("wscr", [3 * DC, 128, DC, 128], BF16).ap()
    PRI = {"q": 0, "k": 1, "v": 2}
    wcast_done = set()

    taps = {}
    def tap(name, shape, dt=F32):
        ap = nc.dram_tensor("tap_" + name, list(shape), dt,
                            kind="ExternalOutput").ap()
        taps[name] = ap
        return ap

    es = contextlib.ExitStack()
    def sbuf_in(stack, name, shape, dt):
        return stack.enter_context(nc.sbuf_tensor(name, list(shape), dt))

    def sbuf(name, shape, dt):
        return sbuf_in(es, name, shape, dt)

    g_sb = sbuf("g_sb", (128, max(owntok // 128, 1), NE), F32)
    ones1 = sbuf("ones1", (1, 128), BF16)

    ps_f_t = es.enter_context(nc.psum_tensor("ps_f", [128, 512], F32))
    ps_a_t = es.enter_context(nc.psum_tensor("ps_a", [128, 1024], F32))
    ps_b_t = es.enter_context(nc.psum_tensor("ps_b", [128, 1024], F32))
    ps_e_t = es.enter_context(nc.psum_tensor("ps_e", [128, 512], F32))
    ps_tb_t = es.enter_context(nc.psum_tensor("ps_tb", [128, 2048], BF16))

    def pkeys(name, lo_el, hi_el, el_per_bank):
        b0, b1 = lo_el // el_per_bank, (hi_el + el_per_bank - 1) // el_per_bank
        return [f"{name}:{i}" for i in range(b0, b1)]

    def ka(lo, hi):  return pkeys("ps_a", lo, hi, 512)
    def kb(lo, hi):  return pkeys("ps_b", lo, hi, 512)
    def ktb(lo, hi): return pkeys("ps_tb", lo, hi, 1024)
    KF_, KE_ = ["ps_f:0"], ["ps_e:0"]

    seq = [0]
    def utag(pfx=""):
        seq[0] += 1
        return f"{pfx}{seq[0]}"

    def dma(queue, dst, src, reads, writes, tag=""):
        p.emit(queue, lambda e, d=dst, s=src: e.dma_start(d, s),
               reads=reads, writes=writes, dma=True, tag=tag or utag("dma"))

    def dma_acc(queue, dst, src, reads, writes, tag=""):
        p.emit(queue,
               lambda e, d=dst, s=src: e.dma_start(d, s, accum_op=ALU.add),
               reads=reads, writes=writes, dma=True, tag=tag or utag("dmac"))

    def mm(out_ap, lhsT, rhs, reads, writes, start=True, stop=True, tag=""):
        p.emit("pe",
               lambda e, o=out_ap, l=lhsT, r=rhs, st=start, sp=stop:
                   e.matmul(o, l, r, start=st, stop=sp, skip_group_check=True),
               reads=reads, writes=writes, tag=tag or utag("mm"))

    def tpose(out_ap, in_ap, idt, reads, writes, tag=""):
        p.emit("pe", lambda e, o=out_ap, i=in_ap, d=idt: e.transpose(o, i, d),
               reads=reads, writes=writes, tag=tag or utag("tp"))

    def act(out_ap, in_ap, func, reads, writes, bias=..., scale=...,
            accum_out=None, tag=""):
        kw = {}
        if bias is not ...:
            kw["bias"] = bias
        if scale is not ...:
            kw["scale"] = scale
        if accum_out is not None:
            kw["accum_out"] = accum_out
        p.emit("act",
               lambda e, o=out_ap, i=in_ap, f=func, kw=kw:
                   e.activation(o, i, f, **kw),
               reads=reads, writes=writes, tag=tag or utag("act"))

    def ts(out_ap, in_ap, s1, op0, s2, op1, reads, writes, tag="", eng="dve"):
        p.emit(eng,
               lambda e, o=out_ap, i=in_ap, a=s1, b=s2, o0=op0, o1=op1:
                   e.tensor_scalar(o, i, a, b, o0, o1),
               reads=reads, writes=writes, tag=tag or utag("ts"))

    def tt(out_ap, in0, in1, op, reads, writes, tag="", eng="dve"):
        p.emit(eng,
               lambda e, o=out_ap, a=in0, b=in1, oo=op:
                   e.tensor_tensor(o, a, b, oo),
               reads=reads, writes=writes, tag=tag or utag("tt"))

    def acopy(out_ap, in_ap, reads, writes, scale=None, tag=""):
        if scale is None:
            p.emit("act", lambda e, o=out_ap, i=in_ap: e.copy(o, i),
                   reads=reads, writes=writes, tag=tag or utag("ac"))
        else:
            p.emit("act",
                   lambda e, o=out_ap, i=in_ap, sc=scale:
                       e.activation(o, i, AF.Copy, scale=sc),
                   reads=reads, writes=writes, tag=tag or utag("ac"))

    def stt(out_ap, in0, scalar, in1, op0, op1, reads, writes, tag=""):
        p.emit("dve",
               lambda e, o=out_ap, a=in0, s=scalar, b=in1, o0=op0, o1=op1:
                   e.scalar_tensor_tensor(o, a, s, b, o0, o1),
               reads=reads, writes=writes, tag=tag or utag("stt"))

    def dve(fn, reads, writes, tag=""):
        p.emit("dve", fn, reads=reads, writes=writes, tag=tag or utag("dve"))

    esm = contextlib.ExitStack()
    def msb(name, shape, dt):
        return sbuf_in(esm, name, shape, dt)

    identb = msb("identb", (128, 128), BF16)
    identf16 = msb("identf16", (16, 16), F32)
    ident64b = msb("ident64b", (128, 64), BF16)
    trils_b = msb("trils_b", (128, 128), BF16)
    md16_b = msb("md16_b", (128, 128), BF16)
    md16t_b = msb("md16t_b", (128, 128), BF16)
    mo16_b = msb("mo16_b", (128, 128), BF16)
    mo32_b = msb("mo32_b", (128, 128), BF16)
    triui_b = msb("triui_b", (128, 128), BF16)
    selnorm_b = msb("selnorm_b", (128, DC, H), BF16)
    selexp_b = msb("selexp_b", (H, DC, 128), BF16)
    nwsel_b = msb("nwsel_b", (H, D), BF16)
    wstr = {pr: msb(f"w{pr}str", (128, 2, DC, 128), BF16) for pr in "qkv"}
    wo_sb = msb("wo_sb", (128, DC, D), BF16)
    wbeta_sb = msb("wbeta_sb", (128, DC, H), BF16)
    wg_sb = msb("wg_sb", (128, DC, NE), BF16)
    bias_sb = {n: msb(n + "_sb", (128, DC), F32) for n in ("bq", "bk", "bv")}
    bconv_sb = {pr: msb(f"b{pr}c_sb", (128, DC), F32) for pr in "qkv"}
    wconv_sb = {pr: msb(f"w{pr}c_sb", (128, DC, KC), F32) for pr in "qkv"}
    diag_sb = [msb(f"diag{i}", (128, KC, 128), BF16) for i in range(2)]
    bo_sb = msb("bo_sb", (1, D), BF16)
    bbeta_sb = msb("bbeta_sb", (H, 1), F32)
    hmask_sb = msb("hmask_sb", (128, 1), F32)
    kvmask_sb = msb("kvmask_sb", (H, SEG), F32) if pre else None
    cpool = msb("cpool", (128, 2), F32)

    x_sb = msb("x_sb", (128, NCH, D), F32)
    stat = msb("stat", (128, 32), F32)
    ln_tm = msb("ln_tm", (128, D), BF16)
    ln_tm2 = msb("ln_tm2", (128, D), BF16)
    x_res = msb("x_res", (128, 2, D), F32)
    ln1t = msb("ln1t", (128, DC, SEG), BF16)
    pre_sc = [msb(f"pre_sc{i}", (128, 3 + SEG), BF16) for i in range(2)]
    tail_sb = {pr: msb(f"tail_{pr}", (128, DC, 3), BF16) for pr in "qkv"}
    sq_sc = [msb(f"sq_sc{i}", (128, SEG), BF16) for i in range(2)]
    hat = {pr: msb(pr + "_hat", (128, 2, DC, SEG), BF16) for pr in "qkv"}
    rn_sc = msb("rn_sc", (H, SEG), F32)
    rnorm = msb("rnorm", (H, 2 * SEG), BF16)
    beta_fm = msb("beta_fm", (H, SEG), F32)
    betat = msb("betat", (128, 2, NCH, H), F32)

    A_b = msb("A_b", (128, H * 128), BF16)
    Bp = [msb(f"Bp{j}", (128, H * 128), BF16) for j in range(2)]
    Cp = [msb(f"Cp{j}", (128, H * 128), BF16) for j in range(2)]
    t_b = [msb(f"t_b{j}", (128, H * 128), BF16) for j in range(2)]
    M_b = msb("M_b", (128, H * 128), BF16)
    MT_b = msb("MT_b", (128, H * 128), BF16)
    kn_b = msb("kn_b", (128, H * HD), BF16)
    wt_b = msb("wt_b", (128, 8 * 128), BF16)
    u_b = msb("u_b", (128, H * HD), BF16)
    s_sb = msb("s_sb", (128, 8 * HD), BF16)
    s_f32 = msb("s_f32", (128, 8 * HD), F32)
    rms_b = msb("rms_b", (128, 32), F32)
    rstdt_b = msb("rstdt_b", (H, 128), BF16)
    rmsb16 = msb("rmsb16", (128, H), BF16)
    nwex_b = msb("nwex_b", (128, H * HD), BF16)
    onat_b = msb("onat_b", (128, H * HD), BF16)
    ot_b = msb("ot_b", (128, DC, SEG), BF16)
    x2_b = msb("x2_b", (128, D), F32)
    ln2t = msb("ln2t", (128, DC, SEG), BF16)
    gt_sc = msb("gt_sc", (128, 32), F32)
    sc_tap = msb("sc_tap", (128, NCH * NE), F32) if "scores" in debug_taps else None

    dma("sp", identf16[:], ident_in[0:16, 0:16], [], ["identf16"])
    dma("pool", identb[:], ident_in, [], ["identb"])
    dma("pool", ident64b[:], ident64_in, [], ["ident64b"])
    dma("pool", trils_b[:], trils_in, [], ["trils_b"])
    dma("pool", md16_b[:], md16_in, [], ["md16_b"])
    dma("pool", md16t_b[:], md16t_in, [], ["md16t_b"])
    dma("pool", mo16_b[:], mo16_in, [], ["mo16_b"])
    dma("pool", mo32_b[:], mo32_in, [], ["mo32_b"])
    dma("pool", triui_b[:], triui_in, [], ["triui_b"])
    dma("pool", selnorm_b[:], selnorm_in.rearrange("c p h -> p c h"), [],
        ["selnorm_b"])
    dma("pool", selexp_b[:], selexp_in.rearrange("c h p -> h c p"), [],
        ["selexp_b"])
    dma("pool", nwsel_b[:], nwsel_in, [], ["nwsel_b"])
    dve(lambda e: e.memset(ones1[:], 1.0), [], ["ones1"])
    dve(lambda e: e.memset(s_sb[:], 0.0), [], ["s_sb"])
    dve(lambda e: e.memset(s_f32[:], 0.0), [], ["s_f32"])
    dma("pool", wo_sb[:], wot.rearrange("(c p) o -> p c o", p=128), [],
        ["wo_sb"])
    wsrc = {"q": wqt.rearrange("(c p) o -> p c o", p=128),
            "k": wkt.rearrange("(c p) o -> p c o", p=128),
            "v": wvt.rearrange("(c p) o -> p c o", p=128)}
    dma("pool", wbeta_sb[:], wbetat.rearrange("c p h -> p c h"), [], ["wbeta_sb"])
    dma("pool", wg_sb[:], wgt.rearrange("c p h -> p c h"), [], ["wg_sb"])
    for nm in ("bq", "bk", "bv"):
        dma("sp", bias_sb[nm][:], b_in[nm], [], [nm + "_sb"])
    for pr in "qkv":
        dma("sp", bconv_sb[pr][:], bconv_in[pr], [], [f"b{pr}c_sb"])
        dma("sp", wconv_sb[pr][:], wconv_in[pr], [], [f"w{pr}c_sb"])
    dma("pool", bo_sb[:], bo_row, [], ["bo_sb"])
    dma("sp", bbeta_sb[:], bbeta_in, [], ["bbeta_sb"])
    dma("sp", hmask_sb[:], halomask, [], ["hmask_sb"])
    dve(lambda e: e.memset(cpool[:, 0:1], LN_EPS), [], ["cpool"])
    dve(lambda e: e.memset(cpool[:, 1:2], EPS), [], ["cpool"])
    for pr in "qkv":
        dve(lambda e, t=tail_sb[pr]: e.memset(t[:], 0.0), [], [f"tail_{pr}"])

    BMAP = {"q": "bq", "k": "bk", "v": "bv"}
    pspar = {0: (ps_a_t, ka), 1: (ps_b_t, kb)}

    def fe_emit(s):
        full = (s >= pre) or full_prefix
        sp1 = s % 2
        SX = f"s{s}f:"

        def wload(pr, oc):
            par = oc % 2
            if (pr, oc) in wcast_done:
                dma("act", wstr[pr][:, par], wscr_t[PRI[pr] * DC + oc],
                    [f"wscr{pr}{oc}"], [f"w{pr}s{par}"],
                    tag=SX + f"wl{pr}{oc}")
            else:
                wcast_done.add((pr, oc))
                dma("pool", wstr[pr][:, par],
                    wsrc[pr][:, :, oc * 128:oc * 128 + 128], [],
                    [f"w{pr}s{par}"], tag=SX + f"wl{pr}{oc}")
                dma("act", wscr_t[PRI[pr] * DC + oc], wstr[pr][:, par],
                    [f"w{pr}s{par}"], [f"wscr{pr}{oc}"],
                    tag=SX + f"wst{pr}{oc}")

        x_hbm = x_in.rearrange("(a p) d -> p a d", p=128)[:, NCH * s:NCH * (s + 1), :]
        for a_ in range(NCH):
            dma("sp", x_sb[:, a_, :], x_hbm[:, a_, :], [], [f"x_sb{a_}"],
                tag=SX + f"xload{a_}")
        if s < pre:
            dma("sp", kvmask_sb[:], kvmask[:, s * SEG:(s + 1) * SEG],
                [], ["kvmask_sb"], tag=SX + "kvm")
        prs0 = [p_ for p_ in ("k", "v") if p_ in fe_parts]
        if "q" in fe_parts and (full or s == pre - 1):
            prs0.append("q")
        for p_ in prs0:
            wload(p_, 0)
            wload(p_, 1)

        for a in range(NCH):
            xt = x_sb[:, a, :]
            dve(lambda e, o=stat[:, 0:6], i=xt[:, 0:512]: e.bn_stats(o, i),
                [f"x_sb{a}"], ["stat"])
            dve(lambda e, o=stat[:, 6:12], i=xt[:, 512:1024]: e.bn_stats(o, i),
                [f"x_sb{a}"], ["stat"])
            dve(lambda e, o=stat[:, 12:14], i=stat[:, 0:12]: e.bn_aggr(o, i),
                ["stat"], ["stat"])
            act(stat[:, 14:15], stat[:, 13:14], AF.Sqrt,
                ["stat", "cpool"], ["stat"], bias=cpool[:, 0:1])
            dve(lambda e, o=stat[:, 15:16], i=stat[:, 14:15]:
                    e.reciprocal(o, i), ["stat"], ["stat"])
            ts(ln_tm[:], xt, stat[:, 12:13], ALU.subtract, stat[:, 15:16],
               ALU.mult, [f"x_sb{a}", "stat"], ["ln_tm"], tag=SX + f"ln{a}")
            for dc in range(DC):
                tpose(ps_tb_t[:, dc * 128:dc * 128 + 128],
                      ln_tm[:, dc * 128:dc * 128 + 128], identb[:],
                      ["ln_tm", "identb"], ktb(dc * 128, dc * 128 + 128))
            ts(ln1t[:, :, a * 128:a * 128 + 128],
               ps_tb_t[:, 0:1024].rearrange("p (c t) -> p c t", t=128),
               1.0, ALU.mult, None, ALU.bypass,
               ktb(0, 1024), ["ln1t"], tag=SX + f"lnT{a}")
            yield

        for dc in (range(DC) if "beta" in fe_parts else ()):
            mm(ps_e_t[0:H, 0:SEG], wbeta_sb[:, dc, :], ln1t[:, dc, :],
               ["wbeta_sb", "ln1t"], KE_, start=(dc == 0), stop=(dc == DC - 1))
        if "beta" not in fe_parts:
            pass
        else:
          act(beta_fm[:], ps_e_t[0:H, 0:SEG], AF.Sigmoid,
            KE_ + ["bbeta_sb"], ["beta_fm"], bias=bbeta_sb[:])
          for c in range(NCH):
            tpose(ps_f_t[:, c * 128:c * 128 + 16],
                  beta_fm[:, c * 128:c * 128 + 128], identf16[:],
                  ["beta_fm", "identf16"], KF_)
          ts(betat[:, sp1],
             ps_f_t[:, 0:512].rearrange("p (c h) -> p c h", h=128)[:, :, 0:H],
             1.0, ALU.mult, None, ALU.bypass, KF_, [f"betat{sp1}"],
             tag=SX + "betat")
          yield

        if pre and s == pre:
            for pr in "qkv":
                ts(tail_sb[pr][:].rearrange("p c t -> p (c t)"),
                   tail_sb[pr][:].rearrange("p c t -> p (c t)"),
                   hmask_sb[:], ALU.mult, None, ALU.bypass,
                   [f"tail_{pr}", "hmask_sb"], [f"tail_{pr}"], tag=SX + pr + "hm")

        FE_B = {0: (ps_f_t, KF_), 1: (ps_e_t, KE_)}

        def proj_emit(pr, oc):
            par = oc % 2
            wkey = f"w{pr}s{par}"
            bkey = BMAP[pr]
            pb_, ppk = FE_B[par]
            pp = pb_[:, 0:512]
            for dc in range(DC):
                mm(pp, wstr[pr][:, par, dc, :],
                   ln1t[:, dc, :], [wkey, "ln1t"], ppk,
                   start=(dc == 0), stop=(dc == DC - 1))
            if oc + 2 < DC:
                wload(pr, oc + 2)
            sc = pre_sc[par]
            sck = f"pre_sc{par}"
            if not conv_on or oc >= conv_ocs:
                ts(hat[pr][:, sp1, oc, :], pp,
                   bias_sb[bkey][:, oc:oc + 1],
                   ALU.add, None, ALU.bypass, ppk + [bkey + "_sb"],
                   [f"{pr}_hat{sp1}"], tag=SX + f"{pr}nc{oc}")
                return
            if "halo" not in cv_omit:
                ts(sc[:, 0:3], tail_sb[pr][:, oc, :], 1.0, ALU.mult, None,
                   ALU.bypass, [f"tail_{pr}"], [sck])
            if "evac" not in cv_omit:
                ts(sc[:, 3:3 + SEG], pp, bias_sb[bkey][:, oc:oc + 1],
                   ALU.add, None, ALU.bypass, ppk + [bkey + "_sb"], [sck],
                   tag=SX + f"{pr}pe{oc}")
            if "diag" not in cv_omit:
                dg = diag_sb[par]
                dgk = f"diag{par}"
                for k in range(KC):
                    ts(dg[:, k, :], identb[:], wconv_sb[pr][:, oc, k:k + 1],
                       ALU.mult, None, ALU.bypass,
                       ["identb", f"w{pr}c_sb"], [dgk])

        def conv_emit(pr, oc):
            if not conv_on or oc >= conv_ocs:
                return
            par = oc % 2
            sc = pre_sc[par]
            sck = f"pre_sc{par}"
            dg = diag_sb[par]
            dgk = f"diag{par}"
            pb_, cpk = FE_B[par]
            cp = pb_[:, 0:512]
            if "mm" not in cv_omit:
                for k in range(KC):
                    mm(cp, dg[:, k, :], sc[:, k:k + SEG],
                       [dgk, sck], cpk, start=(k == 0),
                       stop=(k == KC - 1))
            if "tail" not in cv_omit:
                ts(tail_sb[pr][:, oc, :], sc[:, SEG:SEG + 3], 1.0,
                   ALU.mult, None, ALU.bypass, [sck], [f"tail_{pr}"])
            if pr == "v":
                act(hat["v"][:, sp1, oc, :], cp, AF.Silu,
                    cpk + ["bvc_sb"], [f"v_hat{sp1}"],
                    bias=bconv_sb["v"][:, oc:oc + 1])
            else:
                sq = sq_sc[par]
                sqk = f"sq_sc{par}"
                if "sq" not in cv_omit:
                    act(sq[:], cp, AF.Square, cpk + [f"b{pr}c_sb"], [sqk],
                        bias=bconv_sb[pr][:, oc:oc + 1])
                if "raw" not in cv_omit:
                    ts(hat[pr][:, sp1, oc, :], cp,
                       bconv_sb[pr][:, oc:oc + 1],
                       ALU.add, None, ALU.bypass, cpk + [f"b{pr}c_sb"],
                       [f"{pr}_hat{sp1}"])

        def nm_emit(pr, oc):
            if (not conv_on or oc >= conv_ocs or pr == "v" or not norm_on
                    or "sq" in cv_omit):
                return
            par = oc % 2
            sq = sq_sc[par]
            sqk = f"sq_sc{par}"
            pb_, npk = FE_B[par]
            mm(pb_[0:H, 0:SEG], selnorm_b[:, oc, :], sq[:],
               ["selnorm_b", sqk], npk, tag=SX + f"nm{pr}{oc}")
            if oc == 0:
                acopy(rn_sc[:], pb_[0:H, 0:SEG], npk, ["rn_sc"],
                      tag=SX + f"na{pr}{oc}")
            else:
                tt(rn_sc[:], rn_sc[:], pb_[0:H, 0:SEG], ALU.add,
                   ["rn_sc"] + npk, ["rn_sc"], tag=SX + f"na{pr}{oc}")

        def proj_conv(pr):
            for ocp in range(0, DC, 2):
                proj_emit(pr, ocp)
                proj_emit(pr, ocp + 1)
                yield
                conv_emit(pr, ocp)
                conv_emit(pr, ocp + 1)
                nm_emit(pr, ocp)
                nm_emit(pr, ocp + 1)
                yield

        def finish_norm(pr, off):
            act(rn_sc[:], rn_sc[:], AF.Sqrt, ["rn_sc"], ["rn_sc"])
            ts(rn_sc[:], rn_sc[:], EPS, ALU.add,
               None, ALU.bypass, ["rn_sc"], ["rn_sc"])
            dve(lambda e, o=rn_sc[:], i=rn_sc[:]:
                    e.reciprocal(o, i), ["rn_sc"], ["rn_sc"])
            ts(rnorm[:, off:off + SEG], rn_sc[:], 1.0, ALU.mult,
               None, ALU.bypass, ["rn_sc"], ["rnorm"])
            if pr == "k" and s < pre:
                tt(rnorm[:, off:off + SEG], rnorm[:, off:off + SEG],
                   kvmask_sb[:], ALU.mult,
                   ["rnorm", "kvmask_sb"], ["rnorm"], tag=SX + "kmask")

        if "k" in fe_parts:
            yield from proj_conv("k")
            if conv_on and norm_on:
                finish_norm("k", 0)
        if "v" in fe_parts:
            yield from proj_conv("v")
        for oc in (range(DC) if "v" in fe_parts else ()):
            pe_, pek = FE_B[oc % 2]
            pe_ = pe_[:, 0:512]
            mm(pe_, selexp_b[:, oc, :], rnorm[0:H, 0:SEG],
               ["selexp_b", "rnorm"], pek)
            tt(hat["k"][:, sp1, oc, :], hat["k"][:, sp1, oc, :], pe_,
               ALU.mult, [f"k_hat{sp1}"] + pek, [f"k_hat{sp1}"],
               tag=SX + f"kap{oc}")
            tt(hat["v"][:, sp1, oc, :], hat["v"][:, sp1, oc, :], pe_,
               ALU.mult, [f"v_hat{sp1}"] + pek, [f"v_hat{sp1}"],
               tag=SX + f"vap{oc}")
            if oc % 2 == 1:
                yield
        if full and "q" in fe_parts:
            yield from proj_conv("q")
            finish_norm("q", SEG)
            for oc in range(DC):
                pe_, pek = FE_B[oc % 2]
                pe_ = pe_[:, 0:512]
                mm(pe_, selexp_b[:, oc, :], rnorm[0:H, SEG:2 * SEG],
                   ["selexp_b", "rnorm"], pek)
                tt(hat["q"][:, sp1, oc, :], hat["q"][:, sp1, oc, :], pe_,
                   ALU.mult, [f"q_hat{sp1}"] + pek, [f"q_hat{sp1}"],
                   tag=SX + f"qap{oc}")
                if oc % 2 == 1:
                    yield
        elif s == pre - 1 and "q" in fe_parts:
            for oc in range(DC):
                par = oc % 2
                for dc in range(DC):
                    mm(ps_f_t[:, 0:3], wstr["q"][:, par, dc, :],
                       ln1t[:, dc, SEG - 3:SEG], [f"wqs{par}", "ln1t"], KF_,
                       start=(dc == 0), stop=(dc == DC - 1))
                if oc + 2 < DC:
                    wload("q", oc + 2)
                ts(tail_sb["q"][:, oc, :], ps_f_t[:, 0:3],
                   bias_sb["bq"][:, oc:oc + 1], ALU.add, None, ALU.bypass,
                   KF_ + ["bq_sb"], ["tail_q"], tag=SX + f"qtail{oc}")
                yield

        if s == nseg - 1:
            for nm, key, buf, dt in (("ln1t", "ln1t", ln1t, BF16),
                                     ("khat", f"k_hat{sp1}", hat["k"][:, sp1],
                                      BF16),
                                     ("vhat", f"v_hat{sp1}", hat["v"][:, sp1],
                                      BF16),
                                     ("qhat", f"q_hat{sp1}", hat["q"][:, sp1],
                                      BF16),
                                     ("beta", "beta_fm", beta_fm, F32)):
                if nm in debug_taps:
                    bap = buf[:] if hasattr(buf, "ap") and not isinstance(buf, bass.AP) else buf
                    dma("sp", tap(nm, bap.shape, dt), bap, [key],
                        ["tap_" + nm])

    def dt_emit(s):
        full = (s >= pre) or full_prefix
        osg = s - pre
        sp1 = s % 2
        SX = f"s{s}:"

        def tail_emit(tb):
            t0 = tb * 128
            dma("act", x_res[:, tb % 2],
                x_in.rearrange("(a p) d -> p a d", p=128)[:, NCH * s + tb, :],
                [], [f"x_res{tb % 2}"], tag=SX + f"xres{tb}")
            pw = ps_a_t[:, 0:1024]
            pwk = ka(0, 1024)
            for dc in range(DC):
                for nh in range(2):
                    mm(pw[:, nh * 512:nh * 512 + 512],
                       ot_b[:, dc, t0:t0 + 128],
                       wo_sb[:, dc, nh * 512:nh * 512 + 512],
                       [f"ot_b{tb}", "wo_sb"], pwk,
                       start=(dc == 0), stop=False, tag=SX + f"wo{tb}_{dc}{nh}")
            for nh in range(2):
                mm(pw[:, nh * 512:nh * 512 + 512], ones1[:],
                   bo_sb[:, nh * 512:nh * 512 + 512],
                   ["ones1", "bo_sb"], pwk, start=False, stop=True,
                   tag=SX + f"wob{tb}{nh}")
            tt(x2_b[:], pw, x_res[:, tb % 2], ALU.add,
               pwk + [f"x_res{tb % 2}"], ["x2_b"], tag=SX + f"x2_{tb}")
            dma("sp", out_t[:, osg * NCH + tb, :], x2_b[:],
                ["x2_b"], [f"out:{osg}:{tb}"], tag=SX + f"outinit{tb}")
            yield
            dve(lambda e, o=stat[:, 16:22], i=x2_b[:, 0:512]: e.bn_stats(o, i),
                ["x2_b"], ["stat2"])
            dve(lambda e, o=stat[:, 22:28], i=x2_b[:, 512:1024]:
                    e.bn_stats(o, i), ["x2_b"], ["stat2"])
            dve(lambda e, o=stat[:, 28:30], i=stat[:, 16:28]: e.bn_aggr(o, i),
                ["stat2"], ["stat2"])
            act(stat[:, 30:31], stat[:, 29:30], AF.Sqrt,
                ["stat2", "cpool"], ["stat2"], bias=cpool[:, 0:1])
            dve(lambda e, o=stat[:, 31:32], i=stat[:, 30:31]:
                    e.reciprocal(o, i), ["stat2"], ["stat2"])
            ts(ln_tm2[:], x2_b[:], stat[:, 28:29], ALU.subtract,
               stat[:, 31:32], ALU.mult, ["x2_b", "stat2"], ["ln_tm2"],
               tag=SX + f"ln2_{tb}")
            for dc in range(DC):
                tpose(ps_tb_t[:, dc * 128:dc * 128 + 128],
                      ln_tm2[:, dc * 128:dc * 128 + 128], identb[:],
                      ["ln_tm2", "identb"], ktb(dc * 128, dc * 128 + 128))
            ts(ln2t[:, :, t0:t0 + 128],
               ps_tb_t[:, 0:1024].rearrange("p (c t) -> p c t", t=128),
               1.0, ALU.mult, None, ALU.bypass, ktb(0, 1024), ["ln2t"],
               tag=SX + f"ln2T{tb}")
            yield
            gp = ps_b_t[:, 0:NE]
            KG_ = kb(0, 512)
            for dc in range(DC):
                mm(gp, ln2t[:, dc, t0:t0 + 128], wg_sb[:, dc, :],
                   ["ln2t", "wg_sb"], KG_, start=(dc == 0), stop=(dc == DC - 1))
            gx = osg * NCH + tb
            if sc_tap is not None:
                ts(sc_tap[:, tb * NE:(tb + 1) * NE], gp, 1.0, ALU.mult, None,
                   ALU.bypass, KG_, ["sc_tap"])
            dve(lambda e, o=gt_sc[:, 8:9], i=gp: e.reduce_max(o, i, axis=AX.X),
                KG_, ["gt"])
            ts(gt_sc[:, 0:8], gp, gt_sc[:, 8:9], ALU.is_equal, None,
               ALU.bypass, KG_ + ["gt"], ["gt"])
            stt(gt_sc[:, 16:24], gt_sc[:, 0:8], -1e30, gp, ALU.mult, ALU.add,
                ["gt"] + KG_, ["gt"])
            dve(lambda e, o=gt_sc[:, 9:10], i=gt_sc[:, 16:24]:
                    e.reduce_max(o, i, axis=AX.X), ["gt"], ["gt"])
            ts(gt_sc[:, 24:32], gt_sc[:, 16:24], gt_sc[:, 9:10], ALU.is_equal,
               None, ALU.bypass, ["gt"], ["gt"])
            tt(gt_sc[:, 10:11], gt_sc[:, 9:10], gt_sc[:, 8:9], ALU.subtract,
               ["gt"], ["gt"])
            act(gt_sc[:, 11:12], gt_sc[:, 10:11], AF.Sigmoid, ["gt"], ["gt"])
            ts(gt_sc[:, 12:13], gt_sc[:, 11:12], -1.0, ALU.mult, 1.0, ALU.add,
               ["gt"], ["gt"])
            ts(gt_sc[:, 24:32], gt_sc[:, 24:32], gt_sc[:, 11:12], ALU.mult,
               None, ALU.bypass, ["gt"], ["gt"])
            stt(g_sb[:, gx, :], gt_sc[:, 0:8], gt_sc[:, 12:13],
                gt_sc[:, 24:32], ALU.mult, ALU.add, ["gt"], ["g_sb"],
                tag=SX + f"g{tb}")
            yield
        for c in (range(NCH) if do_delta else ()):
            c0, c1 = c * CH, (c + 1) * CH
            CX = SX + f"c{c}:"

            for h in HORDER:
                pt, kf = pspar[h % 2]
                sl = h // 2
                kh = head_slice(hat["k"], sp1, h, c0, c1)
                mm(pt[:, sl * 128:sl * 128 + 128], kh, kh,
                   [f"k_hat{sp1}"], kf(sl * 128, sl * 128 + 128),
                   tag=CX + f"kk{h}")
            for h in HORDER:
                pt, kf = pspar[h % 2]
                sl = h // 2
                stt(A_b[:, hp(h) * 128:hp(h) * 128 + 128],
                    pt[:, sl * 128:sl * 128 + 128],
                    betat[:, sp1, c, h:h + 1], trils_b[:], ALU.mult, ALU.mult,
                    kf(sl * 128, sl * 128 + 128) + [f"betat{sp1}", "trils_b"],
                    ["A_b"], tag=CX + f"A{h}")
            yield
            for r in range(H):
                tpose(ps_tb_t[:, r * 128:r * 128 + 128],
                      A_b[:, r * 128:r * 128 + 128], identb[:],
                      ["A_b", "identb"], ktb(r * 128, r * 128 + 128))
            acopy(Bp[0][:, 0:1024], ps_tb_t[:, 0:1024], ktb(0, 1024),
                  ["Bp0"], tag=CX + "B0ev0")
            ts(Bp[0][:, 1024:2048], ps_tb_t[:, 1024:2048], 1.0, ALU.mult,
               None, ALU.bypass, ktb(1024, 2048), ["Bp0"], tag=CX + "B0ev1")
            yield
            for h in range(H):
                kh = head_slice(hat["k"], sp1, h, c0, c1)
                vh = head_slice(hat["v"], sp1, h, c0, c1)
                par = h % 2
                idt = ident64b[64 * par:64 * par + 64, :]
                r = hp(h)
                tpose(ps_tb_t[:, r * 128:r * 128 + 64], kh, idt,
                      [f"k_hat{sp1}", "ident64b"], ktb(r * 128, r * 128 + 64))
                tpose(ps_tb_t[:, r * 128 + 64:r * 128 + 128], vh, idt,
                      [f"v_hat{sp1}", "ident64b"],
                      ktb(r * 128 + 64, r * 128 + 128))
            for h in range(H):
                r = hp(h)
                ts(t_b[0][:, r * 128:r * 128 + 128],
                   ps_tb_t[:, r * 128:r * 128 + 128],
                   betat[:, sp1, c, h:h + 1],
                   ALU.mult, None, ALU.bypass,
                   ktb(r * 128, r * 128 + 128) + [f"betat{sp1}"], ["t0"],
                   tag=CX + f"t0{h}")
            acopy(kn_b[:].rearrange("p (r d) -> p r d", d=HD),
                  ps_tb_t[:].rearrange("p (r two d) -> p r two d",
                                       two=2, d=HD)[:, :, 0, :],
                  ktb(0, 2048), ["kn_b"], tag=CX + "knev")

            yield
            for h in range(H):
                r = hp(h)
                sl_ = slice(r * 128, r * 128 + 128)
                tt(Cp[0][:, sl_], A_b[:, sl_], md16_b[:], ALU.mult,
                   ["A_b", "md16_b"], ["Cp0"], tag=CX + f"d0_{h}")
                tt(Cp[1][:, sl_], Bp[0][:, sl_], md16t_b[:], ALU.mult,
                   ["Bp0", "md16t_b"], ["Cp1"], tag=CX + f"d0t_{h}")
                tt(M_b[:, sl_], identb[:], Cp[0][:, sl_], ALU.subtract,
                   ["identb", "Cp0"], ["M_b"], tag=CX + f"mi_{h}")
                tt(MT_b[:, sl_], identb[:], Cp[1][:, sl_], ALU.subtract,
                   ["identb", "Cp1"], ["MT_b"], tag=CX + f"mti_{h}")

            yield

            def sq_pair(srck, srctk, dstbuf, dstk, dsttbuf, dsttk, tg):
                srcb = {"Cp0": Cp[0], "Cp1": Cp[1], "Bp1": Bp[1],
                        "t1": t_b[1]}[srck]
                srctb = {"Cp0": Cp[0], "Cp1": Cp[1], "Bp1": Bp[1],
                         "t1": t_b[1]}[srctk]
                for h in HORDER:
                    pt, kf = pspar[h % 2]
                    sl = h // 2
                    r = hp(h)
                    mm(pt[:, sl * 128:sl * 128 + 128],
                       srctb[:, r * 128:r * 128 + 128],
                       srcb[:, r * 128:r * 128 + 128],
                       [srck, srctk], kf(sl * 128, sl * 128 + 128),
                       tag=CX + tg + f"s{h}")
                acopy(dstbuf[:, 0:1024], pspar[0][0][:, 0:1024],
                      pspar[0][1](0, 1024), [dstk], tag=CX + tg + "e0")
                ts(dstbuf[:, 1024:2048], pspar[1][0][:, 0:1024], 1.0,
                   ALU.mult, None, ALU.bypass, pspar[1][1](0, 1024), [dstk],
                   tag=CX + tg + "e1")
                yield
                for h in HORDER:
                    pt, kf = pspar[h % 2]
                    sl = h // 2
                    r = hp(h)
                    mm(pt[:, sl * 128:sl * 128 + 128],
                       srcb[:, r * 128:r * 128 + 128],
                       srctb[:, r * 128:r * 128 + 128],
                       [srck, srctk], kf(sl * 128, sl * 128 + 128),
                       tag=CX + tg + f"st{h}")
                acopy(dsttbuf[:, 0:1024], pspar[0][0][:, 0:1024],
                      pspar[0][1](0, 1024), [dsttk], tag=CX + tg + "et0")
                ts(dsttbuf[:, 1024:2048], pspar[1][0][:, 0:1024], 1.0,
                   ALU.mult, None, ALU.bypass, pspar[1][1](0, 1024), [dsttk],
                   tag=CX + tg + "et1")
                yield

            def leaf_prod(pk_, ptk_, tg):
                pb_ = {"Cp0": Cp[0], "Cp1": Cp[1], "Bp1": Bp[1],
                       "t1": t_b[1]}[pk_]
                ptb_ = {"Cp0": Cp[0], "Cp1": Cp[1], "Bp1": Bp[1],
                        "t1": t_b[1]}[ptk_]
                for h in HORDER:
                    pt, kf = pspar[h % 2]
                    sl = h // 2
                    r = hp(h)
                    mm(pt[:, sl * 128:sl * 128 + 128],
                       MT_b[:, r * 128:r * 128 + 128],
                       pb_[:, r * 128:r * 128 + 128],
                       ["MT_b", pk_], kf(sl * 128, sl * 128 + 128),
                       tag=CX + tg + f"p{h}")
                for hh in range(2):
                    pt, kf = pspar[hh]
                    tt(M_b[:, hh * 1024:hh * 1024 + 1024],
                       M_b[:, hh * 1024:hh * 1024 + 1024], pt[:, 0:1024],
                       ALU.add, ["M_b"] + kf(0, 1024), ["M_b"],
                       tag=CX + tg + f"pe{hh}")
                yield
                for h in HORDER:
                    pt, kf = pspar[h % 2]
                    sl = h // 2
                    r = hp(h)
                    mm(pt[:, sl * 128:sl * 128 + 128],
                       pb_[:, r * 128:r * 128 + 128],
                       MT_b[:, r * 128:r * 128 + 128],
                       ["MT_b", pk_], kf(sl * 128, sl * 128 + 128),
                       tag=CX + tg + f"q{h}")
                for hh in range(2):
                    pt, kf = pspar[hh]
                    tt(MT_b[:, hh * 1024:hh * 1024 + 1024],
                       MT_b[:, hh * 1024:hh * 1024 + 1024], pt[:, 0:1024],
                       ALU.add, ["MT_b"] + kf(0, 1024), ["MT_b"],
                       tag=CX + tg + f"qe{hh}")
                yield

            yield from sq_pair("Cp0", "Cp1", Bp[1], "Bp1", t_b[1], "t1",
                               "p2")
            yield from leaf_prod("Bp1", "t1", "l2")
            yield from sq_pair("Bp1", "t1", Cp[0], "Cp0", Cp[1], "Cp1", "p4")
            yield from leaf_prod("Cp0", "Cp1", "l4")
            yield from sq_pair("Cp0", "Cp1", Bp[1], "Bp1", t_b[1], "t1", "p8")
            yield from leaf_prod("Bp1", "t1", "l8")

            for lvl, mask_b, maskk in ((16, mo16_b, "mo16_b"),
                                       (32, mo32_b, "mo32_b")):
                for h in range(H):
                    r = hp(h)
                    sl_ = slice(r * 128, r * 128 + 128)
                    tt(Cp[0][:, sl_], A_b[:, sl_], mask_b[:], ALU.mult,
                       ["A_b", maskk], ["Cp0"],
                       tag=CX + f"E{lvl}_{h}")
                for h in HORDER:
                    pt, kf = pspar[h % 2]
                    sl = h // 2
                    r = hp(h)
                    mm(pt[:, sl * 128:sl * 128 + 128],
                       Cp[0][:, r * 128:r * 128 + 128],
                       MT_b[:, r * 128:r * 128 + 128],
                       ["Cp0", "MT_b"], kf(sl * 128, sl * 128 + 128),
                       tag=CX + f"xt{lvl}_{h}")
                acopy(Cp[1][:, 0:1024], pspar[0][0][:, 0:1024],
                      pspar[0][1](0, 1024), ["Cp1"], tag=CX + f"xte{lvl}0")
                ts(Cp[1][:, 1024:2048], pspar[1][0][:, 0:1024], 1.0, ALU.mult,
                   None, ALU.bypass, pspar[1][1](0, 1024), ["Cp1"],
                   tag=CX + f"xte{lvl}1")
                yield
                for h in HORDER:
                    pt, kf = pspar[h % 2]
                    sl = h // 2
                    r = hp(h)
                    mm(pt[:, sl * 128:sl * 128 + 128],
                       M_b[:, r * 128:r * 128 + 128],
                       Cp[1][:, r * 128:r * 128 + 128],
                       ["M_b", "Cp1"], kf(sl * 128, sl * 128 + 128),
                       tag=CX + f"mt{lvl}_{h}")
                for hh in range(2):
                    pt, kf = pspar[hh]
                    tt(MT_b[:, hh * 1024:hh * 1024 + 1024],
                       MT_b[:, hh * 1024:hh * 1024 + 1024], pt[:, 0:1024],
                       ALU.subtract, ["MT_b"] + kf(0, 1024), ["MT_b"],
                       tag=CX + f"mte{lvl}{hh}")
                yield
                for h in HORDER:
                    pt, kf = pspar[h % 2]
                    sl = h // 2
                    r = hp(h)
                    mm(pt[:, sl * 128:sl * 128 + 128],
                       Cp[1][:, r * 128:r * 128 + 128],
                       M_b[:, r * 128:r * 128 + 128],
                       ["Cp1", "M_b"], kf(sl * 128, sl * 128 + 128),
                       tag=CX + f"m{lvl}_{h}")
                for hh in range(2):
                    pt, kf = pspar[hh]
                    tt(M_b[:, hh * 1024:hh * 1024 + 1024],
                       M_b[:, hh * 1024:hh * 1024 + 1024], pt[:, 0:1024],
                       ALU.subtract, ["M_b"] + kf(0, 1024), ["M_b"],
                       tag=CX + f"me{lvl}{hh}")
                yield

            for h in HORDER:
                pt, kf = pspar[h % 2]
                sl = h // 2
                r = hp(h)
                mm(pt[:, sl * 128:sl * 128 + 128],
                   MT_b[:, r * 128:r * 128 + 128],
                   t_b[0][:, r * 128:r * 128 + 128],
                   ["MT_b", "t0"], kf(sl * 128, sl * 128 + 128),
                   tag=CX + f"y_{h}")
            acopy(Bp[1][:, 0:1024], pspar[0][0][:, 0:1024],
                  pspar[0][1](0, 1024), ["Bp1"], tag=CX + "ye0")
            ts(Bp[1][:, 1024:2048], pspar[1][0][:, 0:1024], 1.0, ALU.mult,
               None, ALU.bypass, pspar[1][1](0, 1024), ["Bp1"], tag=CX + "ye1")
            yield
            for h in HORDER:
                pt, kf = pspar[h % 2]
                sl = h // 2
                r = hp(h)
                mm(pt[64:128, sl * 128:sl * 128 + 128],
                   Bp[0][0:64, r * 128 + 64:r * 128 + 128],
                   Bp[1][0:64, r * 128:r * 128 + 128],
                   ["Bp0", "Bp1"], kf(sl * 128, sl * 128 + 128),
                   tag=CX + f"z_{h}")
            acopy(Cp[0][64:128, 0:1024], pspar[0][0][64:128, 0:1024],
                  pspar[0][1](0, 1024), ["Cp0"], tag=CX + "ze0")
            ts(Cp[0][64:128, 1024:2048], pspar[1][0][64:128, 0:1024], 1.0,
               ALU.mult, None, ALU.bypass, pspar[1][1](0, 1024), ["Cp0"],
               tag=CX + "ze1")
            yield
            for h in HORDER:
                pt, kf = pspar[h % 2]
                sl = h // 2
                r = hp(h)
                mm(pt[64:128, sl * 128:sl * 128 + 128],
                   MT_b[64:128, r * 128 + 64:r * 128 + 128],
                   Cp[0][64:128, r * 128:r * 128 + 128],
                   ["MT_b", "Cp0"], kf(sl * 128, sl * 128 + 128),
                   tag=CX + f"c_{h}")
            acopy(t_b[1][:, 0:1024], Bp[1][:, 0:1024], ["Bp1"], ["t1"],
                  tag=CX + "ucp0")
            ts(t_b[1][:, 1024:2048], Bp[1][:, 1024:2048], 1.0, ALU.mult,
               None, ALU.bypass, ["Bp1"], ["t1"], tag=CX + "ucp1")
            for hh in range(2):
                pt, kf = pspar[hh]
                tt(t_b[1][64:128, hh * 1024:hh * 1024 + 1024],
                   t_b[1][64:128, hh * 1024:hh * 1024 + 1024],
                   pt[64:128, 0:1024], ALU.subtract,
                   ["t1"] + kf(0, 1024), ["t1"], tag=CX + f"usub{hh}")
            tfin = t_b[1]
            tkey = "t1"

            yield
            for h in range(H):
                par, sl = h % 2, h // 2
                r = hp(h)
                tpose(ps_tb_t[64 * par:64 * par + 64,
                              sl * 128:sl * 128 + 128],
                      tfin[:, r * 128:r * 128 + 64], identb[:],
                      [tkey, "identb"], ktb(sl * 128, sl * 128 + 128))
            acopy(wt_b[:], ps_tb_t[:, 0:1024], ktb(0, 1024), ["wt_b"],
                  tag=CX + "wtev")
            for h in range(H):
                par, sl = h % 2, h // 2
                r = hp(h)
                mm(ps_a_t[:, r * 64:r * 64 + 64],
                   wt_b[64 * par:64 * par + 64, sl * 128:sl * 128 + 128],
                   s_sb[64 * par:64 * par + 64, sl * 64:sl * 64 + 64],
                   ["wt_b", "s_sb"], ka(r * 64, r * 64 + 64),
                   start=(r in (0, 8)), tag=CX + f"ws{h}")
            tt(u_b[:].rearrange("p (r d) -> p r d", d=HD),
               tfin[:].rearrange("p (r two d) -> p r two d",
                                 two=2, d=HD)[:, :, 1, :],
               ps_a_t[:, 0:1024].rearrange("p (r d) -> p r d", d=HD),
               ALU.subtract, [tkey] + ka(0, 1024), ["u_b"], tag=CX + "uev")

            yield
            if full:
                for h in HORDER:
                    pt, kf = pspar[h % 2]
                    sl = h // 2
                    kh = head_slice(hat["k"], sp1, h, c0, c1)
                    qh = head_slice(hat["q"], sp1, h, c0, c1)
                    mm(pt[:, sl * 128:sl * 128 + 128], kh, qh,
                       [f"k_hat{sp1}", f"q_hat{sp1}"],
                       kf(sl * 128, sl * 128 + 128), tag=CX + f"kq{h}")
                for h in HORDER:
                    pt, kf = pspar[h % 2]
                    sl = h // 2
                    r = hp(h)
                    tt(Cp[1][:, r * 128:r * 128 + 128],
                       pt[:, sl * 128:sl * 128 + 128], triui_b[:], ALU.mult,
                       kf(sl * 128, sl * 128 + 128) + ["triui_b"], ["Cp1"],
                       tag=CX + f"mkq{h}")
                yield
                for h in range(H):
                    par, sl = h % 2, h // 2
                    r = hp(h)
                    qh = head_slice(hat["q"], sp1, h, c0, c1)
                    mm(ps_b_t[:, r * 64:r * 64 + 64], qh,
                       s_sb[64 * par:64 * par + 64, sl * 64:sl * 64 + 64],
                       [f"q_hat{sp1}", "s_sb"], kb(r * 64, r * 64 + 64),
                       start=(r in (0, 8)), stop=False, tag=CX + f"qs{h}")
                for h in range(H):
                    r = hp(h)
                    mm(ps_b_t[:, r * 64:r * 64 + 64],
                       Cp[1][:, r * 128:r * 128 + 128],
                       u_b[:, r * 64:r * 64 + 64],
                       ["Cp1", "u_b"], kb(r * 64, r * 64 + 64),
                       start=False, stop=True, tag=CX + f"oi{h}")

            yield
            for h in range(H):
                par, sl = h % 2, h // 2
                r = hp(h)
                mm(ps_a_t[64 * par:64 * par + 64, sl * 64:sl * 64 + 64],
                   kn_b[:, r * 64:r * 64 + 64], u_b[:, r * 64:r * 64 + 64],
                   ["kn_b", "u_b"], ka(0, 512), tag=CX + f"chain{h}")
            tt(s_f32[:], s_f32[:], ps_a_t[:, 0:512], ALU.add,
               ["s_f32"] + ka(0, 512), ["s_f32"], tag=CX + "sadd")
            acopy(s_sb[:], s_f32[:], ["s_f32"], ["s_sb"], tag=CX + "sev")
            yield

            if full:
                act(ln_tm2[:], ps_b_t[:, 0:1024], AF.Square,
                    kb(0, 1024), ["ln_tm2"])
                dve(lambda e, o=rms_b[:, 0:H],
                        i=ln_tm2[:].rearrange("p (r d) -> p r d", d=HD):
                        e.reduce_sum(o, i, axis=AX.X),
                    ["ln_tm2"], ["rms"], tag=CX + "ssq")
                act(rms_b[:, 16:16 + H], rms_b[:, 0:H], AF.Sqrt,
                    ["rms", "cpool"], ["rms"], scale=1.0 / HD,
                    bias=cpool[:, 1:2])
                dve(lambda e, o=rms_b[:, 0:H], i=rms_b[:, 16:16 + H]:
                        e.reciprocal(o, i), ["rms"], ["rms"])
                ts(rmsb16[:], rms_b[:, 0:H], 1.0, ALU.mult, None, ALU.bypass,
                   ["rms"], ["rmsb16"])
                tpose(ps_tb_t[0:H, 0:128], rmsb16[:], identb[:],
                      ["rmsb16", "identb"], ktb(0, 128))
                ts(rstdt_b[:], ps_tb_t[0:H, 0:128], 1.0, ALU.mult, None,
                   ALU.bypass, ktb(0, 128), ["rstdt_b"], tag=CX + "rstdt")
                yield
                for nh in range(2):
                    mm(ps_a_t[:, nh * 512:nh * 512 + 512], rstdt_b[:],
                       nwsel_b[:, nh * 512:nh * 512 + 512],
                       ["rstdt_b", "nwsel_b"], ka(nh * 512, nh * 512 + 512),
                       tag=CX + f"nwex{nh}")
                ts(nwex_b[:], ps_a_t[:, 0:1024], 1.0, ALU.mult, None,
                   ALU.bypass, ka(0, 1024), ["nwex_b"], tag=CX + "nwev")
                tt(onat_b[:], ps_b_t[:, 0:1024], nwex_b[:], ALU.mult,
                   kb(0, 1024) + ["nwex_b"], ["onat_b"], tag=CX + "onat")
                for h in range(H):
                    par, sl = h % 2, h // 2
                    r = hp(h)
                    tpose(ps_tb_t[64 * par:64 * par + 64,
                                  sl * 128:sl * 128 + 128],
                          onat_b[:, r * 64:r * 64 + 64], identb[:],
                          ["onat_b", "identb"], ktb(sl * 128, sl * 128 + 128))
                ts(ot_b[:, :, c0:c1],
                   ps_tb_t[:, 0:1024].rearrange("p (c t) -> p c t", t=128),
                   1.0, ALU.mult, None, ALU.bypass, ktb(0, 1024),
                   [f"ot_b{c}"], tag=CX + "otev")
                yield
            if full and osg >= 0 and do_wo:
                yield from tail_emit(c)

        if not full or osg < 0:
            return
        if not do_wo:
            dma("sp", out_t[:, osg * NCH:(osg + 1) * NCH, :], x_sb[:],
                [f"x_sb{a_}" for a_ in range(NCH)], [f"out:{osg}:0"],
                tag=SX + "outpass")
            return

        if ffn:
            scr = ln2t_scr.rearrange("p (c t) -> p c t", t=owntok)
            dma("sp", scr[:, :, osg * SEG:(osg + 1) * SEG], ln2t[:],
                ["ln2t"], [f"scr:{osg}"], tag=SX + "scrst")
        if s == nseg - 1:
            for nm, key, buf, dt in (
                                     ("ot", "ot_b3", ot_b, BF16),
                                     ("u", "u_b", u_b, BF16),
                                     ("ssb", "s_sb", s_sb, BF16),
                                     ("g", "g_sb", g_sb, F32),
                                     ("scores", "sc_tap", sc_tap, F32),
                                     ("ln2t", "ln2t", ln2t, BF16),
                                     ("stat", "stat", stat[:, 0:16], F32)):
                if nm in debug_taps:
                    dma("sp", tap(nm, buf.shape, dt), buf[:], [key],
                        ["tap_" + nm])

    FE_EVERY = 2
    _done = object()
    for _ in fe_emit(0):
        pass
    for s in range(nseg):
        dg = dt_emit(s)
        fg = fe_emit(s + 1) if s + 1 < nseg else iter(())
        nq = 0
        while True:
            d = next(dg, _done)
            nq += 1
            if nq % FE_EVERY == 0:
                next(fg, None)
            if d is _done:
                for _ in fg:
                    pass
                break

    esm.close()

    if ffn:
        p.barrier()
        esf = contextlib.ExitStack()
        def fsb(name, shape, dt):
            return sbuf_in(esf, name, shape, dt)
        w1_sb = fsb("w1_sb", (128, DC, FF), BF16)
        w2_sb = fsb("w2_sb", (128, DC, FF), BF16)
        w3_sb = fsb("w3_sb", (128, FC, D), BF16)
        l2_sb = [fsb(f"l2_sb{i}", (128, DC, TFF), BF16) for i in range(2)]
        hid_sb = fsb("hid_sb", (128, FC, TFF), BF16)
        sil_sc = [fsb(f"sil_sc{i}", (128, 512), BF16) for i in range(2)]
        og_sb = [fsb(f"og_sb{i}", (128, 2, D), F32) for i in range(2)]
        scr = ln2t_scr.rearrange("p (c t) -> p c t", t=owntok)

        NSL = 4
        for e_i in range(n_experts):
            EX = f"e{e_i}:"
            w1r = w1t[e_i].rearrange("(c p) f -> p c f", p=128)
            w2r = w2t[e_i].rearrange("(c p) f -> p c f", p=128)
            w3r = w3t[e_i].rearrange("(c p) f -> p c f", p=128)
            FSL = FF // NSL
            for sl_ in range(NSL):
                dma("pool", w1_sb[:, :, sl_ * FSL:(sl_ + 1) * FSL],
                    w1r[:, :, sl_ * FSL:(sl_ + 1) * FSL], [],
                    [f"w1_sb{sl_}"], tag=EX + f"w1_{sl_}")
                dma("pool", w2_sb[:, :, sl_ * FSL:(sl_ + 1) * FSL],
                    w2r[:, :, sl_ * FSL:(sl_ + 1) * FSL], [],
                    [f"w2_sb{sl_}"], tag=EX + f"w2_{sl_}")
            CSL = FC // NSL
            for sl_ in range(NSL):
                dma("pool", w3_sb[:, sl_ * CSL:(sl_ + 1) * CSL, :],
                    w3r[:, sl_ * CSL:(sl_ + 1) * CSL, :], [],
                    [f"w3_sb{sl_}"], tag=EX + f"w3_{sl_}")
            for t in range(nft):
                TX = EX + f"t{t}:"
                l2 = l2_sb[t % 2]
                l2k = f"l2_{t % 2}"
                dma("sp", l2[:], scr[:, :, t * TFF:(t + 1) * TFF],
                    [f"scr:{2 * t}", f"scr:{2 * t + 1}"], [l2k], tag=TX + "l2")
                for fc in range(FC):
                    for th in range(2):
                        ph1 = ps_a_t[:, th * 512:th * 512 + 512]
                        ph1k = ka(th * 512, th * 512 + 512)
                        ph2 = ps_b_t[:, th * 512:th * 512 + 512]
                        ph2k = kb(th * 512, th * 512 + 512)
                        wsl = fc // (FC // NSL)
                        for dc in range(DC):
                            mm(ph1, w1_sb[:, dc, fc * 128:fc * 128 + 128],
                               l2[:, dc, th * 512:th * 512 + 512],
                               [f"w1_sb{wsl}", l2k], ph1k,
                               start=(dc == 0), stop=(dc == DC - 1))
                        for dc in range(DC):
                            mm(ph2, w2_sb[:, dc, fc * 128:fc * 128 + 128],
                               l2[:, dc, th * 512:th * 512 + 512],
                               [f"w2_sb{wsl}", l2k], ph2k,
                               start=(dc == 0), stop=(dc == DC - 1))
                        sil = sil_sc[th]
                        act(sil[:], ph1, AF.Silu, ph1k, [f"sil{th}"],
                            tag=TX + f"sil{fc}_{th}")
                        tt(hid_sb[:, fc, th * 512:th * 512 + 512], sil[:],
                           ph2, ALU.mult, [f"sil{th}"] + ph2k, ["hid"],
                           tag=TX + f"hid{fc}_{th}")
                for tb in range(TFF // 128):
                    po = ps_a_t[:, 0:1024] if tb % 2 == 0 else ps_b_t[:, 0:1024]
                    pok = ka(0, 1024) if tb % 2 == 0 else kb(0, 1024)
                    for fc in range(FC):
                        for nh in range(2):
                            mm(po[:, nh * 512:nh * 512 + 512],
                               hid_sb[:, fc, tb * 128:tb * 128 + 128],
                               w3_sb[:, fc, nh * 512:nh * 512 + 512],
                               ["hid", f"w3_sb{fc // (FC // NSL)}"], pok,
                               start=(fc == 0), stop=(fc == FC - 1),
                               tag=TX + f"w3_{tb}_{fc}{nh}")
                    gx = t * (TFF // 128) + tb
                    ob = og_sb[(tb // 2) % 2]
                    obk = f"og{(tb // 2) % 2}"
                    ts(ob[:, tb % 2, :], po, g_sb[:, gx, e_i:e_i + 1],
                       ALU.mult, None, ALU.bypass, pok + ["g_sb"], [obk],
                       tag=TX + f"ogev{tb}")
                    if tb % 2 == 1:
                        blk = (gx - 1) // 2
                        ok1 = f"out:{(2 * blk) // NCH}:{(2 * blk) % NCH}"
                        ok2 = f"out:{(2 * blk + 1) // NCH}:{(2 * blk + 1) % NCH}"
                        dma_acc("pool", out_t[:, 2 * blk:2 * blk + 2, :],
                                ob[:], [obk, ok1, ok2], [ok1, ok2],
                                tag=TX + f"acc{tb}")
        esf.close()

    allk = [k for k in p.last_writer if k.startswith(("out:", "tap_"))]
    p.emit("sp", lambda e: e.nop(), reads=allk, tag="final")

    p.finalize()
    p.emit_block(es)
    es.close()
    return nc, taps



def consts():
    c = {}
    c["ident"] = np.eye(128, dtype=np.float32)
    c["ident64"] = np.vstack([np.eye(64), np.eye(64)]).astype(np.float32)
    c["trils"] = np.tril(np.ones((128, 128), np.float32), -1)
    md = np.zeros((128, 128), np.float32)
    for s0 in range(0, 128, 16):
        md[s0:s0 + 16, s0:s0 + 16] = np.tril(np.ones((16, 16)), -1)
    c["md16"] = md
    c["md16t"] = np.ascontiguousarray(md.T)
    mo16 = np.zeros((128, 128), np.float32)
    for s0 in range(0, 128, 32):
        mo16[s0 + 16:s0 + 32, s0:s0 + 16] = 1
    c["mo16"] = mo16
    mo32 = np.zeros((128, 128), np.float32)
    for s0 in range(0, 128, 64):
        mo32[s0 + 32:s0 + 64, s0:s0 + 32] = 1
    c["mo32"] = mo32
    c["triui"] = np.triu(np.ones((128, 128), np.float32), 0)
    sel = np.zeros((DC, 128, H), np.float32)
    for oc in range(DC):
        for pp_ in range(128):
            sel[oc, pp_, 2 * oc + pp_ // 64] = 1.0
    c["selnorm"] = sel
    c["selexp"] = np.ascontiguousarray(sel.transpose(0, 2, 1))
    return c


def stage_weights(Wq, bq, Wk, bk, Wv, bv, Wbeta, bbeta, Wo, bo, wq_conv,
                  bq_conv, wk_conv, bk_conv, wv_conv, bv_conv, norm_w,
                  Wg, W1, W2, W3):
    f32 = np.float32

    def wT(w):
        return np.ascontiguousarray(np.asarray(w, f32).T)

    def ccol(v):
        return np.ascontiguousarray(np.asarray(v, f32).reshape(DC, 128).T)

    def cconv(w):
        return np.ascontiguousarray(
            np.asarray(w, f32).reshape(DC, 128, KC).transpose(1, 0, 2))

    nw = np.asarray(norm_w, f32)
    nwsel = np.zeros((H, D), f32)
    for h in range(H):
        nwsel[h, h * HD:(h + 1) * HD] = nw
    return dict(
        wqt=wT(Wq), wkt=wT(Wk), wvt=wT(Wv), wot=wT(Wo),
        wbetat=np.ascontiguousarray(wT(Wbeta).reshape(DC, 128, H)),
        wgt=np.ascontiguousarray(wT(Wg).reshape(DC, 128, NE)),
        bq=ccol(bq), bk=ccol(bk), bv=ccol(bv),
        bo_row=np.ascontiguousarray(np.asarray(bo, f32).reshape(1, D)),
        bbeta=np.ascontiguousarray(np.asarray(bbeta, f32).reshape(H, 1)),
        wqconv=cconv(wq_conv), wkconv=cconv(wk_conv), wvconv=cconv(wv_conv),
        bqconv=ccol(bq_conv), bkconv=ccol(bk_conv), bvconv=ccol(bv_conv),
        w1t=np.ascontiguousarray(np.asarray(W1, f32).transpose(0, 2, 1)),
        w2t=np.ascontiguousarray(np.asarray(W2, f32).transpose(0, 2, 1)),
        w3t=np.ascontiguousarray(np.asarray(W3, f32).transpose(0, 2, 1)),
        nwsel=nwsel,
        **consts(),
    )


def kernel(L0_attn_x, L0_attn_Wq, L0_attn_bq, L0_attn_Wk, L0_attn_bk,
           L0_attn_Wv, L0_attn_bv, L0_attn_Wbeta, L0_attn_bbeta, L0_attn_Wo,
           L0_attn_bo, L0_attn_wq_conv, L0_attn_bq_conv, L0_attn_wk_conv,
           L0_attn_bk_conv, L0_attn_wv_conv, L0_attn_bv_conv, L0_attn_norm_w,
           L0_ffn_Wg, L0_ffn_W1, L0_ffn_W2, L0_ffn_W3):
    f32 = np.float32
    x = np.asarray(L0_attn_x, f32)
    nc, _ = build_program()
    common = stage_weights(
        L0_attn_Wq, L0_attn_bq, L0_attn_Wk, L0_attn_bk, L0_attn_Wv,
        L0_attn_bv, L0_attn_Wbeta, L0_attn_bbeta, L0_attn_Wo, L0_attn_bo,
        L0_attn_wq_conv, L0_attn_bq_conv, L0_attn_wk_conv, L0_attn_bk_conv,
        L0_attn_wv_conv, L0_attn_bv_conv, L0_attn_norm_w, L0_ffn_Wg,
        L0_ffn_W1, L0_ffn_W2, L0_ffn_W3)

    in_maps = []
    for core in range(8):
        b, h = core // 2, core % 2
        xb = x[b]
        if h == 0:
            xp = np.zeros((L, D), f32)
            xp[PRE * SEG:] = xb[:OWN * SEG]
            kvm = np.zeros((H, PRE * SEG), f32)
            hm = np.zeros((128, 1), f32)
        else:
            xp = np.ascontiguousarray(xb)
            kvm = np.ones((H, PRE * SEG), f32)
            hm = np.ones((128, 1), f32)
        m = dict(common)
        m["x"] = xp
        m["kvmask"] = kvm
        m["halomask"] = hm
        in_maps.append(m)

    res = run_bass_kernel_spmd(nc, in_maps, core_ids=list(range(8)))
    outp = np.empty((B, L, D), f32)
    for core in range(8):
        b, h = core // 2, core % 2
        o = res.results[core]["out"]
        if h == 0:
            outp[b, :OWN * SEG] = o
        else:
            outp[b, PRE * SEG:] = o
    return outp


# version 16
# speedup vs baseline: 1.0703x; 1.0703x over previous
import contextlib
import numpy as np
import ml_dtypes
import concourse.bass as bass
import concourse.mybir as mybir
from concourse.bass_utils import run_bass_kernel_spmd

F32 = mybir.dt.float32
BF16 = mybir.dt.bfloat16
AF = mybir.ActivationFunctionType
ALU = mybir.AluOpType
AX = mybir.AxisListType

B, L, D, H, HD = 4, 8192, 1024, 16, 64
KC, NE, FF = 4, 8, 2048
SEG, CH = 512, 128
NSEG = L // SEG
NCH = SEG // CH
PRE = 8
OWN = NSEG - PRE
LN_EPS = 1e-5
EPS = 1e-6
TFF = 1024
DC = D // 128
FC = FF // 128

ENGINES = ("pe", "act", "dve", "pool", "sp")


def hp(h):
    return (h % 2) * 8 + h // 2


HORDER = [0, 2, 4, 6, 8, 10, 12, 14, 1, 3, 5, 7, 9, 11, 13, 15]


class _Op:
    __slots__ = ("engine", "fn", "deps", "idx", "dma", "inc_sem", "inc_amt",
                 "inc_val", "waits", "prewaits", "tag")

    def __init__(self, engine, fn, deps, idx, dma, tag):
        self.engine, self.fn, self.deps, self.idx, self.dma, self.tag = (
            engine, fn, deps, idx, dma, tag)
        self.inc_sem = None
        self.inc_amt = 0
        self.inc_val = 0
        self.waits = []
        self.prewaits = []


class Prog:
    def __init__(self, nc):
        self.nc = nc
        self.ops = []
        self.last_writer = {}
        self.readers = {}
        self.barrier_idx = None

    def emit(self, engine, fn, reads=(), writes=(), dma=False, tag=""):
        ps_r = [k for k in reads if k.startswith("ps_")]
        if ps_r:
            reads = [k for k in reads if not k.startswith("ps_")]
            writes = list(writes) + ps_r
        idx = len(self.ops)
        deps = set()
        if self.barrier_idx is not None:
            deps.add(self.barrier_idx)
        for k in reads:
            w = self.last_writer.get(k)
            if w is not None:
                deps.add(w)
        for k in writes:
            w = self.last_writer.get(k)
            if w is not None:
                deps.add(w)
            for r in self.readers.get(k, ()):
                deps.add(r)
        deps.discard(idx)
        for k in reads:
            self.readers.setdefault(k, []).append(idx)
        for k in writes:
            self.last_writer[k] = idx
            self.readers[k] = []
        self.ops.append(_Op(engine, fn, deps, idx, dma, tag))
        return idx

    def barrier(self, engine="sp"):
        keys = list(self.last_writer.keys())
        idx = self.emit(engine, lambda e: e.nop(), reads=keys, tag="barrier")
        self.barrier_idx = idx
        return idx

    DMA_SEM_ROT = 8

    def finalize(self):
        ops = self.ops
        for o in ops:
            pruned = set()
            for d in o.deps:
                pp = ops[d]
                if (pp.engine == "pe" and o.engine == "pe"
                        and not pp.dma and not o.dma):
                    continue
                pruned.add(d)
            best = {}
            keep = set()
            for d in pruned:
                pp = ops[d]
                if pp.dma:
                    keep.add(d)
                else:
                    key = pp.engine
                    if key not in best or d > best[key]:
                        best[key] = d
            o.deps = keep | set(best.values())
        needed = set()
        for o in ops:
            needed.update(o.deps)
        cnt = {}
        dma_idx = {}
        R = self.DMA_SEM_ROT
        sem_names = set()
        for o in ops:
            if o.dma:
                i = dma_idx.get(o.engine, 0)
                dma_idx[o.engine] = i + 1
                sem = f"dq_{o.engine}_{i % R}"
                sem_names.add(sem)
                o.inc_sem, o.inc_amt = sem, 16
                o.inc_val = 16 * (i // R + 1)
                if i >= R:
                    o.prewaits = [(sem, 16 * (i // R))]
            elif o.idx in needed:
                sem = "eng_" + o.engine
                sem_names.add(sem)
                cnt[sem] = cnt.get(sem, 0) + 1
                o.inc_sem, o.inc_amt, o.inc_val = sem, 1, cnt[sem]
        self.sem_names = sorted(sem_names)
        known = {e: {} for e in ENGINES}
        by_engine = {e: [] for e in ENGINES}
        for o in ops:
            by_engine[o.engine].append(o)
            k = known[o.engine]
            for d in sorted(o.deps):
                pp = ops[d]
                if pp.inc_sem is None:
                    continue
                if k.get(pp.inc_sem, 0) >= pp.inc_val:
                    continue
                o.waits.append((pp.inc_sem, pp.inc_val))
                k[pp.inc_sem] = pp.inc_val
        finals = {}
        for o in ops:
            if o.dma:
                finals[o.inc_sem] = max(finals.get(o.inc_sem, 0), o.inc_val)
        fin = by_engine["sp"][-1]
        k = known["sp"]
        for sem in sorted(finals):
            if k.get(sem, 0) < finals[sem]:
                fin.waits.append((sem, finals[sem]))
                k[sem] = finals[sem]
        self.by_engine = by_engine

    def emit_block(self, es):
        sems = {name: es.enter_context(self.nc.semaphore(name))
                for name in self.sem_names}
        block = es.enter_context(self.nc.Block())
        prog = self

        def run(eng_name, eng):
            for o in prog.by_engine[eng_name]:
                for sem, val in o.prewaits:
                    eng.wait_ge(sems[sem], val)
                for sem, val in o.waits:
                    eng.wait_ge(sems[sem], val)
                instr = o.fn(eng)
                if o.inc_sem is not None:
                    assert instr is not None, f"op {o.tag} returned no instr"
                    instr.then_inc(sems[o.inc_sem], o.inc_amt)

        @block.tensor
        def _(t):
            run("pe", t)

        @block.scalar
        def _(s):
            run("act", s)

        @block.vector
        def _(v):
            run("dve", v)

        @block.gpsimd
        def _(g):
            run("pool", g)

        @block.sync
        def _(sy):
            run("sp", sy)


def head_slice(buf, sp_, h, c0, c1):
    par, oc = h % 2, h // 2
    return buf[64 * par:64 * par + 64, sp_, oc, c0:c1]


def build_program(pre=PRE, own=OWN, ffn=True, n_experts=NE, debug_taps=(),
                  do_delta=True, do_wo=True, fe_parts=("ln", "beta", "k", "v", "q"),
                  conv_on=True, norm_on=True, conv_ocs=DC,
                  cv_omit=(), full_prefix=False):
    nseg = pre + own
    ntok = nseg * SEG
    owntok = own * SEG
    nft = owntok // TFF if ffn else 0
    nc = bass.Bass("TRN2", target_bir_lowering=False)
    p = Prog(nc)

    def din(name, shape, dt=F32):
        return nc.dram_tensor(name, list(shape), dt, kind="ExternalInput").ap()

    x_in = din("x", (ntok, D))
    kvmask = din("kvmask", (H, pre * SEG)) if pre else None
    halomask = din("halomask", (128, 1))
    wqt = din("wqt", (D, D)); wkt = din("wkt", (D, D)); wvt = din("wvt", (D, D))
    wot = din("wot", (D, D))
    wbetat = din("wbetat", (DC, 128, H))
    wgt = din("wgt", (DC, 128, NE))
    b_in = {n: din(n, (128, DC)) for n in ("bq", "bk", "bv")}
    bo_row = din("bo_row", (1, D))
    bbeta_in = din("bbeta", (H, 1))
    wconv_in = {pr: din(f"w{pr}conv", (128, DC, KC)) for pr in "qkv"}
    bconv_in = {pr: din(f"b{pr}conv", (128, DC)) for pr in "qkv"}
    w1t = din("w1t", (n_experts, D, FF)); w2t = din("w2t", (n_experts, D, FF))
    w3t = din("w3t", (n_experts, FF, D))
    ident_in = din("ident", (128, 128))
    ident64_in = din("ident64", (128, 64))
    trils_in = din("trils", (128, 128))
    md16_in = din("md16", (128, 128))
    md16t_in = din("md16t", (128, 128))
    mo16_in = din("mo16", (128, 128))
    mo32_in = din("mo32", (128, 128))
    triui_in = din("triui", (128, 128))
    selnorm_in = din("selnorm", (DC, 128, H))
    selexp_in = din("selexp", (DC, H, 128))
    nwsel_in = din("nwsel", (H, D))

    out = nc.dram_tensor("out", [owntok, D], F32, kind="ExternalOutput").ap()
    out_t = out.rearrange("(a p) d -> p a d", p=128)
    ln2t_scr = (nc.dram_tensor("ln2t_scr", [128, DC * owntok], BF16).ap()
                if ffn else None)
    wscr_t = nc.dram_tensor("wscr", [3 * DC, 128, DC, 128], BF16).ap()
    PRI = {"q": 0, "k": 1, "v": 2}
    wcast_done = set()

    taps = {}
    def tap(name, shape, dt=F32):
        ap = nc.dram_tensor("tap_" + name, list(shape), dt,
                            kind="ExternalOutput").ap()
        taps[name] = ap
        return ap

    es = contextlib.ExitStack()
    def sbuf_in(stack, name, shape, dt):
        return stack.enter_context(nc.sbuf_tensor(name, list(shape), dt))

    def sbuf(name, shape, dt):
        return sbuf_in(es, name, shape, dt)

    g_sb = sbuf("g_sb", (128, max(owntok // 128, 1), NE), F32)
    ones1 = sbuf("ones1", (1, 128), BF16)

    ps_f_t = es.enter_context(nc.psum_tensor("ps_f", [128, 512], F32))
    ps_a_t = es.enter_context(nc.psum_tensor("ps_a", [128, 1024], F32))
    ps_b_t = es.enter_context(nc.psum_tensor("ps_b", [128, 1024], F32))
    ps_e_t = es.enter_context(nc.psum_tensor("ps_e", [128, 512], F32))
    ps_tb_t = es.enter_context(nc.psum_tensor("ps_tb", [128, 2048], BF16))

    def pkeys(name, lo_el, hi_el, el_per_bank):
        b0, b1 = lo_el // el_per_bank, (hi_el + el_per_bank - 1) // el_per_bank
        return [f"{name}:{i}" for i in range(b0, b1)]

    def ka(lo, hi):  return pkeys("ps_a", lo, hi, 512)
    def kb(lo, hi):  return pkeys("ps_b", lo, hi, 512)
    def ktb(lo, hi): return pkeys("ps_tb", lo, hi, 1024)
    KF_, KE_ = ["ps_f:0"], ["ps_e:0"]

    seq = [0]
    def utag(pfx=""):
        seq[0] += 1
        return f"{pfx}{seq[0]}"

    def dma(queue, dst, src, reads, writes, tag=""):
        p.emit(queue, lambda e, d=dst, s=src: e.dma_start(d, s),
               reads=reads, writes=writes, dma=True, tag=tag or utag("dma"))

    def dma_acc(queue, dst, src, reads, writes, tag=""):
        p.emit(queue,
               lambda e, d=dst, s=src: e.dma_start(d, s, accum_op=ALU.add),
               reads=reads, writes=writes, dma=True, tag=tag or utag("dmac"))

    def mm(out_ap, lhsT, rhs, reads, writes, start=True, stop=True, tag=""):
        p.emit("pe",
               lambda e, o=out_ap, l=lhsT, r=rhs, st=start, sp=stop:
                   e.matmul(o, l, r, start=st, stop=sp, skip_group_check=True),
               reads=reads, writes=writes, tag=tag or utag("mm"))

    def tpose(out_ap, in_ap, idt, reads, writes, tag=""):
        p.emit("pe", lambda e, o=out_ap, i=in_ap, d=idt: e.transpose(o, i, d),
               reads=reads, writes=writes, tag=tag or utag("tp"))

    def act(out_ap, in_ap, func, reads, writes, bias=..., scale=...,
            accum_out=None, tag=""):
        kw = {}
        if bias is not ...:
            kw["bias"] = bias
        if scale is not ...:
            kw["scale"] = scale
        if accum_out is not None:
            kw["accum_out"] = accum_out
        p.emit("act",
               lambda e, o=out_ap, i=in_ap, f=func, kw=kw:
                   e.activation(o, i, f, **kw),
               reads=reads, writes=writes, tag=tag or utag("act"))

    def ts(out_ap, in_ap, s1, op0, s2, op1, reads, writes, tag="", eng="dve"):
        p.emit(eng,
               lambda e, o=out_ap, i=in_ap, a=s1, b=s2, o0=op0, o1=op1:
                   e.tensor_scalar(o, i, a, b, o0, o1),
               reads=reads, writes=writes, tag=tag or utag("ts"))

    def tt(out_ap, in0, in1, op, reads, writes, tag="", eng="dve"):
        p.emit(eng,
               lambda e, o=out_ap, a=in0, b=in1, oo=op:
                   e.tensor_tensor(o, a, b, oo),
               reads=reads, writes=writes, tag=tag or utag("tt"))

    def acopy(out_ap, in_ap, reads, writes, scale=None, tag=""):
        if scale is None:
            p.emit("act", lambda e, o=out_ap, i=in_ap: e.copy(o, i),
                   reads=reads, writes=writes, tag=tag or utag("ac"))
        else:
            p.emit("act",
                   lambda e, o=out_ap, i=in_ap, sc=scale:
                       e.activation(o, i, AF.Copy, scale=sc),
                   reads=reads, writes=writes, tag=tag or utag("ac"))

    def stt(out_ap, in0, scalar, in1, op0, op1, reads, writes, tag=""):
        p.emit("dve",
               lambda e, o=out_ap, a=in0, s=scalar, b=in1, o0=op0, o1=op1:
                   e.scalar_tensor_tensor(o, a, s, b, o0, o1),
               reads=reads, writes=writes, tag=tag or utag("stt"))

    def dve(fn, reads, writes, tag=""):
        p.emit("dve", fn, reads=reads, writes=writes, tag=tag or utag("dve"))

    esm = contextlib.ExitStack()
    def msb(name, shape, dt):
        return sbuf_in(esm, name, shape, dt)

    identb = msb("identb", (128, 128), BF16)
    identf16 = msb("identf16", (16, 16), F32)
    ident64b = msb("ident64b", (128, 64), BF16)
    trils_b = msb("trils_b", (128, 128), BF16)
    md16_b = msb("md16_b", (128, 128), BF16)
    md16t_b = msb("md16t_b", (128, 128), BF16)
    mo16_b = msb("mo16_b", (128, 128), BF16)
    mo32_b = msb("mo32_b", (128, 128), BF16)
    triui_b = msb("triui_b", (128, 128), BF16)
    selnorm_b = msb("selnorm_b", (128, DC, H), BF16)
    selexp_b = msb("selexp_b", (H, DC, 128), BF16)
    nwsel_b = msb("nwsel_b", (H, D), BF16)
    wstr = {pr: msb(f"w{pr}str", (128, 2, DC, 128), BF16) for pr in "qkv"}
    wo_sb = msb("wo_sb", (128, DC, D), BF16)
    wbeta_sb = msb("wbeta_sb", (128, DC, H), BF16)
    wg_sb = msb("wg_sb", (128, DC, NE), BF16)
    bias_sb = {n: msb(n + "_sb", (128, DC), F32) for n in ("bq", "bk", "bv")}
    bconv_sb = {pr: msb(f"b{pr}c_sb", (128, DC), F32) for pr in "qkv"}
    wconv_sb = {pr: msb(f"w{pr}c_sb", (128, DC, KC), F32) for pr in "qkv"}
    diag_sb = [msb(f"diag{i}", (128, KC, 128), BF16) for i in range(2)]
    bo_sb = msb("bo_sb", (1, D), BF16)
    bbeta_sb = msb("bbeta_sb", (H, 1), F32)
    hmask_sb = msb("hmask_sb", (128, 1), F32)
    kvmask_sb = msb("kvmask_sb", (H, SEG), F32) if pre else None
    cpool = msb("cpool", (128, 2), F32)

    x_sb = msb("x_sb", (128, NCH, D), F32)
    stat = msb("stat", (128, 32), F32)
    ln_tm = msb("ln_tm", (128, D), BF16)
    ln_tm2 = msb("ln_tm2", (128, D), BF16)
    x_res = msb("x_res", (128, 2, D), F32)
    ln1t = msb("ln1t", (128, DC, SEG), BF16)
    pre_sc = [msb(f"pre_sc{i}", (128, 3 + SEG), BF16) for i in range(2)]
    tail_sb = {pr: msb(f"tail_{pr}", (128, DC, 3), BF16) for pr in "qkv"}
    sq_sc = [msb(f"sq_sc{i}", (128, SEG), BF16) for i in range(2)]
    hat = {pr: msb(pr + "_hat", (128, 2, DC, SEG), BF16) for pr in "qkv"}
    rn_sc = msb("rn_sc", (H, SEG), F32)
    rnorm = msb("rnorm", (H, 2 * SEG), BF16)
    beta_fm = msb("beta_fm", (H, SEG), F32)
    betat = msb("betat", (128, 2, NCH, H), F32)

    A_b = msb("A_b", (128, H * 128), BF16)
    Bp = [msb(f"Bp{j}", (128, H * 128), BF16) for j in range(2)]
    Cp = [msb(f"Cp{j}", (128, H * 128), BF16) for j in range(2)]
    t_b = [msb(f"t_b{j}", (128, H * 128), BF16) for j in range(2)]
    M_b = msb("M_b", (128, H * 128), BF16)
    MT_b = msb("MT_b", (128, H * 128), BF16)
    kn_b = msb("kn_b", (128, H * HD), BF16)
    wt_b = msb("wt_b", (128, 8 * 128), BF16)
    u_b = msb("u_b", (128, H * HD), BF16)
    s_sb = msb("s_sb", (128, 8 * HD), BF16)
    s_f32 = msb("s_f32", (128, 8 * HD), F32)
    rms_b = msb("rms_b", (128, 32), F32)
    rstdt_b = msb("rstdt_b", (H, 128), BF16)
    rmsb16 = msb("rmsb16", (128, H), BF16)
    nwex_b = msb("nwex_b", (128, H * HD), BF16)
    onat_b = msb("onat_b", (128, H * HD), BF16)
    ot_b = msb("ot_b", (128, DC, SEG), BF16)
    x2_b = msb("x2_b", (128, D), F32)
    ln2t = msb("ln2t", (128, DC, SEG), BF16)
    gt_sc = msb("gt_sc", (128, 32), F32)
    sc_tap = msb("sc_tap", (128, NCH * NE), F32) if "scores" in debug_taps else None

    dma("sp", identf16[:], ident_in[0:16, 0:16], [], ["identf16"])
    dma("pool", identb[:], ident_in, [], ["identb"])
    dma("pool", ident64b[:], ident64_in, [], ["ident64b"])
    dma("pool", trils_b[:], trils_in, [], ["trils_b"])
    dma("pool", md16_b[:], md16_in, [], ["md16_b"])
    dma("pool", md16t_b[:], md16t_in, [], ["md16t_b"])
    dma("pool", mo16_b[:], mo16_in, [], ["mo16_b"])
    dma("pool", mo32_b[:], mo32_in, [], ["mo32_b"])
    dma("pool", triui_b[:], triui_in, [], ["triui_b"])
    dma("pool", selnorm_b[:], selnorm_in.rearrange("c p h -> p c h"), [],
        ["selnorm_b"])
    dma("pool", selexp_b[:], selexp_in.rearrange("c h p -> h c p"), [],
        ["selexp_b"])
    dma("pool", nwsel_b[:], nwsel_in, [], ["nwsel_b"])
    dve(lambda e: e.memset(ones1[:], 1.0), [], ["ones1"])
    dve(lambda e: e.memset(s_sb[:], 0.0), [], ["s_sb"])
    dve(lambda e: e.memset(s_f32[:], 0.0), [], ["s_f32"])
    dma("pool", wo_sb[:], wot.rearrange("(c p) o -> p c o", p=128), [],
        ["wo_sb"])
    wsrc = {"q": wqt.rearrange("(c p) o -> p c o", p=128),
            "k": wkt.rearrange("(c p) o -> p c o", p=128),
            "v": wvt.rearrange("(c p) o -> p c o", p=128)}
    dma("pool", wbeta_sb[:], wbetat.rearrange("c p h -> p c h"), [], ["wbeta_sb"])
    dma("pool", wg_sb[:], wgt.rearrange("c p h -> p c h"), [], ["wg_sb"])
    for nm in ("bq", "bk", "bv"):
        dma("sp", bias_sb[nm][:], b_in[nm], [], [nm + "_sb"])
    for pr in "qkv":
        dma("sp", bconv_sb[pr][:], bconv_in[pr], [], [f"b{pr}c_sb"])
        dma("sp", wconv_sb[pr][:], wconv_in[pr], [], [f"w{pr}c_sb"])
    dma("pool", bo_sb[:], bo_row, [], ["bo_sb"])
    dma("sp", bbeta_sb[:], bbeta_in, [], ["bbeta_sb"])
    dma("sp", hmask_sb[:], halomask, [], ["hmask_sb"])
    dve(lambda e: e.memset(cpool[:, 0:1], LN_EPS), [], ["cpool"])
    dve(lambda e: e.memset(cpool[:, 1:2], EPS), [], ["cpool"])
    for pr in "qkv":
        dve(lambda e, t=tail_sb[pr]: e.memset(t[:], 0.0), [], [f"tail_{pr}"])

    BMAP = {"q": "bq", "k": "bk", "v": "bv"}
    pspar = {0: (ps_a_t, ka), 1: (ps_b_t, kb)}

    def fe_emit(s):
        full = (s >= pre) or full_prefix
        sp1 = s % 2
        SX = f"s{s}f:"

        def wload(pr, oc):
            par = oc % 2
            if (pr, oc) in wcast_done:
                dma("act", wstr[pr][:, par], wscr_t[PRI[pr] * DC + oc],
                    [f"wscr{pr}{oc}"], [f"w{pr}s{par}"],
                    tag=SX + f"wl{pr}{oc}")
            else:
                wcast_done.add((pr, oc))
                dma("pool", wstr[pr][:, par],
                    wsrc[pr][:, :, oc * 128:oc * 128 + 128], [],
                    [f"w{pr}s{par}"], tag=SX + f"wl{pr}{oc}")
                dma("act", wscr_t[PRI[pr] * DC + oc], wstr[pr][:, par],
                    [f"w{pr}s{par}"], [f"wscr{pr}{oc}"],
                    tag=SX + f"wst{pr}{oc}")

        x_hbm = x_in.rearrange("(a p) d -> p a d", p=128)[:, NCH * s:NCH * (s + 1), :]
        dma("sp", x_sb[:], x_hbm, [], ["x_sb"], tag=SX + "xload")
        if s < pre:
            dma("sp", kvmask_sb[:], kvmask[:, s * SEG:(s + 1) * SEG],
                [], ["kvmask_sb"], tag=SX + "kvm")
        prs0 = [p_ for p_ in ("k", "v") if p_ in fe_parts]
        if "q" in fe_parts and (full or s == pre - 1):
            prs0.append("q")
        for p_ in prs0:
            wload(p_, 0)
            wload(p_, 1)

        for a in range(NCH):
            xt = x_sb[:, a, :]
            dve(lambda e, o=stat[:, 0:6], i=xt[:, 0:512]: e.bn_stats(o, i),
                ["x_sb"], ["stat"])
            dve(lambda e, o=stat[:, 6:12], i=xt[:, 512:1024]: e.bn_stats(o, i),
                ["x_sb"], ["stat"])
            dve(lambda e, o=stat[:, 12:14], i=stat[:, 0:12]: e.bn_aggr(o, i),
                ["stat"], ["stat"])
            act(stat[:, 14:15], stat[:, 13:14], AF.Sqrt,
                ["stat", "cpool"], ["stat"], bias=cpool[:, 0:1])
            dve(lambda e, o=stat[:, 15:16], i=stat[:, 14:15]:
                    e.reciprocal(o, i), ["stat"], ["stat"])
            ts(ln_tm[:], xt, stat[:, 12:13], ALU.subtract, stat[:, 15:16],
               ALU.mult, ["x_sb", "stat"], ["ln_tm"], tag=SX + f"ln{a}")
            for dc in range(DC):
                tpose(ps_tb_t[:, dc * 128:dc * 128 + 128],
                      ln_tm[:, dc * 128:dc * 128 + 128], identb[:],
                      ["ln_tm", "identb"], ktb(dc * 128, dc * 128 + 128))
            ts(ln1t[:, :, a * 128:a * 128 + 128],
               ps_tb_t[:, 0:1024].rearrange("p (c t) -> p c t", t=128),
               1.0, ALU.mult, None, ALU.bypass,
               ktb(0, 1024), ["ln1t"], tag=SX + f"lnT{a}")
            yield

        for dc in (range(DC) if "beta" in fe_parts else ()):
            mm(ps_e_t[0:H, 0:SEG], wbeta_sb[:, dc, :], ln1t[:, dc, :],
               ["wbeta_sb", "ln1t"], KE_, start=(dc == 0), stop=(dc == DC - 1))
        if "beta" not in fe_parts:
            pass
        else:
          act(beta_fm[:], ps_e_t[0:H, 0:SEG], AF.Sigmoid,
            KE_ + ["bbeta_sb"], ["beta_fm"], bias=bbeta_sb[:])
          for c in range(NCH):
            tpose(ps_f_t[:, c * 128:c * 128 + 16],
                  beta_fm[:, c * 128:c * 128 + 128], identf16[:],
                  ["beta_fm", "identf16"], KF_)
          ts(betat[:, sp1],
             ps_f_t[:, 0:512].rearrange("p (c h) -> p c h", h=128)[:, :, 0:H],
             1.0, ALU.mult, None, ALU.bypass, KF_, [f"betat{sp1}"],
             tag=SX + "betat")
          yield

        if pre and s == pre:
            for pr in "qkv":
                ts(tail_sb[pr][:].rearrange("p c t -> p (c t)"),
                   tail_sb[pr][:].rearrange("p c t -> p (c t)"),
                   hmask_sb[:], ALU.mult, None, ALU.bypass,
                   [f"tail_{pr}", "hmask_sb"], [f"tail_{pr}"], tag=SX + pr + "hm")

        FE_B = {0: (ps_f_t, KF_), 1: (ps_e_t, KE_)}

        def proj_emit(pr, oc):
            par = oc % 2
            wkey = f"w{pr}s{par}"
            bkey = BMAP[pr]
            pb_, ppk = FE_B[par]
            pp = pb_[:, 0:512]
            for dc in range(DC):
                mm(pp, wstr[pr][:, par, dc, :],
                   ln1t[:, dc, :], [wkey, "ln1t"], ppk,
                   start=(dc == 0), stop=(dc == DC - 1))
            if oc + 2 < DC:
                wload(pr, oc + 2)
            sc = pre_sc[par]
            sck = f"pre_sc{par}"
            if not conv_on or oc >= conv_ocs:
                ts(hat[pr][:, sp1, oc, :], pp,
                   bias_sb[bkey][:, oc:oc + 1],
                   ALU.add, None, ALU.bypass, ppk + [bkey + "_sb"],
                   [f"{pr}_hat{sp1}"], tag=SX + f"{pr}nc{oc}")
                return
            if "halo" not in cv_omit:
                ts(sc[:, 0:3], tail_sb[pr][:, oc, :], 1.0, ALU.mult, None,
                   ALU.bypass, [f"tail_{pr}"], [sck])
            if "evac" not in cv_omit:
                ts(sc[:, 3:3 + SEG], pp, bias_sb[bkey][:, oc:oc + 1],
                   ALU.add, None, ALU.bypass, ppk + [bkey + "_sb"], [sck],
                   tag=SX + f"{pr}pe{oc}")
            if "diag" not in cv_omit:
                dg = diag_sb[par]
                dgk = f"diag{par}"
                for k in range(KC):
                    ts(dg[:, k, :], identb[:], wconv_sb[pr][:, oc, k:k + 1],
                       ALU.mult, None, ALU.bypass,
                       ["identb", f"w{pr}c_sb"], [dgk])

        def conv_emit(pr, oc):
            if not conv_on or oc >= conv_ocs:
                return
            par = oc % 2
            sc = pre_sc[par]
            sck = f"pre_sc{par}"
            dg = diag_sb[par]
            dgk = f"diag{par}"
            pb_, cpk = FE_B[par]
            cp = pb_[:, 0:512]
            if "mm" not in cv_omit:
                for k in range(KC):
                    mm(cp, dg[:, k, :], sc[:, k:k + SEG],
                       [dgk, sck], cpk, start=(k == 0),
                       stop=(k == KC - 1))
            if "tail" not in cv_omit:
                ts(tail_sb[pr][:, oc, :], sc[:, SEG:SEG + 3], 1.0,
                   ALU.mult, None, ALU.bypass, [sck], [f"tail_{pr}"])
            if pr == "v":
                act(hat["v"][:, sp1, oc, :], cp, AF.Silu,
                    cpk + ["bvc_sb"], [f"v_hat{sp1}"],
                    bias=bconv_sb["v"][:, oc:oc + 1])
            else:
                sq = sq_sc[par]
                sqk = f"sq_sc{par}"
                if "sq" not in cv_omit:
                    act(sq[:], cp, AF.Square, cpk + [f"b{pr}c_sb"], [sqk],
                        bias=bconv_sb[pr][:, oc:oc + 1])
                if "raw" not in cv_omit:
                    ts(hat[pr][:, sp1, oc, :], cp,
                       bconv_sb[pr][:, oc:oc + 1],
                       ALU.add, None, ALU.bypass, cpk + [f"b{pr}c_sb"],
                       [f"{pr}_hat{sp1}"])

        def nm_emit(pr, oc):
            if (not conv_on or oc >= conv_ocs or pr == "v" or not norm_on
                    or "sq" in cv_omit):
                return
            par = oc % 2
            sq = sq_sc[par]
            sqk = f"sq_sc{par}"
            pb_, npk = FE_B[par]
            mm(pb_[0:H, 0:SEG], selnorm_b[:, oc, :], sq[:],
               ["selnorm_b", sqk], npk, tag=SX + f"nm{pr}{oc}")
            if oc == 0:
                acopy(rn_sc[:], pb_[0:H, 0:SEG], npk, ["rn_sc"],
                      tag=SX + f"na{pr}{oc}")
            else:
                tt(rn_sc[:], rn_sc[:], pb_[0:H, 0:SEG], ALU.add,
                   ["rn_sc"] + npk, ["rn_sc"], tag=SX + f"na{pr}{oc}")

        def proj_conv(pr):
            for ocp in range(0, DC, 2):
                proj_emit(pr, ocp)
                yield
                proj_emit(pr, ocp + 1)
                yield
                conv_emit(pr, ocp)
                nm_emit(pr, ocp)
                yield
                conv_emit(pr, ocp + 1)
                nm_emit(pr, ocp + 1)
                yield

        def finish_norm(pr, off):
            act(rn_sc[:], rn_sc[:], AF.Sqrt, ["rn_sc"], ["rn_sc"])
            ts(rn_sc[:], rn_sc[:], EPS, ALU.add,
               None, ALU.bypass, ["rn_sc"], ["rn_sc"])
            dve(lambda e, o=rn_sc[:], i=rn_sc[:]:
                    e.reciprocal(o, i), ["rn_sc"], ["rn_sc"])
            ts(rnorm[:, off:off + SEG], rn_sc[:], 1.0, ALU.mult,
               None, ALU.bypass, ["rn_sc"], ["rnorm"])
            if pr == "k" and s < pre:
                tt(rnorm[:, off:off + SEG], rnorm[:, off:off + SEG],
                   kvmask_sb[:], ALU.mult,
                   ["rnorm", "kvmask_sb"], ["rnorm"], tag=SX + "kmask")

        if "k" in fe_parts:
            yield from proj_conv("k")
            if conv_on and norm_on:
                finish_norm("k", 0)
        if "v" in fe_parts:
            yield from proj_conv("v")
        for oc in (range(DC) if "v" in fe_parts else ()):
            pe_, pek = FE_B[oc % 2]
            pe_ = pe_[:, 0:512]
            mm(pe_, selexp_b[:, oc, :], rnorm[0:H, 0:SEG],
               ["selexp_b", "rnorm"], pek)
            tt(hat["k"][:, sp1, oc, :], hat["k"][:, sp1, oc, :], pe_,
               ALU.mult, [f"k_hat{sp1}"] + pek, [f"k_hat{sp1}"],
               tag=SX + f"kap{oc}")
            tt(hat["v"][:, sp1, oc, :], hat["v"][:, sp1, oc, :], pe_,
               ALU.mult, [f"v_hat{sp1}"] + pek, [f"v_hat{sp1}"],
               tag=SX + f"vap{oc}")
            if oc % 2 == 1:
                yield
        if full and "q" in fe_parts:
            yield from proj_conv("q")
            finish_norm("q", SEG)
            for oc in range(DC):
                pe_, pek = FE_B[oc % 2]
                pe_ = pe_[:, 0:512]
                mm(pe_, selexp_b[:, oc, :], rnorm[0:H, SEG:2 * SEG],
                   ["selexp_b", "rnorm"], pek)
                tt(hat["q"][:, sp1, oc, :], hat["q"][:, sp1, oc, :], pe_,
                   ALU.mult, [f"q_hat{sp1}"] + pek, [f"q_hat{sp1}"],
                   tag=SX + f"qap{oc}")
                if oc % 2 == 1:
                    yield
        elif s == pre - 1 and "q" in fe_parts:
            for oc in range(DC):
                par = oc % 2
                for dc in range(DC):
                    mm(ps_f_t[:, 0:3], wstr["q"][:, par, dc, :],
                       ln1t[:, dc, SEG - 3:SEG], [f"wqs{par}", "ln1t"], KF_,
                       start=(dc == 0), stop=(dc == DC - 1))
                if oc + 2 < DC:
                    wload("q", oc + 2)
                ts(tail_sb["q"][:, oc, :], ps_f_t[:, 0:3],
                   bias_sb["bq"][:, oc:oc + 1], ALU.add, None, ALU.bypass,
                   KF_ + ["bq_sb"], ["tail_q"], tag=SX + f"qtail{oc}")
                yield

        if s == nseg - 1:
            for nm, key, buf, dt in (("ln1t", "ln1t", ln1t, BF16),
                                     ("khat", f"k_hat{sp1}", hat["k"][:, sp1],
                                      BF16),
                                     ("vhat", f"v_hat{sp1}", hat["v"][:, sp1],
                                      BF16),
                                     ("qhat", f"q_hat{sp1}", hat["q"][:, sp1],
                                      BF16),
                                     ("beta", "beta_fm", beta_fm, F32)):
                if nm in debug_taps:
                    bap = buf[:] if hasattr(buf, "ap") and not isinstance(buf, bass.AP) else buf
                    dma("sp", tap(nm, bap.shape, dt), bap, [key],
                        ["tap_" + nm])

    def dt_emit(s):
        full = (s >= pre) or full_prefix
        osg = s - pre
        sp1 = s % 2
        SX = f"s{s}:"

        def tail_emit(tb):
            t0 = tb * 128
            dma("act", x_res[:, tb % 2],
                x_in.rearrange("(a p) d -> p a d", p=128)[:, NCH * s + tb, :],
                [], [f"x_res{tb % 2}"], tag=SX + f"xres{tb}")
            pw = ps_a_t[:, 0:1024]
            pwk = ka(0, 1024)
            for dc in range(DC):
                for nh in range(2):
                    mm(pw[:, nh * 512:nh * 512 + 512],
                       ot_b[:, dc, t0:t0 + 128],
                       wo_sb[:, dc, nh * 512:nh * 512 + 512],
                       [f"ot_b{tb}", "wo_sb"], pwk,
                       start=(dc == 0), stop=False, tag=SX + f"wo{tb}_{dc}{nh}")
            for nh in range(2):
                mm(pw[:, nh * 512:nh * 512 + 512], ones1[:],
                   bo_sb[:, nh * 512:nh * 512 + 512],
                   ["ones1", "bo_sb"], pwk, start=False, stop=True,
                   tag=SX + f"wob{tb}{nh}")
            tt(x2_b[:], pw, x_res[:, tb % 2], ALU.add,
               pwk + [f"x_res{tb % 2}"], ["x2_b"], tag=SX + f"x2_{tb}")
            dma("sp", out_t[:, osg * NCH + tb, :], x2_b[:],
                ["x2_b"], [f"out:{osg}:{tb}"], tag=SX + f"outinit{tb}")
            yield
            dve(lambda e, o=stat[:, 16:22], i=x2_b[:, 0:512]: e.bn_stats(o, i),
                ["x2_b"], ["stat2"])
            dve(lambda e, o=stat[:, 22:28], i=x2_b[:, 512:1024]:
                    e.bn_stats(o, i), ["x2_b"], ["stat2"])
            dve(lambda e, o=stat[:, 28:30], i=stat[:, 16:28]: e.bn_aggr(o, i),
                ["stat2"], ["stat2"])
            act(stat[:, 30:31], stat[:, 29:30], AF.Sqrt,
                ["stat2", "cpool"], ["stat2"], bias=cpool[:, 0:1])
            dve(lambda e, o=stat[:, 31:32], i=stat[:, 30:31]:
                    e.reciprocal(o, i), ["stat2"], ["stat2"])
            ts(ln_tm2[:], x2_b[:], stat[:, 28:29], ALU.subtract,
               stat[:, 31:32], ALU.mult, ["x2_b", "stat2"], ["ln_tm2"],
               tag=SX + f"ln2_{tb}")
            for dc in range(DC):
                tpose(ps_tb_t[:, dc * 128:dc * 128 + 128],
                      ln_tm2[:, dc * 128:dc * 128 + 128], identb[:],
                      ["ln_tm2", "identb"], ktb(dc * 128, dc * 128 + 128))
            ts(ln2t[:, :, t0:t0 + 128],
               ps_tb_t[:, 0:1024].rearrange("p (c t) -> p c t", t=128),
               1.0, ALU.mult, None, ALU.bypass, ktb(0, 1024), ["ln2t"],
               tag=SX + f"ln2T{tb}")
            yield
            gp = ps_b_t[:, 0:NE]
            KG_ = kb(0, 512)
            for dc in range(DC):
                mm(gp, ln2t[:, dc, t0:t0 + 128], wg_sb[:, dc, :],
                   ["ln2t", "wg_sb"], KG_, start=(dc == 0), stop=(dc == DC - 1))
            gx = osg * NCH + tb
            if sc_tap is not None:
                ts(sc_tap[:, tb * NE:(tb + 1) * NE], gp, 1.0, ALU.mult, None,
                   ALU.bypass, KG_, ["sc_tap"])
            dve(lambda e, o=gt_sc[:, 8:9], i=gp: e.reduce_max(o, i, axis=AX.X),
                KG_, ["gt"])
            ts(gt_sc[:, 0:8], gp, gt_sc[:, 8:9], ALU.is_equal, None,
               ALU.bypass, KG_ + ["gt"], ["gt"])
            stt(gt_sc[:, 16:24], gt_sc[:, 0:8], -1e30, gp, ALU.mult, ALU.add,
                ["gt"] + KG_, ["gt"])
            dve(lambda e, o=gt_sc[:, 9:10], i=gt_sc[:, 16:24]:
                    e.reduce_max(o, i, axis=AX.X), ["gt"], ["gt"])
            ts(gt_sc[:, 24:32], gt_sc[:, 16:24], gt_sc[:, 9:10], ALU.is_equal,
               None, ALU.bypass, ["gt"], ["gt"])
            tt(gt_sc[:, 10:11], gt_sc[:, 9:10], gt_sc[:, 8:9], ALU.subtract,
               ["gt"], ["gt"])
            act(gt_sc[:, 11:12], gt_sc[:, 10:11], AF.Sigmoid, ["gt"], ["gt"])
            ts(gt_sc[:, 12:13], gt_sc[:, 11:12], -1.0, ALU.mult, 1.0, ALU.add,
               ["gt"], ["gt"])
            ts(gt_sc[:, 24:32], gt_sc[:, 24:32], gt_sc[:, 11:12], ALU.mult,
               None, ALU.bypass, ["gt"], ["gt"])
            stt(g_sb[:, gx, :], gt_sc[:, 0:8], gt_sc[:, 12:13],
                gt_sc[:, 24:32], ALU.mult, ALU.add, ["gt"], ["g_sb"],
                tag=SX + f"g{tb}")
            yield
        for c in (range(NCH) if do_delta else ()):
            c0, c1 = c * CH, (c + 1) * CH
            CX = SX + f"c{c}:"

            for h in HORDER:
                pt, kf = pspar[h % 2]
                sl = h // 2
                kh = head_slice(hat["k"], sp1, h, c0, c1)
                mm(pt[:, sl * 128:sl * 128 + 128], kh, kh,
                   [f"k_hat{sp1}"], kf(sl * 128, sl * 128 + 128),
                   tag=CX + f"kk{h}")
            for h in HORDER:
                pt, kf = pspar[h % 2]
                sl = h // 2
                stt(A_b[:, hp(h) * 128:hp(h) * 128 + 128],
                    pt[:, sl * 128:sl * 128 + 128],
                    betat[:, sp1, c, h:h + 1], trils_b[:], ALU.mult, ALU.mult,
                    kf(sl * 128, sl * 128 + 128) + [f"betat{sp1}", "trils_b"],
                    ["A_b"], tag=CX + f"A{h}")
            yield
            for r in range(H):
                tpose(ps_tb_t[:, r * 128:r * 128 + 128],
                      A_b[:, r * 128:r * 128 + 128], identb[:],
                      ["A_b", "identb"], ktb(r * 128, r * 128 + 128))
            acopy(Bp[0][:, 0:1024], ps_tb_t[:, 0:1024], ktb(0, 1024),
                  ["Bp0"], tag=CX + "B0ev0")
            ts(Bp[0][:, 1024:2048], ps_tb_t[:, 1024:2048], 1.0, ALU.mult,
               None, ALU.bypass, ktb(1024, 2048), ["Bp0"], tag=CX + "B0ev1")
            yield
            for h in range(H):
                kh = head_slice(hat["k"], sp1, h, c0, c1)
                vh = head_slice(hat["v"], sp1, h, c0, c1)
                par = h % 2
                idt = ident64b[64 * par:64 * par + 64, :]
                r = hp(h)
                tpose(ps_tb_t[:, r * 128:r * 128 + 64], kh, idt,
                      [f"k_hat{sp1}", "ident64b"], ktb(r * 128, r * 128 + 64))
                tpose(ps_tb_t[:, r * 128 + 64:r * 128 + 128], vh, idt,
                      [f"v_hat{sp1}", "ident64b"],
                      ktb(r * 128 + 64, r * 128 + 128))
            for h in range(H):
                r = hp(h)
                ts(t_b[0][:, r * 128:r * 128 + 128],
                   ps_tb_t[:, r * 128:r * 128 + 128],
                   betat[:, sp1, c, h:h + 1],
                   ALU.mult, None, ALU.bypass,
                   ktb(r * 128, r * 128 + 128) + [f"betat{sp1}"], ["t0"],
                   tag=CX + f"t0{h}")
            acopy(kn_b[:].rearrange("p (r d) -> p r d", d=HD),
                  ps_tb_t[:].rearrange("p (r two d) -> p r two d",
                                       two=2, d=HD)[:, :, 0, :],
                  ktb(0, 2048), ["kn_b"], tag=CX + "knev")

            yield
            for h in range(H):
                r = hp(h)
                sl_ = slice(r * 128, r * 128 + 128)
                tt(Cp[0][:, sl_], A_b[:, sl_], md16_b[:], ALU.mult,
                   ["A_b", "md16_b"], ["Cp0"], tag=CX + f"d0_{h}", eng="pool")
                tt(Cp[1][:, sl_], Bp[0][:, sl_], md16t_b[:], ALU.mult,
                   ["Bp0", "md16t_b"], ["Cp1"], tag=CX + f"d0t_{h}", eng="pool")
                tt(M_b[:, sl_], identb[:], Cp[0][:, sl_], ALU.subtract,
                   ["identb", "Cp0"], ["M_b"], tag=CX + f"mi_{h}", eng="pool")
                tt(MT_b[:, sl_], identb[:], Cp[1][:, sl_], ALU.subtract,
                   ["identb", "Cp1"], ["MT_b"], tag=CX + f"mti_{h}", eng="pool")

            yield

            def sq_pair(srck, srctk, dstbuf, dstk, dsttbuf, dsttk, tg):
                srcb = {"Cp0": Cp[0], "Cp1": Cp[1], "Bp1": Bp[1],
                        "t1": t_b[1]}[srck]
                srctb = {"Cp0": Cp[0], "Cp1": Cp[1], "Bp1": Bp[1],
                         "t1": t_b[1]}[srctk]
                for h in HORDER:
                    pt, kf = pspar[h % 2]
                    sl = h // 2
                    r = hp(h)
                    mm(pt[:, sl * 128:sl * 128 + 128],
                       srctb[:, r * 128:r * 128 + 128],
                       srcb[:, r * 128:r * 128 + 128],
                       [srck, srctk], kf(sl * 128, sl * 128 + 128),
                       tag=CX + tg + f"s{h}")
                acopy(dstbuf[:, 0:1024], pspar[0][0][:, 0:1024],
                      pspar[0][1](0, 1024), [dstk], tag=CX + tg + "e0")
                ts(dstbuf[:, 1024:2048], pspar[1][0][:, 0:1024], 1.0,
                   ALU.mult, None, ALU.bypass, pspar[1][1](0, 1024), [dstk],
                   tag=CX + tg + "e1")
                yield
                for h in HORDER:
                    pt, kf = pspar[h % 2]
                    sl = h // 2
                    r = hp(h)
                    mm(pt[:, sl * 128:sl * 128 + 128],
                       srcb[:, r * 128:r * 128 + 128],
                       srctb[:, r * 128:r * 128 + 128],
                       [srck, srctk], kf(sl * 128, sl * 128 + 128),
                       tag=CX + tg + f"st{h}")
                acopy(dsttbuf[:, 0:1024], pspar[0][0][:, 0:1024],
                      pspar[0][1](0, 1024), [dsttk], tag=CX + tg + "et0")
                ts(dsttbuf[:, 1024:2048], pspar[1][0][:, 0:1024], 1.0,
                   ALU.mult, None, ALU.bypass, pspar[1][1](0, 1024), [dsttk],
                   tag=CX + tg + "et1")
                yield

            def leaf_prod(pk_, ptk_, tg):
                pb_ = {"Cp0": Cp[0], "Cp1": Cp[1], "Bp1": Bp[1],
                       "t1": t_b[1]}[pk_]
                ptb_ = {"Cp0": Cp[0], "Cp1": Cp[1], "Bp1": Bp[1],
                        "t1": t_b[1]}[ptk_]
                for h in HORDER:
                    pt, kf = pspar[h % 2]
                    sl = h // 2
                    r = hp(h)
                    mm(pt[:, sl * 128:sl * 128 + 128],
                       MT_b[:, r * 128:r * 128 + 128],
                       pb_[:, r * 128:r * 128 + 128],
                       ["MT_b", pk_], kf(sl * 128, sl * 128 + 128),
                       tag=CX + tg + f"p{h}")
                for hh in range(2):
                    pt, kf = pspar[hh]
                    tt(M_b[:, hh * 1024:hh * 1024 + 1024],
                       M_b[:, hh * 1024:hh * 1024 + 1024], pt[:, 0:1024],
                       ALU.add, ["M_b"] + kf(0, 1024), ["M_b"],
                       tag=CX + tg + f"pe{hh}")
                yield
                for h in HORDER:
                    pt, kf = pspar[h % 2]
                    sl = h // 2
                    r = hp(h)
                    mm(pt[:, sl * 128:sl * 128 + 128],
                       pb_[:, r * 128:r * 128 + 128],
                       MT_b[:, r * 128:r * 128 + 128],
                       ["MT_b", pk_], kf(sl * 128, sl * 128 + 128),
                       tag=CX + tg + f"q{h}")
                for hh in range(2):
                    pt, kf = pspar[hh]
                    tt(MT_b[:, hh * 1024:hh * 1024 + 1024],
                       MT_b[:, hh * 1024:hh * 1024 + 1024], pt[:, 0:1024],
                       ALU.add, ["MT_b"] + kf(0, 1024), ["MT_b"],
                       tag=CX + tg + f"qe{hh}")
                yield

            yield from sq_pair("Cp0", "Cp1", Bp[1], "Bp1", t_b[1], "t1",
                               "p2")
            yield from leaf_prod("Bp1", "t1", "l2")
            yield from sq_pair("Bp1", "t1", Cp[0], "Cp0", Cp[1], "Cp1", "p4")
            yield from leaf_prod("Cp0", "Cp1", "l4")
            yield from sq_pair("Cp0", "Cp1", Bp[1], "Bp1", t_b[1], "t1", "p8")
            yield from leaf_prod("Bp1", "t1", "l8")

            for lvl, mask_b, maskk in ((16, mo16_b, "mo16_b"),
                                       (32, mo32_b, "mo32_b")):
                for h in range(H):
                    r = hp(h)
                    sl_ = slice(r * 128, r * 128 + 128)
                    tt(Cp[0][:, sl_], A_b[:, sl_], mask_b[:], ALU.mult,
                       ["A_b", maskk], ["Cp0"], tag=CX + f"E{lvl}_{h}",
                       eng="pool")
                for h in HORDER:
                    pt, kf = pspar[h % 2]
                    sl = h // 2
                    r = hp(h)
                    mm(pt[:, sl * 128:sl * 128 + 128],
                       Cp[0][:, r * 128:r * 128 + 128],
                       MT_b[:, r * 128:r * 128 + 128],
                       ["Cp0", "MT_b"], kf(sl * 128, sl * 128 + 128),
                       tag=CX + f"xt{lvl}_{h}")
                acopy(Cp[1][:, 0:1024], pspar[0][0][:, 0:1024],
                      pspar[0][1](0, 1024), ["Cp1"], tag=CX + f"xte{lvl}0")
                ts(Cp[1][:, 1024:2048], pspar[1][0][:, 0:1024], 1.0, ALU.mult,
                   None, ALU.bypass, pspar[1][1](0, 1024), ["Cp1"],
                   tag=CX + f"xte{lvl}1")
                yield
                for h in HORDER:
                    pt, kf = pspar[h % 2]
                    sl = h // 2
                    r = hp(h)
                    mm(pt[:, sl * 128:sl * 128 + 128],
                       M_b[:, r * 128:r * 128 + 128],
                       Cp[1][:, r * 128:r * 128 + 128],
                       ["M_b", "Cp1"], kf(sl * 128, sl * 128 + 128),
                       tag=CX + f"mt{lvl}_{h}")
                for hh in range(2):
                    pt, kf = pspar[hh]
                    tt(MT_b[:, hh * 1024:hh * 1024 + 1024],
                       MT_b[:, hh * 1024:hh * 1024 + 1024], pt[:, 0:1024],
                       ALU.subtract, ["MT_b"] + kf(0, 1024), ["MT_b"],
                       tag=CX + f"mte{lvl}{hh}")
                yield
                for h in HORDER:
                    pt, kf = pspar[h % 2]
                    sl = h // 2
                    r = hp(h)
                    mm(pt[:, sl * 128:sl * 128 + 128],
                       Cp[1][:, r * 128:r * 128 + 128],
                       M_b[:, r * 128:r * 128 + 128],
                       ["Cp1", "M_b"], kf(sl * 128, sl * 128 + 128),
                       tag=CX + f"m{lvl}_{h}")
                for hh in range(2):
                    pt, kf = pspar[hh]
                    tt(M_b[:, hh * 1024:hh * 1024 + 1024],
                       M_b[:, hh * 1024:hh * 1024 + 1024], pt[:, 0:1024],
                       ALU.subtract, ["M_b"] + kf(0, 1024), ["M_b"],
                       tag=CX + f"me{lvl}{hh}")
                yield

            for h in HORDER:
                pt, kf = pspar[h % 2]
                sl = h // 2
                r = hp(h)
                mm(pt[:, sl * 128:sl * 128 + 128],
                   MT_b[:, r * 128:r * 128 + 128],
                   t_b[0][:, r * 128:r * 128 + 128],
                   ["MT_b", "t0"], kf(sl * 128, sl * 128 + 128),
                   tag=CX + f"y_{h}")
            acopy(Bp[1][:, 0:1024], pspar[0][0][:, 0:1024],
                  pspar[0][1](0, 1024), ["Bp1"], tag=CX + "ye0")
            ts(Bp[1][:, 1024:2048], pspar[1][0][:, 0:1024], 1.0, ALU.mult,
               None, ALU.bypass, pspar[1][1](0, 1024), ["Bp1"], tag=CX + "ye1")
            yield
            for h in HORDER:
                pt, kf = pspar[h % 2]
                sl = h // 2
                r = hp(h)
                mm(pt[64:128, sl * 128:sl * 128 + 128],
                   Bp[0][0:64, r * 128 + 64:r * 128 + 128],
                   Bp[1][0:64, r * 128:r * 128 + 128],
                   ["Bp0", "Bp1"], kf(sl * 128, sl * 128 + 128),
                   tag=CX + f"z_{h}")
            acopy(Cp[0][64:128, 0:1024], pspar[0][0][64:128, 0:1024],
                  pspar[0][1](0, 1024), ["Cp0"], tag=CX + "ze0")
            ts(Cp[0][64:128, 1024:2048], pspar[1][0][64:128, 0:1024], 1.0,
               ALU.mult, None, ALU.bypass, pspar[1][1](0, 1024), ["Cp0"],
               tag=CX + "ze1")
            yield
            for h in HORDER:
                pt, kf = pspar[h % 2]
                sl = h // 2
                r = hp(h)
                mm(pt[64:128, sl * 128:sl * 128 + 128],
                   MT_b[64:128, r * 128 + 64:r * 128 + 128],
                   Cp[0][64:128, r * 128:r * 128 + 128],
                   ["MT_b", "Cp0"], kf(sl * 128, sl * 128 + 128),
                   tag=CX + f"c_{h}")
            acopy(t_b[1][:, 0:1024], Bp[1][:, 0:1024], ["Bp1"], ["t1"],
                  tag=CX + "ucp0")
            ts(t_b[1][:, 1024:2048], Bp[1][:, 1024:2048], 1.0, ALU.mult,
               None, ALU.bypass, ["Bp1"], ["t1"], tag=CX + "ucp1")
            for hh in range(2):
                pt, kf = pspar[hh]
                tt(t_b[1][64:128, hh * 1024:hh * 1024 + 1024],
                   t_b[1][64:128, hh * 1024:hh * 1024 + 1024],
                   pt[64:128, 0:1024], ALU.subtract,
                   ["t1"] + kf(0, 1024), ["t1"], tag=CX + f"usub{hh}")
            tfin = t_b[1]
            tkey = "t1"

            yield
            for h in range(H):
                par, sl = h % 2, h // 2
                r = hp(h)
                tpose(ps_tb_t[64 * par:64 * par + 64,
                              sl * 128:sl * 128 + 128],
                      tfin[:, r * 128:r * 128 + 64], identb[:],
                      [tkey, "identb"], ktb(sl * 128, sl * 128 + 128))
            acopy(wt_b[:], ps_tb_t[:, 0:1024], ktb(0, 1024), ["wt_b"],
                  tag=CX + "wtev")
            for h in range(H):
                par, sl = h % 2, h // 2
                r = hp(h)
                mm(ps_a_t[:, r * 64:r * 64 + 64],
                   wt_b[64 * par:64 * par + 64, sl * 128:sl * 128 + 128],
                   s_sb[64 * par:64 * par + 64, sl * 64:sl * 64 + 64],
                   ["wt_b", "s_sb"], ka(r * 64, r * 64 + 64),
                   start=(r in (0, 8)), tag=CX + f"ws{h}")
            tt(u_b[:].rearrange("p (r d) -> p r d", d=HD),
               tfin[:].rearrange("p (r two d) -> p r two d",
                                 two=2, d=HD)[:, :, 1, :],
               ps_a_t[:, 0:1024].rearrange("p (r d) -> p r d", d=HD),
               ALU.subtract, [tkey] + ka(0, 1024), ["u_b"], tag=CX + "uev")

            yield
            if full:
                for h in HORDER:
                    pt, kf = pspar[h % 2]
                    sl = h // 2
                    kh = head_slice(hat["k"], sp1, h, c0, c1)
                    qh = head_slice(hat["q"], sp1, h, c0, c1)
                    mm(pt[:, sl * 128:sl * 128 + 128], kh, qh,
                       [f"k_hat{sp1}", f"q_hat{sp1}"],
                       kf(sl * 128, sl * 128 + 128), tag=CX + f"kq{h}")
                for h in HORDER:
                    pt, kf = pspar[h % 2]
                    sl = h // 2
                    r = hp(h)
                    tt(Cp[1][:, r * 128:r * 128 + 128],
                       pt[:, sl * 128:sl * 128 + 128], triui_b[:], ALU.mult,
                       kf(sl * 128, sl * 128 + 128) + ["triui_b"], ["Cp1"],
                       tag=CX + f"mkq{h}")
                yield
                for h in range(H):
                    par, sl = h % 2, h // 2
                    r = hp(h)
                    qh = head_slice(hat["q"], sp1, h, c0, c1)
                    mm(ps_b_t[:, r * 64:r * 64 + 64], qh,
                       s_sb[64 * par:64 * par + 64, sl * 64:sl * 64 + 64],
                       [f"q_hat{sp1}", "s_sb"], kb(r * 64, r * 64 + 64),
                       start=(r in (0, 8)), stop=False, tag=CX + f"qs{h}")
                for h in range(H):
                    r = hp(h)
                    mm(ps_b_t[:, r * 64:r * 64 + 64],
                       Cp[1][:, r * 128:r * 128 + 128],
                       u_b[:, r * 64:r * 64 + 64],
                       ["Cp1", "u_b"], kb(r * 64, r * 64 + 64),
                       start=False, stop=True, tag=CX + f"oi{h}")

            yield
            for h in range(H):
                par, sl = h % 2, h // 2
                r = hp(h)
                mm(ps_a_t[64 * par:64 * par + 64, sl * 64:sl * 64 + 64],
                   kn_b[:, r * 64:r * 64 + 64], u_b[:, r * 64:r * 64 + 64],
                   ["kn_b", "u_b"], ka(0, 512), tag=CX + f"chain{h}")
            tt(s_f32[:], s_f32[:], ps_a_t[:, 0:512], ALU.add,
               ["s_f32"] + ka(0, 512), ["s_f32"], tag=CX + "sadd")
            acopy(s_sb[:], s_f32[:], ["s_f32"], ["s_sb"], tag=CX + "sev")
            yield

            if full:
                act(ln_tm2[:], ps_b_t[:, 0:1024], AF.Square,
                    kb(0, 1024), ["ln_tm2"])
                dve(lambda e, o=rms_b[:, 0:H],
                        i=ln_tm2[:].rearrange("p (r d) -> p r d", d=HD):
                        e.reduce_sum(o, i, axis=AX.X),
                    ["ln_tm2"], ["rms"], tag=CX + "ssq")
                act(rms_b[:, 16:16 + H], rms_b[:, 0:H], AF.Sqrt,
                    ["rms", "cpool"], ["rms"], scale=1.0 / HD,
                    bias=cpool[:, 1:2])
                dve(lambda e, o=rms_b[:, 0:H], i=rms_b[:, 16:16 + H]:
                        e.reciprocal(o, i), ["rms"], ["rms"])
                ts(rmsb16[:], rms_b[:, 0:H], 1.0, ALU.mult, None, ALU.bypass,
                   ["rms"], ["rmsb16"])
                tpose(ps_tb_t[0:H, 0:128], rmsb16[:], identb[:],
                      ["rmsb16", "identb"], ktb(0, 128))
                ts(rstdt_b[:], ps_tb_t[0:H, 0:128], 1.0, ALU.mult, None,
                   ALU.bypass, ktb(0, 128), ["rstdt_b"], tag=CX + "rstdt")
                yield
                for nh in range(2):
                    mm(ps_a_t[:, nh * 512:nh * 512 + 512], rstdt_b[:],
                       nwsel_b[:, nh * 512:nh * 512 + 512],
                       ["rstdt_b", "nwsel_b"], ka(nh * 512, nh * 512 + 512),
                       tag=CX + f"nwex{nh}")
                ts(nwex_b[:], ps_a_t[:, 0:1024], 1.0, ALU.mult, None,
                   ALU.bypass, ka(0, 1024), ["nwex_b"], tag=CX + "nwev")
                tt(onat_b[:], ps_b_t[:, 0:1024], nwex_b[:], ALU.mult,
                   kb(0, 1024) + ["nwex_b"], ["onat_b"], tag=CX + "onat")
                for h in range(H):
                    par, sl = h % 2, h // 2
                    r = hp(h)
                    tpose(ps_tb_t[64 * par:64 * par + 64,
                                  sl * 128:sl * 128 + 128],
                          onat_b[:, r * 64:r * 64 + 64], identb[:],
                          ["onat_b", "identb"], ktb(sl * 128, sl * 128 + 128))
                ts(ot_b[:, :, c0:c1],
                   ps_tb_t[:, 0:1024].rearrange("p (c t) -> p c t", t=128),
                   1.0, ALU.mult, None, ALU.bypass, ktb(0, 1024),
                   [f"ot_b{c}"], tag=CX + "otev")
                yield
            if full and osg >= 0 and do_wo:
                yield from tail_emit(c)

        if not full or osg < 0:
            return
        if not do_wo:
            dma("sp", out_t[:, osg * NCH:(osg + 1) * NCH, :], x_sb[:],
                ["x_sb"], [f"out:{osg}:0"], tag=SX + "outpass")
            return

        if ffn:
            scr = ln2t_scr.rearrange("p (c t) -> p c t", t=owntok)
            dma("sp", scr[:, :, osg * SEG:(osg + 1) * SEG], ln2t[:],
                ["ln2t"], [f"scr:{osg}"], tag=SX + "scrst")
        if s == nseg - 1:
            for nm, key, buf, dt in (
                                     ("ot", "ot_b3", ot_b, BF16),
                                     ("u", "u_b", u_b, BF16),
                                     ("ssb", "s_sb", s_sb, BF16),
                                     ("g", "g_sb", g_sb, F32),
                                     ("scores", "sc_tap", sc_tap, F32),
                                     ("ln2t", "ln2t", ln2t, BF16),
                                     ("stat", "stat", stat[:, 0:16], F32)):
                if nm in debug_taps:
                    dma("sp", tap(nm, buf.shape, dt), buf[:], [key],
                        ["tap_" + nm])

    FE_EVERY = 2
    _done = object()
    for _ in fe_emit(0):
        pass
    for s in range(nseg):
        dg = dt_emit(s)
        fg = fe_emit(s + 1) if s + 1 < nseg else iter(())
        nq = 0
        while True:
            d = next(dg, _done)
            nq += 1
            if nq % FE_EVERY == 0:
                next(fg, None)
            if d is _done:
                for _ in fg:
                    pass
                break

    esm.close()

    if ffn:
        p.barrier()
        esf = contextlib.ExitStack()
        def fsb(name, shape, dt):
            return sbuf_in(esf, name, shape, dt)
        w1_sb = fsb("w1_sb", (128, DC, FF), BF16)
        w2_sb = fsb("w2_sb", (128, DC, FF), BF16)
        w3_sb = fsb("w3_sb", (128, FC, D), BF16)
        l2_sb = [fsb(f"l2_sb{i}", (128, DC, TFF), BF16) for i in range(2)]
        hid_sb = fsb("hid_sb", (128, FC, TFF), BF16)
        sil_sc = [fsb(f"sil_sc{i}", (128, 512), BF16) for i in range(2)]
        og_sb = [fsb(f"og_sb{i}", (128, 2, D), F32) for i in range(2)]
        scr = ln2t_scr.rearrange("p (c t) -> p c t", t=owntok)

        for e_i in range(n_experts):
            EX = f"e{e_i}:"
            dma("pool", w1_sb[:], w1t[e_i].rearrange("(c p) f -> p c f", p=128),
                [], ["w1_sb"], tag=EX + "w1")
            dma("pool", w2_sb[:], w2t[e_i].rearrange("(c p) f -> p c f", p=128),
                [], ["w2_sb"], tag=EX + "w2")
            dma("pool", w3_sb[:], w3t[e_i].rearrange("(c p) f -> p c f", p=128),
                [], ["w3_sb"], tag=EX + "w3")
            for t in range(nft):
                TX = EX + f"t{t}:"
                l2 = l2_sb[t % 2]
                l2k = f"l2_{t % 2}"
                dma("sp", l2[:], scr[:, :, t * TFF:(t + 1) * TFF],
                    [f"scr:{2 * t}", f"scr:{2 * t + 1}"], [l2k], tag=TX + "l2")
                for fc in range(FC):
                    for th in range(2):
                        ph1 = ps_a_t[:, th * 512:th * 512 + 512]
                        ph1k = ka(th * 512, th * 512 + 512)
                        ph2 = ps_b_t[:, th * 512:th * 512 + 512]
                        ph2k = kb(th * 512, th * 512 + 512)
                        for dc in range(DC):
                            mm(ph1, w1_sb[:, dc, fc * 128:fc * 128 + 128],
                               l2[:, dc, th * 512:th * 512 + 512],
                               ["w1_sb", l2k], ph1k,
                               start=(dc == 0), stop=(dc == DC - 1))
                        for dc in range(DC):
                            mm(ph2, w2_sb[:, dc, fc * 128:fc * 128 + 128],
                               l2[:, dc, th * 512:th * 512 + 512],
                               ["w2_sb", l2k], ph2k,
                               start=(dc == 0), stop=(dc == DC - 1))
                        sil = sil_sc[th]
                        act(sil[:], ph1, AF.Silu, ph1k, [f"sil{th}"],
                            tag=TX + f"sil{fc}_{th}")
                        tt(hid_sb[:, fc, th * 512:th * 512 + 512], sil[:],
                           ph2, ALU.mult, [f"sil{th}"] + ph2k, ["hid"],
                           tag=TX + f"hid{fc}_{th}")
                for tb in range(TFF // 128):
                    po = ps_a_t[:, 0:1024] if tb % 2 == 0 else ps_b_t[:, 0:1024]
                    pok = ka(0, 1024) if tb % 2 == 0 else kb(0, 1024)
                    for fc in range(FC):
                        for nh in range(2):
                            mm(po[:, nh * 512:nh * 512 + 512],
                               hid_sb[:, fc, tb * 128:tb * 128 + 128],
                               w3_sb[:, fc, nh * 512:nh * 512 + 512],
                               ["hid", "w3_sb"], pok,
                               start=(fc == 0), stop=(fc == FC - 1),
                               tag=TX + f"w3_{tb}_{fc}{nh}")
                    gx = t * (TFF // 128) + tb
                    ob = og_sb[(tb // 2) % 2]
                    obk = f"og{(tb // 2) % 2}"
                    ts(ob[:, tb % 2, :], po, g_sb[:, gx, e_i:e_i + 1],
                       ALU.mult, None, ALU.bypass, pok + ["g_sb"], [obk],
                       tag=TX + f"ogev{tb}")
                    if tb % 2 == 1:
                        blk = (gx - 1) // 2
                        ok1 = f"out:{(2 * blk) // NCH}:{(2 * blk) % NCH}"
                        ok2 = f"out:{(2 * blk + 1) // NCH}:{(2 * blk + 1) % NCH}"
                        dma_acc("pool", out_t[:, 2 * blk:2 * blk + 2, :],
                                ob[:], [obk, ok1, ok2], [ok1, ok2],
                                tag=TX + f"acc{tb}")
        esf.close()

    allk = [k for k in p.last_writer if k.startswith(("out:", "tap_"))]
    p.emit("sp", lambda e: e.nop(), reads=allk, tag="final")

    p.finalize()
    p.emit_block(es)
    es.close()
    return nc, taps



def consts():
    c = {}
    c["ident"] = np.eye(128, dtype=np.float32)
    c["ident64"] = np.vstack([np.eye(64), np.eye(64)]).astype(np.float32)
    c["trils"] = np.tril(np.ones((128, 128), np.float32), -1)
    md = np.zeros((128, 128), np.float32)
    for s0 in range(0, 128, 16):
        md[s0:s0 + 16, s0:s0 + 16] = np.tril(np.ones((16, 16)), -1)
    c["md16"] = md
    c["md16t"] = np.ascontiguousarray(md.T)
    mo16 = np.zeros((128, 128), np.float32)
    for s0 in range(0, 128, 32):
        mo16[s0 + 16:s0 + 32, s0:s0 + 16] = 1
    c["mo16"] = mo16
    mo32 = np.zeros((128, 128), np.float32)
    for s0 in range(0, 128, 64):
        mo32[s0 + 32:s0 + 64, s0:s0 + 32] = 1
    c["mo32"] = mo32
    c["triui"] = np.triu(np.ones((128, 128), np.float32), 0)
    sel = np.zeros((DC, 128, H), np.float32)
    for oc in range(DC):
        for pp_ in range(128):
            sel[oc, pp_, 2 * oc + pp_ // 64] = 1.0
    c["selnorm"] = sel
    c["selexp"] = np.ascontiguousarray(sel.transpose(0, 2, 1))
    return c


def stage_weights(Wq, bq, Wk, bk, Wv, bv, Wbeta, bbeta, Wo, bo, wq_conv,
                  bq_conv, wk_conv, bk_conv, wv_conv, bv_conv, norm_w,
                  Wg, W1, W2, W3):
    f32 = np.float32

    def wT(w):
        return np.ascontiguousarray(np.asarray(w, f32).T)

    def ccol(v):
        return np.ascontiguousarray(np.asarray(v, f32).reshape(DC, 128).T)

    def cconv(w):
        return np.ascontiguousarray(
            np.asarray(w, f32).reshape(DC, 128, KC).transpose(1, 0, 2))

    nw = np.asarray(norm_w, f32)
    nwsel = np.zeros((H, D), f32)
    for h in range(H):
        nwsel[h, h * HD:(h + 1) * HD] = nw
    return dict(
        wqt=wT(Wq), wkt=wT(Wk), wvt=wT(Wv), wot=wT(Wo),
        wbetat=np.ascontiguousarray(wT(Wbeta).reshape(DC, 128, H)),
        wgt=np.ascontiguousarray(wT(Wg).reshape(DC, 128, NE)),
        bq=ccol(bq), bk=ccol(bk), bv=ccol(bv),
        bo_row=np.ascontiguousarray(np.asarray(bo, f32).reshape(1, D)),
        bbeta=np.ascontiguousarray(np.asarray(bbeta, f32).reshape(H, 1)),
        wqconv=cconv(wq_conv), wkconv=cconv(wk_conv), wvconv=cconv(wv_conv),
        bqconv=ccol(bq_conv), bkconv=ccol(bk_conv), bvconv=ccol(bv_conv),
        w1t=np.ascontiguousarray(np.asarray(W1, f32).transpose(0, 2, 1)),
        w2t=np.ascontiguousarray(np.asarray(W2, f32).transpose(0, 2, 1)),
        w3t=np.ascontiguousarray(np.asarray(W3, f32).transpose(0, 2, 1)),
        nwsel=nwsel,
        **consts(),
    )


def kernel(L0_attn_x, L0_attn_Wq, L0_attn_bq, L0_attn_Wk, L0_attn_bk,
           L0_attn_Wv, L0_attn_bv, L0_attn_Wbeta, L0_attn_bbeta, L0_attn_Wo,
           L0_attn_bo, L0_attn_wq_conv, L0_attn_bq_conv, L0_attn_wk_conv,
           L0_attn_bk_conv, L0_attn_wv_conv, L0_attn_bv_conv, L0_attn_norm_w,
           L0_ffn_Wg, L0_ffn_W1, L0_ffn_W2, L0_ffn_W3):
    f32 = np.float32
    x = np.asarray(L0_attn_x, f32)
    nc, _ = build_program()
    common = stage_weights(
        L0_attn_Wq, L0_attn_bq, L0_attn_Wk, L0_attn_bk, L0_attn_Wv,
        L0_attn_bv, L0_attn_Wbeta, L0_attn_bbeta, L0_attn_Wo, L0_attn_bo,
        L0_attn_wq_conv, L0_attn_bq_conv, L0_attn_wk_conv, L0_attn_bk_conv,
        L0_attn_wv_conv, L0_attn_bv_conv, L0_attn_norm_w, L0_ffn_Wg,
        L0_ffn_W1, L0_ffn_W2, L0_ffn_W3)

    in_maps = []
    for core in range(8):
        b, h = core // 2, core % 2
        xb = x[b]
        if h == 0:
            xp = np.zeros((L, D), f32)
            xp[PRE * SEG:] = xb[:OWN * SEG]
            kvm = np.zeros((H, PRE * SEG), f32)
            hm = np.zeros((128, 1), f32)
        else:
            xp = np.ascontiguousarray(xb)
            kvm = np.ones((H, PRE * SEG), f32)
            hm = np.ones((128, 1), f32)
        m = dict(common)
        m["x"] = xp
        m["kvmask"] = kvm
        m["halomask"] = hm
        in_maps.append(m)

    res = run_bass_kernel_spmd(nc, in_maps, core_ids=list(range(8)))
    outp = np.empty((B, L, D), f32)
    for core in range(8):
        b, h = core // 2, core % 2
        o = res.results[core]["out"]
        if h == 0:
            outp[b, :OWN * SEG] = o
        else:
            outp[b, PRE * SEG:] = o
    return outp


# version 17
# speedup vs baseline: 1.0942x; 1.0224x over previous
import contextlib
import numpy as np
import ml_dtypes
import concourse.bass as bass
import concourse.mybir as mybir
from concourse.bass_utils import run_bass_kernel_spmd

F32 = mybir.dt.float32
BF16 = mybir.dt.bfloat16
AF = mybir.ActivationFunctionType
ALU = mybir.AluOpType
AX = mybir.AxisListType

B, L, D, H, HD = 4, 8192, 1024, 16, 64
KC, NE, FF = 4, 8, 2048
SEG, CH = 512, 128
NSEG = L // SEG
NCH = SEG // CH
PRE = 8
OWN = NSEG - PRE
LN_EPS = 1e-5
EPS = 1e-6
TFF = 1024
DC = D // 128
FC = FF // 128

ENGINES = ("pe", "act", "dve", "pool", "sp")


def hp(h):
    return (h % 2) * 8 + h // 2


HORDER = [0, 2, 4, 6, 8, 10, 12, 14, 1, 3, 5, 7, 9, 11, 13, 15]


class _Op:
    __slots__ = ("engine", "fn", "deps", "idx", "dma", "inc_sem", "inc_amt",
                 "inc_val", "waits", "prewaits", "tag")

    def __init__(self, engine, fn, deps, idx, dma, tag):
        self.engine, self.fn, self.deps, self.idx, self.dma, self.tag = (
            engine, fn, deps, idx, dma, tag)
        self.inc_sem = None
        self.inc_amt = 0
        self.inc_val = 0
        self.waits = []
        self.prewaits = []


class Prog:
    def __init__(self, nc):
        self.nc = nc
        self.ops = []
        self.last_writer = {}
        self.readers = {}
        self.barrier_idx = None

    def emit(self, engine, fn, reads=(), writes=(), dma=False, tag=""):
        ps_r = [k for k in reads if k.startswith("ps_")]
        if ps_r:
            reads = [k for k in reads if not k.startswith("ps_")]
            writes = list(writes) + ps_r
        idx = len(self.ops)
        deps = set()
        if self.barrier_idx is not None:
            deps.add(self.barrier_idx)
        for k in reads:
            w = self.last_writer.get(k)
            if w is not None:
                deps.add(w)
        for k in writes:
            w = self.last_writer.get(k)
            if w is not None:
                deps.add(w)
            for r in self.readers.get(k, ()):
                deps.add(r)
        deps.discard(idx)
        for k in reads:
            self.readers.setdefault(k, []).append(idx)
        for k in writes:
            self.last_writer[k] = idx
            self.readers[k] = []
        self.ops.append(_Op(engine, fn, deps, idx, dma, tag))
        return idx

    def barrier(self, engine="sp"):
        keys = list(self.last_writer.keys())
        idx = self.emit(engine, lambda e: e.nop(), reads=keys, tag="barrier")
        self.barrier_idx = idx
        return idx

    DMA_SEM_ROT = 8

    def finalize(self):
        ops = self.ops
        for o in ops:
            pruned = set()
            for d in o.deps:
                pp = ops[d]
                if (pp.engine == "pe" and o.engine == "pe"
                        and not pp.dma and not o.dma):
                    continue
                pruned.add(d)
            best = {}
            keep = set()
            for d in pruned:
                pp = ops[d]
                if pp.dma:
                    keep.add(d)
                else:
                    key = pp.engine
                    if key not in best or d > best[key]:
                        best[key] = d
            o.deps = keep | set(best.values())
        needed = set()
        for o in ops:
            needed.update(o.deps)
        cnt = {}
        dma_idx = {}
        R = self.DMA_SEM_ROT
        sem_names = set()
        for o in ops:
            if o.dma:
                i = dma_idx.get(o.engine, 0)
                dma_idx[o.engine] = i + 1
                sem = f"dq_{o.engine}_{i % R}"
                sem_names.add(sem)
                o.inc_sem, o.inc_amt = sem, 16
                o.inc_val = 16 * (i // R + 1)
                if i >= R:
                    o.prewaits = [(sem, 16 * (i // R))]
            elif o.idx in needed:
                sem = "eng_" + o.engine
                sem_names.add(sem)
                cnt[sem] = cnt.get(sem, 0) + 1
                o.inc_sem, o.inc_amt, o.inc_val = sem, 1, cnt[sem]
        self.sem_names = sorted(sem_names)
        known = {e: {} for e in ENGINES}
        by_engine = {e: [] for e in ENGINES}
        for o in ops:
            by_engine[o.engine].append(o)
            k = known[o.engine]
            for d in sorted(o.deps):
                pp = ops[d]
                if pp.inc_sem is None:
                    continue
                if k.get(pp.inc_sem, 0) >= pp.inc_val:
                    continue
                o.waits.append((pp.inc_sem, pp.inc_val))
                k[pp.inc_sem] = pp.inc_val
        finals = {}
        for o in ops:
            if o.dma:
                finals[o.inc_sem] = max(finals.get(o.inc_sem, 0), o.inc_val)
        fin = by_engine["sp"][-1]
        k = known["sp"]
        for sem in sorted(finals):
            if k.get(sem, 0) < finals[sem]:
                fin.waits.append((sem, finals[sem]))
                k[sem] = finals[sem]
        self.by_engine = by_engine

    def emit_block(self, es):
        sems = {name: es.enter_context(self.nc.semaphore(name))
                for name in self.sem_names}
        block = es.enter_context(self.nc.Block())
        prog = self

        def run(eng_name, eng):
            for o in prog.by_engine[eng_name]:
                for sem, val in o.prewaits:
                    eng.wait_ge(sems[sem], val)
                for sem, val in o.waits:
                    eng.wait_ge(sems[sem], val)
                instr = o.fn(eng)
                if o.inc_sem is not None:
                    assert instr is not None, f"op {o.tag} returned no instr"
                    instr.then_inc(sems[o.inc_sem], o.inc_amt)

        @block.tensor
        def _(t):
            run("pe", t)

        @block.scalar
        def _(s):
            run("act", s)

        @block.vector
        def _(v):
            run("dve", v)

        @block.gpsimd
        def _(g):
            run("pool", g)

        @block.sync
        def _(sy):
            run("sp", sy)


def head_slice(buf, sp_, h, c0, c1):
    par, oc = h % 2, h // 2
    return buf[64 * par:64 * par + 64, sp_, oc, c0:c1]


def build_program(pre=PRE, own=OWN, ffn=True, n_experts=NE, debug_taps=(),
                  do_delta=True, do_wo=True, fe_parts=("ln", "beta", "k", "v", "q"),
                  conv_on=True, norm_on=True, conv_ocs=DC,
                  cv_omit=(), full_prefix=False):
    nseg = pre + own
    ntok = nseg * SEG
    owntok = own * SEG
    nft = owntok // TFF if ffn else 0
    nc = bass.Bass("TRN2", target_bir_lowering=False)
    p = Prog(nc)

    def din(name, shape, dt=F32):
        return nc.dram_tensor(name, list(shape), dt, kind="ExternalInput").ap()

    x_in = din("x", (ntok, D))
    kvmask = din("kvmask", (H, pre * SEG)) if pre else None
    halomask = din("halomask", (128, 1))
    wqt = din("wqt", (D, D)); wkt = din("wkt", (D, D)); wvt = din("wvt", (D, D))
    wot = din("wot", (D, D))
    wbetat = din("wbetat", (DC, 128, H))
    wgt = din("wgt", (DC, 128, NE))
    b_in = {n: din(n, (128, DC)) for n in ("bq", "bk", "bv")}
    bo_row = din("bo_row", (1, D))
    bbeta_in = din("bbeta", (H, 1))
    wconv_in = {pr: din(f"w{pr}conv", (128, DC, KC)) for pr in "qkv"}
    bconv_in = {pr: din(f"b{pr}conv", (128, DC)) for pr in "qkv"}
    w1t = din("w1t", (n_experts, D, FF)); w2t = din("w2t", (n_experts, D, FF))
    w3t = din("w3t", (n_experts, FF, D))
    ident_in = din("ident", (128, 128))
    ident64_in = din("ident64", (128, 64))
    trils_in = din("trils", (128, 128))
    md16_in = din("md16", (128, 128))
    md16t_in = din("md16t", (128, 128))
    mo16_in = din("mo16", (128, 128))
    mo32_in = din("mo32", (128, 128))
    triui_in = din("triui", (128, 128))
    selnorm_in = din("selnorm", (DC, 128, H))
    selexp_in = din("selexp", (DC, H, 128))
    nwsel_in = din("nwsel", (H, D))

    out = nc.dram_tensor("out", [owntok, D], F32, kind="ExternalOutput").ap()
    out_t = out.rearrange("(a p) d -> p a d", p=128)
    ln2t_scr = (nc.dram_tensor("ln2t_scr", [128, DC * owntok], BF16).ap()
                if ffn else None)
    wscr_t = nc.dram_tensor("wscr", [3 * DC, 128, DC, 128], BF16).ap()
    PRI = {"q": 0, "k": 1, "v": 2}
    wcast_done = set()

    taps = {}
    def tap(name, shape, dt=F32):
        ap = nc.dram_tensor("tap_" + name, list(shape), dt,
                            kind="ExternalOutput").ap()
        taps[name] = ap
        return ap

    es = contextlib.ExitStack()
    def sbuf_in(stack, name, shape, dt):
        return stack.enter_context(nc.sbuf_tensor(name, list(shape), dt))

    def sbuf(name, shape, dt):
        return sbuf_in(es, name, shape, dt)

    g_sb = sbuf("g_sb", (128, max(owntok // 128, 1), NE), F32)
    ones1 = sbuf("ones1", (1, 128), BF16)

    ps_f_t = es.enter_context(nc.psum_tensor("ps_f", [128, 512], F32))
    ps_a_t = es.enter_context(nc.psum_tensor("ps_a", [128, 1024], F32))
    ps_b_t = es.enter_context(nc.psum_tensor("ps_b", [128, 1024], F32))
    ps_e_t = es.enter_context(nc.psum_tensor("ps_e", [128, 512], F32))
    ps_tb_t = es.enter_context(nc.psum_tensor("ps_tb", [128, 2048], BF16))

    def pkeys(name, lo_el, hi_el, el_per_bank):
        b0, b1 = lo_el // el_per_bank, (hi_el + el_per_bank - 1) // el_per_bank
        return [f"{name}:{i}" for i in range(b0, b1)]

    def ka(lo, hi):  return pkeys("ps_a", lo, hi, 512)
    def kb(lo, hi):  return pkeys("ps_b", lo, hi, 512)
    def ktb(lo, hi): return pkeys("ps_tb", lo, hi, 1024)
    KF_, KE_ = ["ps_f:0"], ["ps_e:0"]

    seq = [0]
    def utag(pfx=""):
        seq[0] += 1
        return f"{pfx}{seq[0]}"

    def dma(queue, dst, src, reads, writes, tag=""):
        p.emit(queue, lambda e, d=dst, s=src: e.dma_start(d, s),
               reads=reads, writes=writes, dma=True, tag=tag or utag("dma"))

    def dma_acc(queue, dst, src, reads, writes, tag=""):
        p.emit(queue,
               lambda e, d=dst, s=src: e.dma_start(d, s, accum_op=ALU.add),
               reads=reads, writes=writes, dma=True, tag=tag or utag("dmac"))

    def mm(out_ap, lhsT, rhs, reads, writes, start=True, stop=True, tag=""):
        p.emit("pe",
               lambda e, o=out_ap, l=lhsT, r=rhs, st=start, sp=stop:
                   e.matmul(o, l, r, start=st, stop=sp, skip_group_check=True),
               reads=reads, writes=writes, tag=tag or utag("mm"))

    def tpose(out_ap, in_ap, idt, reads, writes, tag=""):
        p.emit("pe", lambda e, o=out_ap, i=in_ap, d=idt: e.transpose(o, i, d),
               reads=reads, writes=writes, tag=tag or utag("tp"))

    def act(out_ap, in_ap, func, reads, writes, bias=..., scale=...,
            accum_out=None, tag=""):
        kw = {}
        if bias is not ...:
            kw["bias"] = bias
        if scale is not ...:
            kw["scale"] = scale
        if accum_out is not None:
            kw["accum_out"] = accum_out
        p.emit("act",
               lambda e, o=out_ap, i=in_ap, f=func, kw=kw:
                   e.activation(o, i, f, **kw),
               reads=reads, writes=writes, tag=tag or utag("act"))

    def ts(out_ap, in_ap, s1, op0, s2, op1, reads, writes, tag="", eng="dve"):
        p.emit(eng,
               lambda e, o=out_ap, i=in_ap, a=s1, b=s2, o0=op0, o1=op1:
                   e.tensor_scalar(o, i, a, b, o0, o1),
               reads=reads, writes=writes, tag=tag or utag("ts"))

    def tt(out_ap, in0, in1, op, reads, writes, tag="", eng="dve"):
        p.emit(eng,
               lambda e, o=out_ap, a=in0, b=in1, oo=op:
                   e.tensor_tensor(o, a, b, oo),
               reads=reads, writes=writes, tag=tag or utag("tt"))

    def acopy(out_ap, in_ap, reads, writes, scale=None, tag=""):
        if scale is None:
            p.emit("act", lambda e, o=out_ap, i=in_ap: e.copy(o, i),
                   reads=reads, writes=writes, tag=tag or utag("ac"))
        else:
            p.emit("act",
                   lambda e, o=out_ap, i=in_ap, sc=scale:
                       e.activation(o, i, AF.Copy, scale=sc),
                   reads=reads, writes=writes, tag=tag or utag("ac"))

    def stt(out_ap, in0, scalar, in1, op0, op1, reads, writes, tag=""):
        p.emit("dve",
               lambda e, o=out_ap, a=in0, s=scalar, b=in1, o0=op0, o1=op1:
                   e.scalar_tensor_tensor(o, a, s, b, o0, o1),
               reads=reads, writes=writes, tag=tag or utag("stt"))

    def dve(fn, reads, writes, tag=""):
        p.emit("dve", fn, reads=reads, writes=writes, tag=tag or utag("dve"))

    esm = contextlib.ExitStack()
    def msb(name, shape, dt):
        return sbuf_in(esm, name, shape, dt)

    identb = msb("identb", (128, 128), BF16)
    identf16 = msb("identf16", (16, 16), F32)
    ident64b = msb("ident64b", (128, 64), BF16)
    trils_b = msb("trils_b", (128, 128), BF16)
    md16_b = msb("md16_b", (128, 128), BF16)
    md16t_b = msb("md16t_b", (128, 128), BF16)
    mo16_b = msb("mo16_b", (128, 128), BF16)
    mo32_b = msb("mo32_b", (128, 128), BF16)
    triui_b = msb("triui_b", (128, 128), BF16)
    selnorm_b = msb("selnorm_b", (128, DC, H), BF16)
    selexp_b = msb("selexp_b", (H, DC, 128), BF16)
    nwsel_b = msb("nwsel_b", (H, D), BF16)
    wstr = {pr: msb(f"w{pr}str", (128, 2, DC, 128), BF16) for pr in "qkv"}
    wo_sb = msb("wo_sb", (128, DC, D), BF16)
    wbeta_sb = msb("wbeta_sb", (128, DC, H), BF16)
    wg_sb = msb("wg_sb", (128, DC, NE), BF16)
    bias_sb = {n: msb(n + "_sb", (128, DC), F32) for n in ("bq", "bk", "bv")}
    bconv_sb = {pr: msb(f"b{pr}c_sb", (128, DC), F32) for pr in "qkv"}
    wconv_sb = {pr: msb(f"w{pr}c_sb", (128, DC, KC), F32) for pr in "qkv"}
    diag_sb = [msb(f"diag{i}", (128, KC, 128), BF16) for i in range(2)]
    bo_sb = msb("bo_sb", (1, D), BF16)
    bbeta_sb = msb("bbeta_sb", (H, 1), F32)
    hmask_sb = msb("hmask_sb", (128, 1), F32)
    kvmask_sb = msb("kvmask_sb", (H, SEG), F32) if pre else None
    cpool = msb("cpool", (128, 2), F32)

    x_sb = msb("x_sb", (128, NCH, D), F32)
    stat = msb("stat", (128, 32), F32)
    ln_tm = msb("ln_tm", (128, D), BF16)
    ln_tm2 = msb("ln_tm2", (128, D), BF16)
    x_res = msb("x_res", (128, 2, D), F32)
    ln1t = msb("ln1t", (128, DC, SEG), BF16)
    pre_sc = [msb(f"pre_sc{i}", (128, 3 + SEG), BF16) for i in range(2)]
    tail_sb = {pr: msb(f"tail_{pr}", (128, DC, 3), BF16) for pr in "qkv"}
    sq_sc = [msb(f"sq_sc{i}", (128, SEG), BF16) for i in range(2)]
    hat = {pr: msb(pr + "_hat", (128, 2, DC, SEG), BF16) for pr in "qkv"}
    rn_sc = msb("rn_sc", (H, SEG), F32)
    rnorm = msb("rnorm", (H, 2 * SEG), BF16)
    beta_fm = msb("beta_fm", (H, SEG), F32)
    betat = msb("betat", (128, 2, NCH, H), F32)

    A_b = msb("A_b", (128, H * 128), BF16)
    Bp = [msb(f"Bp{j}", (128, H * 128), BF16) for j in range(2)]
    Cp = [msb(f"Cp{j}", (128, H * 128), BF16) for j in range(2)]
    t_b = [msb(f"t_b{j}", (128, H * 128), BF16) for j in range(2)]
    M_b = msb("M_b", (128, H * 128), BF16)
    MT_b = msb("MT_b", (128, H * 128), BF16)
    kn_b = msb("kn_b", (128, H * HD), BF16)
    wt_b = msb("wt_b", (128, 8 * 128), BF16)
    u_b = msb("u_b", (128, H * HD), BF16)
    s_sb = msb("s_sb", (128, 8 * HD), BF16)
    s_f32 = msb("s_f32", (128, 8 * HD), F32)
    rms_b = msb("rms_b", (128, 32), F32)
    rstdt_b = msb("rstdt_b", (H, 128), BF16)
    rmsb16 = msb("rmsb16", (128, H), BF16)
    nwex_b = msb("nwex_b", (128, H * HD), BF16)
    onat_b = msb("onat_b", (128, H * HD), BF16)
    ot_b = msb("ot_b", (128, DC, SEG), BF16)
    x2_b = msb("x2_b", (128, D), F32)
    ln2t = msb("ln2t", (128, DC, SEG), BF16)
    gt_sc = msb("gt_sc", (128, 32), F32)
    sc_tap = msb("sc_tap", (128, NCH * NE), F32) if "scores" in debug_taps else None

    dma("sp", identf16[:], ident_in[0:16, 0:16], [], ["identf16"])
    dma("pool", identb[:], ident_in, [], ["identb"])
    dma("pool", ident64b[:], ident64_in, [], ["ident64b"])
    dma("pool", trils_b[:], trils_in, [], ["trils_b"])
    dma("pool", md16_b[:], md16_in, [], ["md16_b"])
    dma("pool", md16t_b[:], md16t_in, [], ["md16t_b"])
    dma("pool", mo16_b[:], mo16_in, [], ["mo16_b"])
    dma("pool", mo32_b[:], mo32_in, [], ["mo32_b"])
    dma("pool", triui_b[:], triui_in, [], ["triui_b"])
    dma("pool", selnorm_b[:], selnorm_in.rearrange("c p h -> p c h"), [],
        ["selnorm_b"])
    dma("pool", selexp_b[:], selexp_in.rearrange("c h p -> h c p"), [],
        ["selexp_b"])
    dma("pool", nwsel_b[:], nwsel_in, [], ["nwsel_b"])
    dve(lambda e: e.memset(ones1[:], 1.0), [], ["ones1"])
    dve(lambda e: e.memset(s_sb[:], 0.0), [], ["s_sb"])
    dve(lambda e: e.memset(s_f32[:], 0.0), [], ["s_f32"])
    dma("pool", wo_sb[:], wot.rearrange("(c p) o -> p c o", p=128), [],
        ["wo_sb"])
    wsrc = {"q": wqt.rearrange("(c p) o -> p c o", p=128),
            "k": wkt.rearrange("(c p) o -> p c o", p=128),
            "v": wvt.rearrange("(c p) o -> p c o", p=128)}
    dma("pool", wbeta_sb[:], wbetat.rearrange("c p h -> p c h"), [], ["wbeta_sb"])
    dma("pool", wg_sb[:], wgt.rearrange("c p h -> p c h"), [], ["wg_sb"])
    for nm in ("bq", "bk", "bv"):
        dma("sp", bias_sb[nm][:], b_in[nm], [], [nm + "_sb"])
    for pr in "qkv":
        dma("sp", bconv_sb[pr][:], bconv_in[pr], [], [f"b{pr}c_sb"])
        dma("sp", wconv_sb[pr][:], wconv_in[pr], [], [f"w{pr}c_sb"])
    dma("pool", bo_sb[:], bo_row, [], ["bo_sb"])
    dma("sp", bbeta_sb[:], bbeta_in, [], ["bbeta_sb"])
    dma("sp", hmask_sb[:], halomask, [], ["hmask_sb"])
    dve(lambda e: e.memset(cpool[:, 0:1], LN_EPS), [], ["cpool"])
    dve(lambda e: e.memset(cpool[:, 1:2], EPS), [], ["cpool"])
    for pr in "qkv":
        dve(lambda e, t=tail_sb[pr]: e.memset(t[:], 0.0), [], [f"tail_{pr}"])

    BMAP = {"q": "bq", "k": "bk", "v": "bv"}
    pspar = {0: (ps_a_t, ka), 1: (ps_b_t, kb)}

    def fe_emit(s):
        full = (s >= pre) or full_prefix
        sp1 = s % 2
        SX = f"s{s}f:"

        def wload(pr, oc):
            par = oc % 2
            if (pr, oc) in wcast_done:
                dma("act", wstr[pr][:, par], wscr_t[PRI[pr] * DC + oc],
                    [f"wscr{pr}{oc}"], [f"w{pr}s{par}"],
                    tag=SX + f"wl{pr}{oc}")
            else:
                wcast_done.add((pr, oc))
                dma("pool", wstr[pr][:, par],
                    wsrc[pr][:, :, oc * 128:oc * 128 + 128], [],
                    [f"w{pr}s{par}"], tag=SX + f"wl{pr}{oc}")
                dma("act", wscr_t[PRI[pr] * DC + oc], wstr[pr][:, par],
                    [f"w{pr}s{par}"], [f"wscr{pr}{oc}"],
                    tag=SX + f"wst{pr}{oc}")

        x_hbm = x_in.rearrange("(a p) d -> p a d", p=128)[:, NCH * s:NCH * (s + 1), :]
        dma("sp", x_sb[:], x_hbm, [], ["x_sb"], tag=SX + "xload")
        if s < pre:
            dma("sp", kvmask_sb[:], kvmask[:, s * SEG:(s + 1) * SEG],
                [], ["kvmask_sb"], tag=SX + "kvm")
        prs0 = [p_ for p_ in ("k", "v") if p_ in fe_parts]
        if "q" in fe_parts and (full or s == pre - 1):
            prs0.append("q")
        for p_ in prs0:
            wload(p_, 0)
            wload(p_, 1)

        for a in range(NCH):
            xt = x_sb[:, a, :]
            dve(lambda e, o=stat[:, 0:6], i=xt[:, 0:512]: e.bn_stats(o, i),
                ["x_sb"], ["stat"])
            dve(lambda e, o=stat[:, 6:12], i=xt[:, 512:1024]: e.bn_stats(o, i),
                ["x_sb"], ["stat"])
            dve(lambda e, o=stat[:, 12:14], i=stat[:, 0:12]: e.bn_aggr(o, i),
                ["stat"], ["stat"])
            act(stat[:, 14:15], stat[:, 13:14], AF.Sqrt,
                ["stat", "cpool"], ["stat"], bias=cpool[:, 0:1])
            dve(lambda e, o=stat[:, 15:16], i=stat[:, 14:15]:
                    e.reciprocal(o, i), ["stat"], ["stat"])
            ts(ln_tm[:], xt, stat[:, 12:13], ALU.subtract, stat[:, 15:16],
               ALU.mult, ["x_sb", "stat"], ["ln_tm"], tag=SX + f"ln{a}")
            for dc in range(DC):
                tpose(ps_tb_t[:, dc * 128:dc * 128 + 128],
                      ln_tm[:, dc * 128:dc * 128 + 128], identb[:],
                      ["ln_tm", "identb"], ktb(dc * 128, dc * 128 + 128))
            ts(ln1t[:, :, a * 128:a * 128 + 128],
               ps_tb_t[:, 0:1024].rearrange("p (c t) -> p c t", t=128),
               1.0, ALU.mult, None, ALU.bypass,
               ktb(0, 1024), ["ln1t"], tag=SX + f"lnT{a}")
            yield

        for dc in (range(DC) if "beta" in fe_parts else ()):
            mm(ps_e_t[0:H, 0:SEG], wbeta_sb[:, dc, :], ln1t[:, dc, :],
               ["wbeta_sb", "ln1t"], KE_, start=(dc == 0), stop=(dc == DC - 1))
        if "beta" not in fe_parts:
            pass
        else:
          act(beta_fm[:], ps_e_t[0:H, 0:SEG], AF.Sigmoid,
            KE_ + ["bbeta_sb"], ["beta_fm"], bias=bbeta_sb[:])
          for c in range(NCH):
            tpose(ps_f_t[:, c * 128:c * 128 + 16],
                  beta_fm[:, c * 128:c * 128 + 128], identf16[:],
                  ["beta_fm", "identf16"], KF_)
          ts(betat[:, sp1],
             ps_f_t[:, 0:512].rearrange("p (c h) -> p c h", h=128)[:, :, 0:H],
             1.0, ALU.mult, None, ALU.bypass, KF_, [f"betat{sp1}"],
             tag=SX + "betat")
          yield

        if pre and s == pre:
            for pr in "qkv":
                ts(tail_sb[pr][:].rearrange("p c t -> p (c t)"),
                   tail_sb[pr][:].rearrange("p c t -> p (c t)"),
                   hmask_sb[:], ALU.mult, None, ALU.bypass,
                   [f"tail_{pr}", "hmask_sb"], [f"tail_{pr}"], tag=SX + pr + "hm")

        FE_B = {0: (ps_f_t, KF_), 1: (ps_e_t, KE_)}

        def proj_emit(pr, oc):
            par = oc % 2
            wkey = f"w{pr}s{par}"
            bkey = BMAP[pr]
            pb_, ppk = FE_B[par]
            pp = pb_[:, 0:512]
            for dc in range(DC):
                mm(pp, wstr[pr][:, par, dc, :],
                   ln1t[:, dc, :], [wkey, "ln1t"], ppk,
                   start=(dc == 0), stop=(dc == DC - 1))
            if oc + 2 < DC:
                wload(pr, oc + 2)
            sc = pre_sc[par]
            sck = f"pre_sc{par}"
            if not conv_on or oc >= conv_ocs:
                ts(hat[pr][:, sp1, oc, :], pp,
                   bias_sb[bkey][:, oc:oc + 1],
                   ALU.add, None, ALU.bypass, ppk + [bkey + "_sb"],
                   [f"{pr}_hat{sp1}"], tag=SX + f"{pr}nc{oc}")
                return
            if "halo" not in cv_omit:
                ts(sc[:, 0:3], tail_sb[pr][:, oc, :], 1.0, ALU.mult, None,
                   ALU.bypass, [f"tail_{pr}"], [sck])
            if "evac" not in cv_omit:
                ts(sc[:, 3:3 + SEG], pp, bias_sb[bkey][:, oc:oc + 1],
                   ALU.add, None, ALU.bypass, ppk + [bkey + "_sb"], [sck],
                   tag=SX + f"{pr}pe{oc}")
            if "diag" not in cv_omit:
                dg = diag_sb[par]
                dgk = f"diag{par}"
                for k in range(KC):
                    ts(dg[:, k, :], identb[:], wconv_sb[pr][:, oc, k:k + 1],
                       ALU.mult, None, ALU.bypass,
                       ["identb", f"w{pr}c_sb"], [dgk])

        def conv_emit(pr, oc):
            if not conv_on or oc >= conv_ocs:
                return
            par = oc % 2
            sc = pre_sc[par]
            sck = f"pre_sc{par}"
            dg = diag_sb[par]
            dgk = f"diag{par}"
            pb_, cpk = FE_B[par]
            cp = pb_[:, 0:512]
            if "mm" not in cv_omit:
                for k in range(KC):
                    mm(cp, dg[:, k, :], sc[:, k:k + SEG],
                       [dgk, sck], cpk, start=(k == 0),
                       stop=(k == KC - 1))
            if "tail" not in cv_omit:
                ts(tail_sb[pr][:, oc, :], sc[:, SEG:SEG + 3], 1.0,
                   ALU.mult, None, ALU.bypass, [sck], [f"tail_{pr}"])
            if pr == "v":
                act(hat["v"][:, sp1, oc, :], cp, AF.Silu,
                    cpk + ["bvc_sb"], [f"v_hat{sp1}"],
                    bias=bconv_sb["v"][:, oc:oc + 1])
            else:
                sq = sq_sc[par]
                sqk = f"sq_sc{par}"
                if "sq" not in cv_omit:
                    act(sq[:], cp, AF.Square, cpk + [f"b{pr}c_sb"], [sqk],
                        bias=bconv_sb[pr][:, oc:oc + 1])
                if "raw" not in cv_omit:
                    ts(hat[pr][:, sp1, oc, :], cp,
                       bconv_sb[pr][:, oc:oc + 1],
                       ALU.add, None, ALU.bypass, cpk + [f"b{pr}c_sb"],
                       [f"{pr}_hat{sp1}"])

        def nm_emit(pr, oc):
            if (not conv_on or oc >= conv_ocs or pr == "v" or not norm_on
                    or "sq" in cv_omit):
                return
            par = oc % 2
            sq = sq_sc[par]
            sqk = f"sq_sc{par}"
            pb_, npk = FE_B[par]
            mm(pb_[0:H, 0:SEG], selnorm_b[:, oc, :], sq[:],
               ["selnorm_b", sqk], npk, tag=SX + f"nm{pr}{oc}")
            if oc == 0:
                acopy(rn_sc[:], pb_[0:H, 0:SEG], npk, ["rn_sc"],
                      tag=SX + f"na{pr}{oc}")
            else:
                tt(rn_sc[:], rn_sc[:], pb_[0:H, 0:SEG], ALU.add,
                   ["rn_sc"] + npk, ["rn_sc"], tag=SX + f"na{pr}{oc}")

        def proj_conv(pr):
            for ocp in range(0, DC, 2):
                proj_emit(pr, ocp)
                proj_emit(pr, ocp + 1)
                yield
                conv_emit(pr, ocp)
                conv_emit(pr, ocp + 1)
                nm_emit(pr, ocp)
                nm_emit(pr, ocp + 1)
                yield

        def finish_norm(pr, off):
            act(rn_sc[:], rn_sc[:], AF.Sqrt, ["rn_sc"], ["rn_sc"])
            ts(rn_sc[:], rn_sc[:], EPS, ALU.add,
               None, ALU.bypass, ["rn_sc"], ["rn_sc"])
            dve(lambda e, o=rn_sc[:], i=rn_sc[:]:
                    e.reciprocal(o, i), ["rn_sc"], ["rn_sc"])
            ts(rnorm[:, off:off + SEG], rn_sc[:], 1.0, ALU.mult,
               None, ALU.bypass, ["rn_sc"], ["rnorm"])
            if pr == "k" and s < pre:
                tt(rnorm[:, off:off + SEG], rnorm[:, off:off + SEG],
                   kvmask_sb[:], ALU.mult,
                   ["rnorm", "kvmask_sb"], ["rnorm"], tag=SX + "kmask")

        if "k" in fe_parts:
            yield from proj_conv("k")
            if conv_on and norm_on:
                finish_norm("k", 0)
        if "v" in fe_parts:
            yield from proj_conv("v")
        for oc in (range(DC) if "v" in fe_parts else ()):
            pe_, pek = FE_B[oc % 2]
            pe_ = pe_[:, 0:512]
            mm(pe_, selexp_b[:, oc, :], rnorm[0:H, 0:SEG],
               ["selexp_b", "rnorm"], pek)
            tt(hat["k"][:, sp1, oc, :], hat["k"][:, sp1, oc, :], pe_,
               ALU.mult, [f"k_hat{sp1}"] + pek, [f"k_hat{sp1}"],
               tag=SX + f"kap{oc}")
            tt(hat["v"][:, sp1, oc, :], hat["v"][:, sp1, oc, :], pe_,
               ALU.mult, [f"v_hat{sp1}"] + pek, [f"v_hat{sp1}"],
               tag=SX + f"vap{oc}")
            if oc % 2 == 1:
                yield
        if full and "q" in fe_parts:
            yield from proj_conv("q")
            finish_norm("q", SEG)
            for oc in range(DC):
                pe_, pek = FE_B[oc % 2]
                pe_ = pe_[:, 0:512]
                mm(pe_, selexp_b[:, oc, :], rnorm[0:H, SEG:2 * SEG],
                   ["selexp_b", "rnorm"], pek)
                tt(hat["q"][:, sp1, oc, :], hat["q"][:, sp1, oc, :], pe_,
                   ALU.mult, [f"q_hat{sp1}"] + pek, [f"q_hat{sp1}"],
                   tag=SX + f"qap{oc}")
                if oc % 2 == 1:
                    yield
        elif s == pre - 1 and "q" in fe_parts:
            for oc in range(DC):
                par = oc % 2
                for dc in range(DC):
                    mm(ps_f_t[:, 0:3], wstr["q"][:, par, dc, :],
                       ln1t[:, dc, SEG - 3:SEG], [f"wqs{par}", "ln1t"], KF_,
                       start=(dc == 0), stop=(dc == DC - 1))
                if oc + 2 < DC:
                    wload("q", oc + 2)
                ts(tail_sb["q"][:, oc, :], ps_f_t[:, 0:3],
                   bias_sb["bq"][:, oc:oc + 1], ALU.add, None, ALU.bypass,
                   KF_ + ["bq_sb"], ["tail_q"], tag=SX + f"qtail{oc}")
                yield

        if s == nseg - 1:
            for nm, key, buf, dt in (("ln1t", "ln1t", ln1t, BF16),
                                     ("khat", f"k_hat{sp1}", hat["k"][:, sp1],
                                      BF16),
                                     ("vhat", f"v_hat{sp1}", hat["v"][:, sp1],
                                      BF16),
                                     ("qhat", f"q_hat{sp1}", hat["q"][:, sp1],
                                      BF16),
                                     ("beta", "beta_fm", beta_fm, F32)):
                if nm in debug_taps:
                    bap = buf[:] if hasattr(buf, "ap") and not isinstance(buf, bass.AP) else buf
                    dma("sp", tap(nm, bap.shape, dt), bap, [key],
                        ["tap_" + nm])

    def dt_emit(s):
        full = (s >= pre) or full_prefix
        osg = s - pre
        sp1 = s % 2
        SX = f"s{s}:"

        def tail_emit(tb):
            t0 = tb * 128
            dma("act", x_res[:, tb % 2],
                x_in.rearrange("(a p) d -> p a d", p=128)[:, NCH * s + tb, :],
                [], [f"x_res{tb % 2}"], tag=SX + f"xres{tb}")
            pw = ps_a_t[:, 0:1024]
            pwk = ka(0, 1024)
            for dc in range(DC):
                for nh in range(2):
                    mm(pw[:, nh * 512:nh * 512 + 512],
                       ot_b[:, dc, t0:t0 + 128],
                       wo_sb[:, dc, nh * 512:nh * 512 + 512],
                       [f"ot_b{tb}", "wo_sb"], pwk,
                       start=(dc == 0), stop=False, tag=SX + f"wo{tb}_{dc}{nh}")
            for nh in range(2):
                mm(pw[:, nh * 512:nh * 512 + 512], ones1[:],
                   bo_sb[:, nh * 512:nh * 512 + 512],
                   ["ones1", "bo_sb"], pwk, start=False, stop=True,
                   tag=SX + f"wob{tb}{nh}")
            tt(x2_b[:], pw, x_res[:, tb % 2], ALU.add,
               pwk + [f"x_res{tb % 2}"], ["x2_b"], tag=SX + f"x2_{tb}")
            dma("sp", out_t[:, osg * NCH + tb, :], x2_b[:],
                ["x2_b"], [f"out:{osg}:{tb}"], tag=SX + f"outinit{tb}")
            yield
            dve(lambda e, o=stat[:, 16:22], i=x2_b[:, 0:512]: e.bn_stats(o, i),
                ["x2_b"], ["stat2"])
            dve(lambda e, o=stat[:, 22:28], i=x2_b[:, 512:1024]:
                    e.bn_stats(o, i), ["x2_b"], ["stat2"])
            dve(lambda e, o=stat[:, 28:30], i=stat[:, 16:28]: e.bn_aggr(o, i),
                ["stat2"], ["stat2"])
            act(stat[:, 30:31], stat[:, 29:30], AF.Sqrt,
                ["stat2", "cpool"], ["stat2"], bias=cpool[:, 0:1])
            dve(lambda e, o=stat[:, 31:32], i=stat[:, 30:31]:
                    e.reciprocal(o, i), ["stat2"], ["stat2"])
            ts(ln_tm2[:], x2_b[:], stat[:, 28:29], ALU.subtract,
               stat[:, 31:32], ALU.mult, ["x2_b", "stat2"], ["ln_tm2"],
               tag=SX + f"ln2_{tb}")
            for dc in range(DC):
                tpose(ps_tb_t[:, dc * 128:dc * 128 + 128],
                      ln_tm2[:, dc * 128:dc * 128 + 128], identb[:],
                      ["ln_tm2", "identb"], ktb(dc * 128, dc * 128 + 128))
            ts(ln2t[:, :, t0:t0 + 128],
               ps_tb_t[:, 0:1024].rearrange("p (c t) -> p c t", t=128),
               1.0, ALU.mult, None, ALU.bypass, ktb(0, 1024), ["ln2t"],
               tag=SX + f"ln2T{tb}")
            yield
            gp = ps_b_t[:, 0:NE]
            KG_ = kb(0, 512)
            for dc in range(DC):
                mm(gp, ln2t[:, dc, t0:t0 + 128], wg_sb[:, dc, :],
                   ["ln2t", "wg_sb"], KG_, start=(dc == 0), stop=(dc == DC - 1))
            gx = osg * NCH + tb
            if sc_tap is not None:
                ts(sc_tap[:, tb * NE:(tb + 1) * NE], gp, 1.0, ALU.mult, None,
                   ALU.bypass, KG_, ["sc_tap"])
            dve(lambda e, o=gt_sc[:, 8:9], i=gp: e.reduce_max(o, i, axis=AX.X),
                KG_, ["gt"])
            ts(gt_sc[:, 0:8], gp, gt_sc[:, 8:9], ALU.is_equal, None,
               ALU.bypass, KG_ + ["gt"], ["gt"])
            stt(gt_sc[:, 16:24], gt_sc[:, 0:8], -1e30, gp, ALU.mult, ALU.add,
                ["gt"] + KG_, ["gt"])
            dve(lambda e, o=gt_sc[:, 9:10], i=gt_sc[:, 16:24]:
                    e.reduce_max(o, i, axis=AX.X), ["gt"], ["gt"])
            ts(gt_sc[:, 24:32], gt_sc[:, 16:24], gt_sc[:, 9:10], ALU.is_equal,
               None, ALU.bypass, ["gt"], ["gt"])
            tt(gt_sc[:, 10:11], gt_sc[:, 9:10], gt_sc[:, 8:9], ALU.subtract,
               ["gt"], ["gt"])
            act(gt_sc[:, 11:12], gt_sc[:, 10:11], AF.Sigmoid, ["gt"], ["gt"])
            ts(gt_sc[:, 12:13], gt_sc[:, 11:12], -1.0, ALU.mult, 1.0, ALU.add,
               ["gt"], ["gt"])
            ts(gt_sc[:, 24:32], gt_sc[:, 24:32], gt_sc[:, 11:12], ALU.mult,
               None, ALU.bypass, ["gt"], ["gt"])
            stt(g_sb[:, gx, :], gt_sc[:, 0:8], gt_sc[:, 12:13],
                gt_sc[:, 24:32], ALU.mult, ALU.add, ["gt"], ["g_sb"],
                tag=SX + f"g{tb}")
            yield
        for c in (range(NCH) if do_delta else ()):
            c0, c1 = c * CH, (c + 1) * CH
            CX = SX + f"c{c}:"

            for h in HORDER:
                pt, kf = pspar[h % 2]
                sl = h // 2
                kh = head_slice(hat["k"], sp1, h, c0, c1)
                mm(pt[:, sl * 128:sl * 128 + 128], kh, kh,
                   [f"k_hat{sp1}"], kf(sl * 128, sl * 128 + 128),
                   tag=CX + f"kk{h}")
            for h in HORDER:
                pt, kf = pspar[h % 2]
                sl = h // 2
                stt(A_b[:, hp(h) * 128:hp(h) * 128 + 128],
                    pt[:, sl * 128:sl * 128 + 128],
                    betat[:, sp1, c, h:h + 1], trils_b[:], ALU.mult, ALU.mult,
                    kf(sl * 128, sl * 128 + 128) + [f"betat{sp1}", "trils_b"],
                    ["A_b"], tag=CX + f"A{h}")
            yield
            for r in range(H):
                tpose(ps_tb_t[:, r * 128:r * 128 + 128],
                      A_b[:, r * 128:r * 128 + 128], identb[:],
                      ["A_b", "identb"], ktb(r * 128, r * 128 + 128))
            acopy(Bp[0][:, 0:1024], ps_tb_t[:, 0:1024], ktb(0, 1024),
                  ["Bp0"], tag=CX + "B0ev0")
            ts(Bp[0][:, 1024:2048], ps_tb_t[:, 1024:2048], 1.0, ALU.mult,
               None, ALU.bypass, ktb(1024, 2048), ["Bp0"], tag=CX + "B0ev1")
            yield
            for h in range(H):
                kh = head_slice(hat["k"], sp1, h, c0, c1)
                vh = head_slice(hat["v"], sp1, h, c0, c1)
                par = h % 2
                idt = ident64b[64 * par:64 * par + 64, :]
                r = hp(h)
                tpose(ps_tb_t[:, r * 128:r * 128 + 64], kh, idt,
                      [f"k_hat{sp1}", "ident64b"], ktb(r * 128, r * 128 + 64))
                tpose(ps_tb_t[:, r * 128 + 64:r * 128 + 128], vh, idt,
                      [f"v_hat{sp1}", "ident64b"],
                      ktb(r * 128 + 64, r * 128 + 128))
            for h in range(H):
                r = hp(h)
                ts(t_b[0][:, r * 128:r * 128 + 128],
                   ps_tb_t[:, r * 128:r * 128 + 128],
                   betat[:, sp1, c, h:h + 1],
                   ALU.mult, None, ALU.bypass,
                   ktb(r * 128, r * 128 + 128) + [f"betat{sp1}"], ["t0"],
                   tag=CX + f"t0{h}")
            acopy(kn_b[:].rearrange("p (r d) -> p r d", d=HD),
                  ps_tb_t[:].rearrange("p (r two d) -> p r two d",
                                       two=2, d=HD)[:, :, 0, :],
                  ktb(0, 2048), ["kn_b"], tag=CX + "knev")

            yield
            for h in range(H):
                r = hp(h)
                sl_ = slice(r * 128, r * 128 + 128)
                tt(Cp[0][:, sl_], A_b[:, sl_], md16_b[:], ALU.mult,
                   ["A_b", "md16_b"], ["Cp0"], tag=CX + f"d0_{h}", eng="pool")
                tt(Cp[1][:, sl_], Bp[0][:, sl_], md16t_b[:], ALU.mult,
                   ["Bp0", "md16t_b"], ["Cp1"], tag=CX + f"d0t_{h}", eng="pool")
                tt(M_b[:, sl_], identb[:], Cp[0][:, sl_], ALU.subtract,
                   ["identb", "Cp0"], ["M_b"], tag=CX + f"mi_{h}", eng="pool")
                tt(MT_b[:, sl_], identb[:], Cp[1][:, sl_], ALU.subtract,
                   ["identb", "Cp1"], ["MT_b"], tag=CX + f"mti_{h}", eng="pool")

            yield

            def sq_pair(srck, srctk, dstbuf, dstk, dsttbuf, dsttk, tg):
                srcb = {"Cp0": Cp[0], "Cp1": Cp[1], "Bp1": Bp[1],
                        "t1": t_b[1]}[srck]
                srctb = {"Cp0": Cp[0], "Cp1": Cp[1], "Bp1": Bp[1],
                         "t1": t_b[1]}[srctk]
                for h in HORDER:
                    pt, kf = pspar[h % 2]
                    sl = h // 2
                    r = hp(h)
                    mm(pt[:, sl * 128:sl * 128 + 128],
                       srctb[:, r * 128:r * 128 + 128],
                       srcb[:, r * 128:r * 128 + 128],
                       [srck, srctk], kf(sl * 128, sl * 128 + 128),
                       tag=CX + tg + f"s{h}")
                acopy(dstbuf[:, 0:1024], pspar[0][0][:, 0:1024],
                      pspar[0][1](0, 1024), [dstk], tag=CX + tg + "e0")
                ts(dstbuf[:, 1024:2048], pspar[1][0][:, 0:1024], 1.0,
                   ALU.mult, None, ALU.bypass, pspar[1][1](0, 1024), [dstk],
                   tag=CX + tg + "e1")
                yield
                for h in HORDER:
                    pt, kf = pspar[h % 2]
                    sl = h // 2
                    r = hp(h)
                    mm(pt[:, sl * 128:sl * 128 + 128],
                       srcb[:, r * 128:r * 128 + 128],
                       srctb[:, r * 128:r * 128 + 128],
                       [srck, srctk], kf(sl * 128, sl * 128 + 128),
                       tag=CX + tg + f"st{h}")
                acopy(dsttbuf[:, 0:1024], pspar[0][0][:, 0:1024],
                      pspar[0][1](0, 1024), [dsttk], tag=CX + tg + "et0")
                ts(dsttbuf[:, 1024:2048], pspar[1][0][:, 0:1024], 1.0,
                   ALU.mult, None, ALU.bypass, pspar[1][1](0, 1024), [dsttk],
                   tag=CX + tg + "et1")
                yield

            def leaf_prod(pk_, ptk_, tg):
                pb_ = {"Cp0": Cp[0], "Cp1": Cp[1], "Bp1": Bp[1],
                       "t1": t_b[1]}[pk_]
                ptb_ = {"Cp0": Cp[0], "Cp1": Cp[1], "Bp1": Bp[1],
                        "t1": t_b[1]}[ptk_]
                for h in HORDER:
                    pt, kf = pspar[h % 2]
                    sl = h // 2
                    r = hp(h)
                    mm(pt[:, sl * 128:sl * 128 + 128],
                       MT_b[:, r * 128:r * 128 + 128],
                       pb_[:, r * 128:r * 128 + 128],
                       ["MT_b", pk_], kf(sl * 128, sl * 128 + 128),
                       tag=CX + tg + f"p{h}")
                for hh in range(2):
                    pt, kf = pspar[hh]
                    tt(M_b[:, hh * 1024:hh * 1024 + 1024],
                       M_b[:, hh * 1024:hh * 1024 + 1024], pt[:, 0:1024],
                       ALU.add, ["M_b"] + kf(0, 1024), ["M_b"],
                       tag=CX + tg + f"pe{hh}")
                yield
                for h in HORDER:
                    pt, kf = pspar[h % 2]
                    sl = h // 2
                    r = hp(h)
                    mm(pt[:, sl * 128:sl * 128 + 128],
                       pb_[:, r * 128:r * 128 + 128],
                       MT_b[:, r * 128:r * 128 + 128],
                       ["MT_b", pk_], kf(sl * 128, sl * 128 + 128),
                       tag=CX + tg + f"q{h}")
                for hh in range(2):
                    pt, kf = pspar[hh]
                    tt(MT_b[:, hh * 1024:hh * 1024 + 1024],
                       MT_b[:, hh * 1024:hh * 1024 + 1024], pt[:, 0:1024],
                       ALU.add, ["MT_b"] + kf(0, 1024), ["MT_b"],
                       tag=CX + tg + f"qe{hh}")
                yield

            yield from sq_pair("Cp0", "Cp1", Bp[1], "Bp1", t_b[1], "t1",
                               "p2")
            yield from leaf_prod("Bp1", "t1", "l2")
            yield from sq_pair("Bp1", "t1", Cp[0], "Cp0", Cp[1], "Cp1", "p4")
            yield from leaf_prod("Cp0", "Cp1", "l4")
            yield from sq_pair("Cp0", "Cp1", Bp[1], "Bp1", t_b[1], "t1", "p8")
            yield from leaf_prod("Bp1", "t1", "l8")

            for lvl, mask_b, maskk in ((16, mo16_b, "mo16_b"),
                                       (32, mo32_b, "mo32_b")):
                for h in range(H):
                    r = hp(h)
                    sl_ = slice(r * 128, r * 128 + 128)
                    tt(Cp[0][:, sl_], A_b[:, sl_], mask_b[:], ALU.mult,
                       ["A_b", maskk], ["Cp0"], tag=CX + f"E{lvl}_{h}",
                       eng="pool")
                for h in HORDER:
                    pt, kf = pspar[h % 2]
                    sl = h // 2
                    r = hp(h)
                    mm(pt[:, sl * 128:sl * 128 + 128],
                       Cp[0][:, r * 128:r * 128 + 128],
                       MT_b[:, r * 128:r * 128 + 128],
                       ["Cp0", "MT_b"], kf(sl * 128, sl * 128 + 128),
                       tag=CX + f"xt{lvl}_{h}")
                acopy(Cp[1][:, 0:1024], pspar[0][0][:, 0:1024],
                      pspar[0][1](0, 1024), ["Cp1"], tag=CX + f"xte{lvl}0")
                ts(Cp[1][:, 1024:2048], pspar[1][0][:, 0:1024], 1.0, ALU.mult,
                   None, ALU.bypass, pspar[1][1](0, 1024), ["Cp1"],
                   tag=CX + f"xte{lvl}1")
                yield
                for h in HORDER:
                    pt, kf = pspar[h % 2]
                    sl = h // 2
                    r = hp(h)
                    mm(pt[:, sl * 128:sl * 128 + 128],
                       M_b[:, r * 128:r * 128 + 128],
                       Cp[1][:, r * 128:r * 128 + 128],
                       ["M_b", "Cp1"], kf(sl * 128, sl * 128 + 128),
                       tag=CX + f"mt{lvl}_{h}")
                for hh in range(2):
                    pt, kf = pspar[hh]
                    tt(MT_b[:, hh * 1024:hh * 1024 + 1024],
                       MT_b[:, hh * 1024:hh * 1024 + 1024], pt[:, 0:1024],
                       ALU.subtract, ["MT_b"] + kf(0, 1024), ["MT_b"],
                       tag=CX + f"mte{lvl}{hh}")
                yield
                for h in HORDER:
                    pt, kf = pspar[h % 2]
                    sl = h // 2
                    r = hp(h)
                    mm(pt[:, sl * 128:sl * 128 + 128],
                       Cp[1][:, r * 128:r * 128 + 128],
                       M_b[:, r * 128:r * 128 + 128],
                       ["Cp1", "M_b"], kf(sl * 128, sl * 128 + 128),
                       tag=CX + f"m{lvl}_{h}")
                for hh in range(2):
                    pt, kf = pspar[hh]
                    tt(M_b[:, hh * 1024:hh * 1024 + 1024],
                       M_b[:, hh * 1024:hh * 1024 + 1024], pt[:, 0:1024],
                       ALU.subtract, ["M_b"] + kf(0, 1024), ["M_b"],
                       tag=CX + f"me{lvl}{hh}")
                yield

            for h in HORDER:
                pt, kf = pspar[h % 2]
                sl = h // 2
                r = hp(h)
                mm(pt[:, sl * 128:sl * 128 + 128],
                   MT_b[:, r * 128:r * 128 + 128],
                   t_b[0][:, r * 128:r * 128 + 128],
                   ["MT_b", "t0"], kf(sl * 128, sl * 128 + 128),
                   tag=CX + f"y_{h}")
            acopy(Bp[1][:, 0:1024], pspar[0][0][:, 0:1024],
                  pspar[0][1](0, 1024), ["Bp1"], tag=CX + "ye0")
            ts(Bp[1][:, 1024:2048], pspar[1][0][:, 0:1024], 1.0, ALU.mult,
               None, ALU.bypass, pspar[1][1](0, 1024), ["Bp1"], tag=CX + "ye1")
            yield
            for h in HORDER:
                pt, kf = pspar[h % 2]
                sl = h // 2
                r = hp(h)
                mm(pt[64:128, sl * 128:sl * 128 + 128],
                   Bp[0][0:64, r * 128 + 64:r * 128 + 128],
                   Bp[1][0:64, r * 128:r * 128 + 128],
                   ["Bp0", "Bp1"], kf(sl * 128, sl * 128 + 128),
                   tag=CX + f"z_{h}")
            acopy(Cp[0][64:128, 0:1024], pspar[0][0][64:128, 0:1024],
                  pspar[0][1](0, 1024), ["Cp0"], tag=CX + "ze0")
            ts(Cp[0][64:128, 1024:2048], pspar[1][0][64:128, 0:1024], 1.0,
               ALU.mult, None, ALU.bypass, pspar[1][1](0, 1024), ["Cp0"],
               tag=CX + "ze1")
            yield
            for h in HORDER:
                pt, kf = pspar[h % 2]
                sl = h // 2
                r = hp(h)
                mm(pt[64:128, sl * 128:sl * 128 + 128],
                   MT_b[64:128, r * 128 + 64:r * 128 + 128],
                   Cp[0][64:128, r * 128:r * 128 + 128],
                   ["MT_b", "Cp0"], kf(sl * 128, sl * 128 + 128),
                   tag=CX + f"c_{h}")
            acopy(t_b[1][:, 0:1024], Bp[1][:, 0:1024], ["Bp1"], ["t1"],
                  tag=CX + "ucp0")
            ts(t_b[1][:, 1024:2048], Bp[1][:, 1024:2048], 1.0, ALU.mult,
               None, ALU.bypass, ["Bp1"], ["t1"], tag=CX + "ucp1")
            for hh in range(2):
                pt, kf = pspar[hh]
                tt(t_b[1][64:128, hh * 1024:hh * 1024 + 1024],
                   t_b[1][64:128, hh * 1024:hh * 1024 + 1024],
                   pt[64:128, 0:1024], ALU.subtract,
                   ["t1"] + kf(0, 1024), ["t1"], tag=CX + f"usub{hh}")
            tfin = t_b[1]
            tkey = "t1"

            yield
            for h in range(H):
                par, sl = h % 2, h // 2
                r = hp(h)
                tpose(ps_tb_t[64 * par:64 * par + 64,
                              sl * 128:sl * 128 + 128],
                      tfin[:, r * 128:r * 128 + 64], identb[:],
                      [tkey, "identb"], ktb(sl * 128, sl * 128 + 128))
            acopy(wt_b[:], ps_tb_t[:, 0:1024], ktb(0, 1024), ["wt_b"],
                  tag=CX + "wtev")
            for h in range(H):
                par, sl = h % 2, h // 2
                r = hp(h)
                mm(ps_a_t[:, r * 64:r * 64 + 64],
                   wt_b[64 * par:64 * par + 64, sl * 128:sl * 128 + 128],
                   s_sb[64 * par:64 * par + 64, sl * 64:sl * 64 + 64],
                   ["wt_b", "s_sb"], ka(r * 64, r * 64 + 64),
                   start=(r in (0, 8)), tag=CX + f"ws{h}")
            tt(u_b[:].rearrange("p (r d) -> p r d", d=HD),
               tfin[:].rearrange("p (r two d) -> p r two d",
                                 two=2, d=HD)[:, :, 1, :],
               ps_a_t[:, 0:1024].rearrange("p (r d) -> p r d", d=HD),
               ALU.subtract, [tkey] + ka(0, 1024), ["u_b"], tag=CX + "uev")

            yield
            if full:
                for h in HORDER:
                    pt, kf = pspar[h % 2]
                    sl = h // 2
                    kh = head_slice(hat["k"], sp1, h, c0, c1)
                    qh = head_slice(hat["q"], sp1, h, c0, c1)
                    mm(pt[:, sl * 128:sl * 128 + 128], kh, qh,
                       [f"k_hat{sp1}", f"q_hat{sp1}"],
                       kf(sl * 128, sl * 128 + 128), tag=CX + f"kq{h}")
                for h in HORDER:
                    pt, kf = pspar[h % 2]
                    sl = h // 2
                    r = hp(h)
                    tt(Cp[1][:, r * 128:r * 128 + 128],
                       pt[:, sl * 128:sl * 128 + 128], triui_b[:], ALU.mult,
                       kf(sl * 128, sl * 128 + 128) + ["triui_b"], ["Cp1"],
                       tag=CX + f"mkq{h}")
                yield
                for h in range(H):
                    par, sl = h % 2, h // 2
                    r = hp(h)
                    qh = head_slice(hat["q"], sp1, h, c0, c1)
                    mm(ps_b_t[:, r * 64:r * 64 + 64], qh,
                       s_sb[64 * par:64 * par + 64, sl * 64:sl * 64 + 64],
                       [f"q_hat{sp1}", "s_sb"], kb(r * 64, r * 64 + 64),
                       start=(r in (0, 8)), stop=False, tag=CX + f"qs{h}")
                for h in range(H):
                    r = hp(h)
                    mm(ps_b_t[:, r * 64:r * 64 + 64],
                       Cp[1][:, r * 128:r * 128 + 128],
                       u_b[:, r * 64:r * 64 + 64],
                       ["Cp1", "u_b"], kb(r * 64, r * 64 + 64),
                       start=False, stop=True, tag=CX + f"oi{h}")

            yield
            for h in range(H):
                par, sl = h % 2, h // 2
                r = hp(h)
                mm(ps_a_t[64 * par:64 * par + 64, sl * 64:sl * 64 + 64],
                   kn_b[:, r * 64:r * 64 + 64], u_b[:, r * 64:r * 64 + 64],
                   ["kn_b", "u_b"], ka(0, 512), tag=CX + f"chain{h}")
            tt(s_f32[:], s_f32[:], ps_a_t[:, 0:512], ALU.add,
               ["s_f32"] + ka(0, 512), ["s_f32"], tag=CX + "sadd")
            acopy(s_sb[:], s_f32[:], ["s_f32"], ["s_sb"], tag=CX + "sev")
            yield

            if full:
                act(ln_tm2[:], ps_b_t[:, 0:1024], AF.Square,
                    kb(0, 1024), ["ln_tm2"])
                dve(lambda e, o=rms_b[:, 0:H],
                        i=ln_tm2[:].rearrange("p (r d) -> p r d", d=HD):
                        e.reduce_sum(o, i, axis=AX.X),
                    ["ln_tm2"], ["rms"], tag=CX + "ssq")
                act(rms_b[:, 16:16 + H], rms_b[:, 0:H], AF.Sqrt,
                    ["rms", "cpool"], ["rms"], scale=1.0 / HD,
                    bias=cpool[:, 1:2])
                dve(lambda e, o=rms_b[:, 0:H], i=rms_b[:, 16:16 + H]:
                        e.reciprocal(o, i), ["rms"], ["rms"])
                ts(rmsb16[:], rms_b[:, 0:H], 1.0, ALU.mult, None, ALU.bypass,
                   ["rms"], ["rmsb16"])
                tpose(ps_tb_t[0:H, 0:128], rmsb16[:], identb[:],
                      ["rmsb16", "identb"], ktb(0, 128))
                ts(rstdt_b[:], ps_tb_t[0:H, 0:128], 1.0, ALU.mult, None,
                   ALU.bypass, ktb(0, 128), ["rstdt_b"], tag=CX + "rstdt")
                yield
                for nh in range(2):
                    mm(ps_a_t[:, nh * 512:nh * 512 + 512], rstdt_b[:],
                       nwsel_b[:, nh * 512:nh * 512 + 512],
                       ["rstdt_b", "nwsel_b"], ka(nh * 512, nh * 512 + 512),
                       tag=CX + f"nwex{nh}")
                ts(nwex_b[:], ps_a_t[:, 0:1024], 1.0, ALU.mult, None,
                   ALU.bypass, ka(0, 1024), ["nwex_b"], tag=CX + "nwev")
                tt(onat_b[:], ps_b_t[:, 0:1024], nwex_b[:], ALU.mult,
                   kb(0, 1024) + ["nwex_b"], ["onat_b"], tag=CX + "onat")
                for h in range(H):
                    par, sl = h % 2, h // 2
                    r = hp(h)
                    tpose(ps_tb_t[64 * par:64 * par + 64,
                                  sl * 128:sl * 128 + 128],
                          onat_b[:, r * 64:r * 64 + 64], identb[:],
                          ["onat_b", "identb"], ktb(sl * 128, sl * 128 + 128))
                ts(ot_b[:, :, c0:c1],
                   ps_tb_t[:, 0:1024].rearrange("p (c t) -> p c t", t=128),
                   1.0, ALU.mult, None, ALU.bypass, ktb(0, 1024),
                   [f"ot_b{c}"], tag=CX + "otev")
                yield
            if full and osg >= 0 and do_wo:
                yield from tail_emit(c)

        if not full or osg < 0:
            return
        if not do_wo:
            dma("sp", out_t[:, osg * NCH:(osg + 1) * NCH, :], x_sb[:],
                ["x_sb"], [f"out:{osg}:0"], tag=SX + "outpass")
            return

        if ffn:
            scr = ln2t_scr.rearrange("p (c t) -> p c t", t=owntok)
            dma("sp", scr[:, :, osg * SEG:(osg + 1) * SEG], ln2t[:],
                ["ln2t"], [f"scr:{osg}"], tag=SX + "scrst")
        if s == nseg - 1:
            for nm, key, buf, dt in (
                                     ("ot", "ot_b3", ot_b, BF16),
                                     ("u", "u_b", u_b, BF16),
                                     ("ssb", "s_sb", s_sb, BF16),
                                     ("g", "g_sb", g_sb, F32),
                                     ("scores", "sc_tap", sc_tap, F32),
                                     ("ln2t", "ln2t", ln2t, BF16),
                                     ("stat", "stat", stat[:, 0:16], F32)):
                if nm in debug_taps:
                    dma("sp", tap(nm, buf.shape, dt), buf[:], [key],
                        ["tap_" + nm])

    FE_EVERY = 2
    _done = object()
    for _ in fe_emit(0):
        pass
    for s in range(nseg):
        dg = dt_emit(s)
        fg = fe_emit(s + 1) if s + 1 < nseg else iter(())
        nq = 0
        while True:
            d = next(dg, _done)
            nq += 1
            if nq % FE_EVERY == 0:
                next(fg, None)
            if d is _done:
                for _ in fg:
                    pass
                break

    esm.close()

    if ffn:
        p.barrier()
        esf = contextlib.ExitStack()
        def fsb(name, shape, dt):
            return sbuf_in(esf, name, shape, dt)
        w1_sb = fsb("w1_sb", (128, DC, FF), BF16)
        w2_sb = fsb("w2_sb", (128, DC, FF), BF16)
        w3_sb = fsb("w3_sb", (128, FC, D), BF16)
        l2_sb = [fsb(f"l2_sb{i}", (128, DC, TFF), BF16) for i in range(2)]
        hid_sb = fsb("hid_sb", (128, FC, TFF), BF16)
        sil_sc = [fsb(f"sil_sc{i}", (128, 512), BF16) for i in range(2)]
        og_sb = [fsb(f"og_sb{i}", (128, 2, D), F32) for i in range(2)]
        scr = ln2t_scr.rearrange("p (c t) -> p c t", t=owntok)

        NSL = 4
        for e_i in range(n_experts):
            EX = f"e{e_i}:"
            w1r = w1t[e_i].rearrange("(c p) f -> p c f", p=128)
            w2r = w2t[e_i].rearrange("(c p) f -> p c f", p=128)
            w3r = w3t[e_i].rearrange("(c p) f -> p c f", p=128)
            FSL = FF // NSL
            for sl_ in range(NSL):
                dma("pool", w1_sb[:, :, sl_ * FSL:(sl_ + 1) * FSL],
                    w1r[:, :, sl_ * FSL:(sl_ + 1) * FSL], [],
                    [f"w1_sb{sl_}"], tag=EX + f"w1_{sl_}")
                dma("pool", w2_sb[:, :, sl_ * FSL:(sl_ + 1) * FSL],
                    w2r[:, :, sl_ * FSL:(sl_ + 1) * FSL], [],
                    [f"w2_sb{sl_}"], tag=EX + f"w2_{sl_}")
            CSL = FC // NSL
            for sl_ in range(NSL):
                dma("pool", w3_sb[:, sl_ * CSL:(sl_ + 1) * CSL, :],
                    w3r[:, sl_ * CSL:(sl_ + 1) * CSL, :], [],
                    [f"w3_sb{sl_}"], tag=EX + f"w3_{sl_}")
            for t in range(nft):
                TX = EX + f"t{t}:"
                l2 = l2_sb[t % 2]
                l2k = f"l2_{t % 2}"
                dma("sp", l2[:], scr[:, :, t * TFF:(t + 1) * TFF],
                    [f"scr:{2 * t}", f"scr:{2 * t + 1}"], [l2k], tag=TX + "l2")
                for fc in range(FC):
                    for th in range(2):
                        ph1 = ps_a_t[:, th * 512:th * 512 + 512]
                        ph1k = ka(th * 512, th * 512 + 512)
                        ph2 = ps_b_t[:, th * 512:th * 512 + 512]
                        ph2k = kb(th * 512, th * 512 + 512)
                        wsl = fc // (FC // NSL)
                        for dc in range(DC):
                            mm(ph1, w1_sb[:, dc, fc * 128:fc * 128 + 128],
                               l2[:, dc, th * 512:th * 512 + 512],
                               [f"w1_sb{wsl}", l2k], ph1k,
                               start=(dc == 0), stop=(dc == DC - 1))
                        for dc in range(DC):
                            mm(ph2, w2_sb[:, dc, fc * 128:fc * 128 + 128],
                               l2[:, dc, th * 512:th * 512 + 512],
                               [f"w2_sb{wsl}", l2k], ph2k,
                               start=(dc == 0), stop=(dc == DC - 1))
                        sil = sil_sc[th]
                        act(sil[:], ph1, AF.Silu, ph1k, [f"sil{th}"],
                            tag=TX + f"sil{fc}_{th}")
                        tt(hid_sb[:, fc, th * 512:th * 512 + 512], sil[:],
                           ph2, ALU.mult, [f"sil{th}"] + ph2k, ["hid"],
                           tag=TX + f"hid{fc}_{th}")
                for tb in range(TFF // 128):
                    po = ps_a_t[:, 0:1024] if tb % 2 == 0 else ps_b_t[:, 0:1024]
                    pok = ka(0, 1024) if tb % 2 == 0 else kb(0, 1024)
                    for fc in range(FC):
                        for nh in range(2):
                            mm(po[:, nh * 512:nh * 512 + 512],
                               hid_sb[:, fc, tb * 128:tb * 128 + 128],
                               w3_sb[:, fc, nh * 512:nh * 512 + 512],
                               ["hid", f"w3_sb{fc // (FC // NSL)}"], pok,
                               start=(fc == 0), stop=(fc == FC - 1),
                               tag=TX + f"w3_{tb}_{fc}{nh}")
                    gx = t * (TFF // 128) + tb
                    ob = og_sb[(tb // 2) % 2]
                    obk = f"og{(tb // 2) % 2}"
                    ts(ob[:, tb % 2, :], po, g_sb[:, gx, e_i:e_i + 1],
                       ALU.mult, None, ALU.bypass, pok + ["g_sb"], [obk],
                       tag=TX + f"ogev{tb}")
                    if tb % 2 == 1:
                        blk = (gx - 1) // 2
                        ok1 = f"out:{(2 * blk) // NCH}:{(2 * blk) % NCH}"
                        ok2 = f"out:{(2 * blk + 1) // NCH}:{(2 * blk + 1) % NCH}"
                        dma_acc("pool", out_t[:, 2 * blk:2 * blk + 2, :],
                                ob[:], [obk, ok1, ok2], [ok1, ok2],
                                tag=TX + f"acc{tb}")
        esf.close()

    allk = [k for k in p.last_writer if k.startswith(("out:", "tap_"))]
    p.emit("sp", lambda e: e.nop(), reads=allk, tag="final")

    p.finalize()
    p.emit_block(es)
    es.close()
    return nc, taps



def consts():
    c = {}
    c["ident"] = np.eye(128, dtype=np.float32)
    c["ident64"] = np.vstack([np.eye(64), np.eye(64)]).astype(np.float32)
    c["trils"] = np.tril(np.ones((128, 128), np.float32), -1)
    md = np.zeros((128, 128), np.float32)
    for s0 in range(0, 128, 16):
        md[s0:s0 + 16, s0:s0 + 16] = np.tril(np.ones((16, 16)), -1)
    c["md16"] = md
    c["md16t"] = np.ascontiguousarray(md.T)
    mo16 = np.zeros((128, 128), np.float32)
    for s0 in range(0, 128, 32):
        mo16[s0 + 16:s0 + 32, s0:s0 + 16] = 1
    c["mo16"] = mo16
    mo32 = np.zeros((128, 128), np.float32)
    for s0 in range(0, 128, 64):
        mo32[s0 + 32:s0 + 64, s0:s0 + 32] = 1
    c["mo32"] = mo32
    c["triui"] = np.triu(np.ones((128, 128), np.float32), 0)
    sel = np.zeros((DC, 128, H), np.float32)
    for oc in range(DC):
        for pp_ in range(128):
            sel[oc, pp_, 2 * oc + pp_ // 64] = 1.0
    c["selnorm"] = sel
    c["selexp"] = np.ascontiguousarray(sel.transpose(0, 2, 1))
    return c


def stage_weights(Wq, bq, Wk, bk, Wv, bv, Wbeta, bbeta, Wo, bo, wq_conv,
                  bq_conv, wk_conv, bk_conv, wv_conv, bv_conv, norm_w,
                  Wg, W1, W2, W3):
    f32 = np.float32

    def wT(w):
        return np.ascontiguousarray(np.asarray(w, f32).T)

    def ccol(v):
        return np.ascontiguousarray(np.asarray(v, f32).reshape(DC, 128).T)

    def cconv(w):
        return np.ascontiguousarray(
            np.asarray(w, f32).reshape(DC, 128, KC).transpose(1, 0, 2))

    nw = np.asarray(norm_w, f32)
    nwsel = np.zeros((H, D), f32)
    for h in range(H):
        nwsel[h, h * HD:(h + 1) * HD] = nw
    return dict(
        wqt=wT(Wq), wkt=wT(Wk), wvt=wT(Wv), wot=wT(Wo),
        wbetat=np.ascontiguousarray(wT(Wbeta).reshape(DC, 128, H)),
        wgt=np.ascontiguousarray(wT(Wg).reshape(DC, 128, NE)),
        bq=ccol(bq), bk=ccol(bk), bv=ccol(bv),
        bo_row=np.ascontiguousarray(np.asarray(bo, f32).reshape(1, D)),
        bbeta=np.ascontiguousarray(np.asarray(bbeta, f32).reshape(H, 1)),
        wqconv=cconv(wq_conv), wkconv=cconv(wk_conv), wvconv=cconv(wv_conv),
        bqconv=ccol(bq_conv), bkconv=ccol(bk_conv), bvconv=ccol(bv_conv),
        w1t=np.ascontiguousarray(np.asarray(W1, f32).transpose(0, 2, 1)),
        w2t=np.ascontiguousarray(np.asarray(W2, f32).transpose(0, 2, 1)),
        w3t=np.ascontiguousarray(np.asarray(W3, f32).transpose(0, 2, 1)),
        nwsel=nwsel,
        **consts(),
    )


def kernel(L0_attn_x, L0_attn_Wq, L0_attn_bq, L0_attn_Wk, L0_attn_bk,
           L0_attn_Wv, L0_attn_bv, L0_attn_Wbeta, L0_attn_bbeta, L0_attn_Wo,
           L0_attn_bo, L0_attn_wq_conv, L0_attn_bq_conv, L0_attn_wk_conv,
           L0_attn_bk_conv, L0_attn_wv_conv, L0_attn_bv_conv, L0_attn_norm_w,
           L0_ffn_Wg, L0_ffn_W1, L0_ffn_W2, L0_ffn_W3):
    f32 = np.float32
    x = np.asarray(L0_attn_x, f32)
    nc, _ = build_program()
    common = stage_weights(
        L0_attn_Wq, L0_attn_bq, L0_attn_Wk, L0_attn_bk, L0_attn_Wv,
        L0_attn_bv, L0_attn_Wbeta, L0_attn_bbeta, L0_attn_Wo, L0_attn_bo,
        L0_attn_wq_conv, L0_attn_bq_conv, L0_attn_wk_conv, L0_attn_bk_conv,
        L0_attn_wv_conv, L0_attn_bv_conv, L0_attn_norm_w, L0_ffn_Wg,
        L0_ffn_W1, L0_ffn_W2, L0_ffn_W3)

    in_maps = []
    for core in range(8):
        b, h = core // 2, core % 2
        xb = x[b]
        if h == 0:
            xp = np.zeros((L, D), f32)
            xp[PRE * SEG:] = xb[:OWN * SEG]
            kvm = np.zeros((H, PRE * SEG), f32)
            hm = np.zeros((128, 1), f32)
        else:
            xp = np.ascontiguousarray(xb)
            kvm = np.ones((H, PRE * SEG), f32)
            hm = np.ones((128, 1), f32)
        m = dict(common)
        m["x"] = xp
        m["kvmask"] = kvm
        m["halomask"] = hm
        in_maps.append(m)

    res = run_bass_kernel_spmd(nc, in_maps, core_ids=list(range(8)))
    outp = np.empty((B, L, D), f32)
    for core in range(8):
        b, h = core // 2, core % 2
        o = res.results[core]["out"]
        if h == 0:
            outp[b, :OWN * SEG] = o
        else:
            outp[b, PRE * SEG:] = o
    return outp
